# version 01
# baseline (speedup 1.0000x reference)
import math
import contextlib
import numpy as np
import concourse.bass as bass
import concourse.mybir as mybir
from concourse.bass_utils import run_bass_kernel_spmd

F32 = mybir.dt.float32
BF16 = mybir.dt.bfloat16
U32 = mybir.dt.uint32
AF = mybir.ActivationFunctionType
ALU = mybir.AluOpType
AX = mybir.AxisListType

NCORES = 8
B, S, H = 4, 2048, 2048
TOK = 1024
NH, NOPE, ROPE, VD, QLR = 16, 128, 64, 128, 1536
IH, ID = 8, 128
NE = 8
EPS = 1e-5
NEG = -1.0e30
ZIG = [[0, 3, 4, 7, 8, 11, 12, 15], [1, 2, 5, 6, 9, 10, 13, 14]]
SC_ATT = 1.0 / math.sqrt(NOPE + ROPE)
SC_SP = 1.0 / math.sqrt(ID)

ENGS = ("sync", "tensor", "vector", "scalar", "gpsimd")

DEV_STOP = "full"



def _block_kxm(w, KC, MC):
    return np.ascontiguousarray(w.reshape(KC, 128, MC, 128).transpose(1, 2, 0, 3))


def _ckv_col_perm():
    cols = []
    for pair in range(8):
        for h in (2 * pair, 2 * pair + 1):
            base = h * (ROPE + NOPE)
            cols.extend(base + np.arange(0, ROPE, 2))
            cols.extend(base + np.arange(1, ROPE, 2))
    for h in range(NH):
        base = h * (ROPE + NOPE)
        cols.extend(base + ROPE + np.arange(NOPE))
    return np.array(cols)


def _q_col_perm_pair(pair):
    a, b = 2 * pair, 2 * pair + 1
    cols = []
    for h in (a, b):
        cols.extend(h * (NOPE + ROPE) + np.arange(NOPE))
    for h in (a, b):
        base = h * (NOPE + ROPE) + NOPE
        cols.extend(base + np.arange(0, ROPE, 2))
        cols.extend(base + np.arange(1, ROPE, 2))
    return np.array(cols)


def _rope_tabs(positions):
    f = 1.0 / 10000.0 ** (np.arange(0, ROPE, 2, dtype=np.float64) / ROPE)
    ang = np.outer(f, positions.astype(np.float64))
    c, s = np.cos(ang), np.sin(ang)
    tab1 = np.concatenate([c, s, c, s], 0)
    tab2 = np.concatenate([s, c, s, c], 0)
    return (tab1.astype(np.float32).astype(np.dtype("float32")),
            tab2.astype(np.float32))


def host_inputs(ins):
    import ml_dtypes
    x = np.asarray(ins["L0_attn_x"], dtype=np.float32)
    sh = {}
    sh["eblk"] = np.stack([
        _block_kxm(np.asarray(ins["L0_ffn_expert_w"], np.float32)[e], 16, 16)
        for e in range(NE)])
    sh["owblk"] = _block_kxm(np.asarray(ins["L0_ffn_out_w"], np.float32), 16, 16)
    sh["cqblk"] = _block_kxm(np.asarray(ins["L0_attn_W_cq"], np.float32), 16, 12)
    ckv = np.asarray(ins["L0_attn_W_ckv"], np.float32)[:, _ckv_col_perm()]
    sh["ckvblk"] = _block_kxm(ckv, 16, 24)
    wq = np.asarray(ins["L0_attn_W_q"], np.float32)
    sh["wqblk"] = np.stack([
        _block_kxm(wq[:, _q_col_perm_pair(p)], 12, 3) for p in range(8)])
    sh["ipblk"] = np.ascontiguousarray(
        np.asarray(ins["L0_attn_W_ip"], np.float32).reshape(16, 128, IH * ID))
    sh["wrk"] = np.ascontiguousarray(
        np.asarray(ins["L0_ffn_router_w"], np.float32).reshape(16, 128, NE))
    sh["wigk"] = np.ascontiguousarray(
        np.asarray(ins["L0_attn_W_ig"], np.float32).reshape(16, 128, IH))
    sh["rb"] = np.asarray(ins["L0_ffn_router_b"], np.float32).reshape(1, NE)
    eb = np.asarray(ins["L0_ffn_expert_b"], np.float32)
    sh["ebias"] = np.ascontiguousarray(
        eb.reshape(NE, 16, 128).transpose(2, 0, 1).reshape(128, NE * 16))
    sh["obias"] = np.ascontiguousarray(
        np.asarray(ins["L0_ffn_out_b"], np.float32).reshape(16, 128).T)
    sh["woblk"] = _block_kxm(np.asarray(ins["L0_attn_W_o"], np.float32), 16, 16)
    sh["wkv"] = np.asarray(ins["L0_attn_W_kv"], np.float32)
    sh["kvnw"] = np.asarray(ins["L0_attn_kv_norm_w"], np.float32).reshape(128, 1)
    sh["kvnb"] = np.asarray(ins["L0_attn_kv_norm_b"], np.float32).reshape(128, 1)
    sh["qnw"] = np.ascontiguousarray(
        np.asarray(ins["L0_attn_q_norm_w"], np.float32).reshape(12, 128).T)
    sh["qnb"] = np.ascontiguousarray(
        np.asarray(ins["L0_attn_q_norm_b"], np.float32).reshape(12, 128).T)
    sh["wsq"] = np.asarray(ins["L0_attn_W_sq"], np.float32)
    sh["wsk"] = np.asarray(ins["L0_attn_W_sk"], np.float32)
    sh["wsv"] = np.asarray(ins["L0_attn_W_sv"], np.float32)
    sh["wio"] = np.ascontiguousarray(
        np.asarray(ins["L0_attn_W_io"], np.float32)[:, :VD])
    sh["ident"] = np.eye(128, dtype=np.float32)
    sh["ones"] = np.ones((128, 1), np.float32)
    sh["onesrow"] = np.ones((1, 128), np.float32)
    sh["tri"] = np.where(np.triu(np.ones((128, 128), bool), k=1),
                         np.float32(NEG), np.float32(0.0)).astype(np.float32)
    esel = np.zeros((8, NE * 128), np.float32)
    for e in range(NE):
        esel[e, e * 128:(e + 1) * 128] = 1.0
    sh["esel"] = esel
    fc = np.zeros((128, 8), np.float32)
    fc[:, 0] = EPS
    fc[:, 1] = 0.5
    fc[:, 2] = SC_ATT
    fc[:, 3] = SC_SP
    fc[:, 4] = 0.125
    fc[:, 5] = 1.0
    fc[:, 6] = 1.0 / NOPE
    fc[:, 7] = -1.0 / NOPE
    sh["fconsts"] = fc
    s16c = np.zeros((128, 16 * 16), np.float32)
    for hh in range(16):
        s16c[:, hh * 16 + hh] = 1.0
    sh["sel16c"] = s16c
    s16w = np.zeros((16, 16 * 128), np.float32)
    kvw = np.asarray(ins["L0_attn_kv_norm_w"], np.float32).reshape(128)
    for hh in range(16):
        s16w[hh, hh * 128:(hh + 1) * 128] = kvw
    sh["sel16w"] = s16w

    in_maps = []
    for core in range(NCORES):
        b, z = core // 2, core % 2
        own, part = ZIG[z], ZIG[1 - z]
        corder = np.concatenate(
            [np.arange(c * 128, (c + 1) * 128) for c in own + part])
        xc = x[b][corder]
        m = dict(sh)
        m["xT_own"] = np.ascontiguousarray(xc[:TOK].T.reshape(16, 128, TOK))
        m["xT_part"] = np.ascontiguousarray(xc[TOK:].T.reshape(16, 128, TOK))
        pos = np.concatenate([np.arange(c * 128, (c + 1) * 128) for c in own])
        t1, t2 = _rope_tabs(pos)
        m["tabq1"], m["tabq2"] = t1, t2
        posk = np.concatenate(
            [np.arange(c * 128, (c + 1) * 128) for c in own + part])
        t1k, t2k = _rope_tabs(posk)
        m["tabk1"], m["tabk2"] = t1k, t2k
        pm = np.zeros((1, 36 * 128), np.float32)
        off = 0
        for p in range(8):
            for mm in range(p + 1):
                if not (part[mm] < own[p]):
                    pm[:, (off + mm) * 128:(off + mm + 1) * 128] = NEG
            off += p + 1
        m["maskrow"] = pm
        in_maps.append(m)
    return in_maps



class Prog:
    def __init__(self, nc, top):
        self.nc = nc
        self.top = top
        self.sems = {}
        self.cnt = {}
        self.q = {e: [] for e in ENGS}

    def sem(self, name):
        if name not in self.sems:
            self.sems[name] = self.top.enter_context(
                self.nc.semaphore("sem_" + name))
            self.cnt[name] = 0
        return self.sems[name]

    def wait(self, eng, sem, tok):
        if tok is None or tok <= 0:
            return
        h = self.sem(sem)
        self.q[eng].append(lambda E, h=h, t=tok: E.wait_ge(h, t))

    def op(self, eng, fn, inc=None):
        if inc is not None:
            self.sem(inc)
            self.cnt[inc] += 1
            tok = self.cnt[inc]
            h = self.sems[inc]
            self.q[eng].append(lambda E, fn=fn, h=h: fn(E).then_inc(h, 1))
            return tok
        self.q[eng].append(fn)
        return None

    def inc_last(self, eng, sem):
        self.sem(sem)
        self.cnt[sem] += 1
        tok = self.cnt[sem]
        h = self.sems[sem]
        prev = self.q[eng].pop()
        self.q[eng].append(lambda E, f=prev, h=h: f(E).then_inc(h, 1))
        return tok

    def dma(self, eng, out, in_, sem):
        self.sem(sem)
        self.cnt[sem] += 16
        tok = self.cnt[sem]
        h = self.sems[sem]
        self.q[eng].append(
            lambda E, o=out, i=in_, h=h: E.dma_start(out=o, in_=i).then_inc(h, 16))
        return tok

    def idma(self, out, in_, in_offset, sem="g"):
        self.sem(sem)
        self.cnt[sem] += 16
        tok = self.cnt[sem]
        h = self.sems[sem]
        self.q["gpsimd"].append(
            lambda E, o=out, i=in_, io=in_offset, h=h:
            E.indirect_dma_start(out=o, out_offset=None, in_=i, in_offset=io)
            .then_inc(h, 16))
        return tok



def build_and_run(in_maps, dev_stop="full"):
    nc = bass.Bass("TRN2", target_bir_lowering=False, debug=False,
                   detect_race_conditions=False)
    top = contextlib.ExitStack()

    def din(name, shape, dt=F32):
        return nc.dram_tensor(name, list(shape), dt, kind="ExternalInput")

    D = {}
    D["xT_own"] = din("xT_own", [16, 128, TOK])
    D["xT_part"] = din("xT_part", [16, 128, TOK])
    D["eblk"] = din("eblk", [NE, 128, 16, 16, 128])
    D["owblk"] = din("owblk", [128, 16, 16, 128])
    D["cqblk"] = din("cqblk", [128, 12, 16, 128])
    D["ckvblk"] = din("ckvblk", [128, 24, 16, 128])
    D["wqblk"] = din("wqblk", [8, 128, 3, 12, 128])
    D["ipblk"] = din("ipblk", [16, 128, IH * ID])
    D["wrk"] = din("wrk", [16, 128, NE])
    D["wigk"] = din("wigk", [16, 128, IH])
    D["rb"] = din("rb", [1, NE])
    D["ebias"] = din("ebias", [128, NE * 16])
    D["obias"] = din("obias", [128, 16])
    D["woblk"] = din("woblk", [128, 16, 16, 128])
    D["wkv"] = din("wkv", [128, 256])
    D["kvnw"] = din("kvnw", [128, 1])
    D["kvnb"] = din("kvnb", [128, 1])
    D["qnw"] = din("qnw", [128, 12])
    D["qnb"] = din("qnb", [128, 12])
    D["wsq"] = din("wsq", [128, 128])
    D["wsk"] = din("wsk", [128, 128])
    D["wsv"] = din("wsv", [128, 128])
    D["wio"] = din("wio", [128, VD])
    D["ident"] = din("ident", [128, 128])
    D["ones"] = din("ones", [128, 1])
    D["onesrow"] = din("onesrow", [1, 128])
    D["tri"] = din("tri", [128, 128])
    D["esel"] = din("esel", [8, NE * 128])
    D["fconsts"] = din("fconsts", [128, 8])
    D["sel16c"] = din("sel16c", [128, 16 * 16])
    D["sel16w"] = din("sel16w", [16, 16 * 128])
    D["tabq1"] = din("tabq1", [128, TOK])
    D["tabq2"] = din("tabq2", [128, TOK])
    D["tabk1"] = din("tabk1", [128, S])
    D["tabk2"] = din("tabk2", [128, S])
    D["maskrow"] = din("maskrow", [1, 36 * 128])

    out_d = nc.dram_tensor("out", [16, 128, TOK], F32, kind="ExternalOutput")

    kvc_dram = nc.dram_tensor("kvc_sc", [24, 128, S], BF16, kind="Internal")
    ist_dram = nc.dram_tensor("ist_sc", [S, IH * ID], BF16, kind="Internal")
    xres_dram = nc.dram_tensor("xres_sc", [16, 128, TOK], F32, kind="Internal")
    x1p_dram = nc.dram_tensor("x1p_sc", [16, 128, TOK], BF16, kind="Internal")

    sb = nc.alloc_sbuf_tensor
    x_own = sb("x_own", [128, 16 * TOK], F32)
    ident = sb("sb_ident", [128, 128], BF16)
    ones_bf = sb("ones_bf", [128, 1], BF16)
    onesrow_f = sb("onesrow_f", [1, 128], F32)
    wkv = sb("sb_wkv", [128, 256], BF16)
    kvnw = sb("sb_kvnw", [128, 1], F32)
    kvnb = sb("sb_kvnb", [128, 1], F32)
    qnw = sb("sb_qnw", [128, 12], F32)
    qnb = sb("sb_qnb", [128, 12], F32)
    rbias = sb("rbias", [1, NE], F32)
    ebias = sb("sb_ebias", [128, NE * 16], F32)
    obias2 = sb("obias2", [128, 16], F32)
    wr16 = sb("wr16", [128, 16 * NE], BF16)
    wig16 = sb("wig16", [128, 16 * IH], BF16)
    esel = sb("sb_esel", [8, NE * 128], F32)
    fc = sb("sb_fc", [128, 8], F32)
    onesrow_bf = sb("onesrow_bf", [1, 128], BF16)
    iot = sb("iot", [128, 32], F32)

    P = Prog(nc, top)
    fence_scr = sb("fence_scr", [128, 4], F32)
    P.op("vector", lambda E: E.memset(fence_scr[:, :], 0.0))

    def mark_fence():
        tt = P.cnt["t"]
        vt = P.op("vector", lambda E: E.tensor_copy(
            fence_scr[:, 0:1], fence_scr[:, 1:2]), inc="v")
        at = P.op("scalar", lambda E: E.activation(
            fence_scr[:, 2:3], fence_scr[:, 3:4], AF.Copy), inc="a")
        return tt, vt, at

    def fence_wait(toks, engines=("tensor", "vector", "scalar", "gpsimd", "sync")):
        tt, vt, at = toks
        for eng in engines:
            P.wait(eng, "t", tt)
            P.wait(eng, "v", vt)
            P.wait(eng, "a", at)

    for dst, src in ((ident, "ident"), (ones_bf, "ones"), (onesrow_f, "onesrow"),
                     (wkv, "wkv"), (kvnw, "kvnw"), (kvnb, "kvnb"),
                     (qnw, "qnw"), (qnb, "qnb"), (rbias, "rb"),
                     (ebias, "ebias"), (esel, "esel"), (fc, "fconsts"),
                     (onesrow_bf, "onesrow")):
        ctok = P.dma("gpsimd", dst[:, :], D[src].ap(), "ldc")
    for k in range(16):
        ctok = P.dma("gpsimd", wr16[:, k * NE:(k + 1) * NE], D["wrk"].ap()[k], "ldc")
        ctok = P.dma("gpsimd", wig16[:, k * IH:(k + 1) * IH], D["wigk"].ap()[k], "ldc")
    obias_raw = sb("obias_raw", [128, 16], F32)
    ctok = P.dma("gpsimd", obias_raw[:, :], D["obias"].ap(), "ldc")
    identf = sb("sb_identf", [128, 128], F32)
    ctok = P.dma("gpsimd", identf[:, :], D["ident"].ap(), "ldc")
    xtok = 0
    for c in range(16):
        xtok = P.dma("sync", x_own[:, c * TOK:(c + 1) * TOK],
                     D["xT_own"].ap()[c], "ldx")
    CONST_G, CONST_S = ctok, xtok
    P.wait("vector", "ldc", CONST_G)
    P.wait("vector", "ldx", CONST_S)
    P.op("vector", lambda E: E.tensor_scalar_mul(obias2[:, :], obias_raw[:, :], 0.5),
         inc="v")

    def emit_ln(ctx, srcs_fn, T, dst, affine=None, name="ln", nchunks=16,
                nfeat=H, dst_fn=None):
        es = contextlib.ExitStack()
        ps_sum = es.enter_context(nc.psum_tensor(name + "_psum", [1, T], F32))
        ps_sq = es.enter_context(nc.psum_tensor(name + "_pssq", [1, T], F32))
        ps_bc = es.enter_context(nc.psum_tensor(name + "_psbc", [128, T], F32))
        sq_scr = [es.enter_context(nc.sbuf_tensor(name + "_sq%d" % i, [128, T], BF16))
                  for i in range(2)]
        cv_scr = [es.enter_context(nc.sbuf_tensor(name + "_cv%d" % i, [128, T], BF16))
                  for i in range(2)]
        rows = es.enter_context(nc.sbuf_tensor(name + "_rows", [1, 3 * T], F32))
        abc = es.enter_context(nc.sbuf_tensor(name + "_abc", [128, T], F32))
        bbc = es.enter_context(nc.sbuf_tensor(name + "_bbc", [128, T], F32))
        if affine is not None:
            t_scr = [es.enter_context(
                nc.sbuf_tensor(name + "_t%d" % i, [128, T], BF16))
                for i in range(2)]

        pe_tok = [0] * nchunks
        sq_tok = [0] * nchunks
        for c in range(nchunks):
            srcs = srcs_fn(c)
            cv = cv_scr[c % 2]
            sq = sq_scr[c % 2]
            bf_srcs = []
            for (apc, off, w, isf32) in srcs:
                if isf32:
                    if c >= 2:
                        P.wait("vector", "t", pe_tok[c - 2])
                        P.wait("vector", "a", sq_tok[c - 2])
                    vt = P.op("vector", lambda E, a=apc, s=cv, o=off, w=w:
                              E.tensor_copy(s[:, o:o + w], a), inc="v")
                    bf_srcs.append((cv[:, off:off + w], off, w, vt))
                else:
                    bf_srcs.append((apc, off, w, None))
            for i, (apc, off, w, vt) in enumerate(bf_srcs):
                if vt:
                    P.wait("scalar", "v", vt)
                if c >= 2:
                    P.wait("scalar", "t", pe_tok[c - 2])
                at = P.op("scalar", lambda E, a=apc, s=sq, o=off, w=w:
                          E.activation(s[:, o:o + w], a, AF.Square), inc="a")
                sq_tok[c] = at
                if vt:
                    P.wait("tensor", "v", vt)
                for sl in range(0, w, 512):
                    ee = min(sl + 512, w)
                    P.op("tensor", lambda E, a=apc, o=off, sl=sl, ee=ee,
                         st=(c == 0), sp=(c == nchunks - 1):
                         E.matmul(ps_sum[0:1, o + sl:o + ee], ones_bf[:, :],
                                  a[:, sl:ee], start=st, stop=sp))
                P.wait("tensor", "a", at)
                last = None
                for sl in range(0, w, 512):
                    ee = min(sl + 512, w)
                    last = P.op("tensor", lambda E, s=sq, o=off, sl=sl, ee=ee,
                                st=(c == 0), sp=(c == nchunks - 1):
                                E.matmul(ps_sq[0:1, o + sl:o + ee], ones_bf[:, :],
                                         s[:, sl:ee], start=st, stop=sp),
                                inc="t")
                pe_tok[c] = last
        tt = P.cnt["t"]
        P.wait("vector", "t", tt)
        mean = rows[0:1, 0:T]
        m2 = rows[0:1, T:2 * T]
        aux = rows[0:1, 2 * T:3 * T]
        rstd, brow = aux, mean
        P.op("vector", lambda E: E.tensor_scalar_mul(
            mean, ps_sum[0:1, :], 1.0 / nfeat))
        vt = P.op("vector", lambda E: E.tensor_scalar_mul(
            m2, ps_sq[0:1, :], 1.0 / nfeat), inc="v")
        P.wait("scalar", "v", vt)
        at = P.op("scalar", lambda E: E.activation(
            aux, mean, AF.Square), inc="a")
        P.wait("vector", "a", at)
        vt = P.op("vector", lambda E: E.tensor_sub(m2, m2, aux), inc="v")
        P.wait("scalar", "v", vt)
        at = P.op("scalar", lambda E: E.activation(
            aux, m2, AF.Sqrt, bias=fc[0:1, 0:1]), inc="a")
        P.wait("vector", "a", at)
        vt = P.op("vector", lambda E: E.reciprocal(rstd, rstd), inc="v")
        P.wait("vector", "v", vt)
        vt = P.op("vector", lambda E: E.scalar_tensor_tensor(
            brow, mean, -1.0, rstd, ALU.mult, ALU.mult), inc="v")
        P.wait("tensor", "v", vt)
        for sl in range(0, T, 512):
            ee = min(sl + 512, T)
            P.op("tensor", lambda E, sl=sl, ee=ee: E.matmul(
                ps_bc[:, sl:ee], onesrow_f[:, :], rstd[:, sl:ee],
                start=True, stop=True))
        tt = P.inc_last("tensor", "t")
        P.wait("vector", "t", tt)
        vt = P.op("vector", lambda E: E.tensor_copy(abc[:, :], ps_bc[:, :]),
                  inc="v")
        P.wait("tensor", "v", vt)
        for sl in range(0, T, 512):
            ee = min(sl + 512, T)
            P.op("tensor", lambda E, sl=sl, ee=ee: E.matmul(
                ps_bc[:, sl:ee], onesrow_f[:, :], brow[:, sl:ee],
                start=True, stop=True))
        tt = P.inc_last("tensor", "t")
        P.wait("vector", "t", tt)
        vt = P.op("vector", lambda E: E.tensor_copy(bbc[:, :], ps_bc[:, :]),
                  inc="v")
        P.wait("vector", "v", vt)
        mul_tok = [0] * (nchunks + 1)
        vt = 0
        if dst_fn is None:
            def dst_fn(c):
                return dst[:, c * T:(c + 1) * T]

        def emit_mult(c):
            nonlocal vt
            ts = t_scr[c % 2] if affine is not None else None
            for (apc, off, w, isf32) in srcs_fn(c):
                if affine is not None:
                    tgt = ts[:, off:off + w]
                else:
                    tgt = dst_fn(c)[:, off:off + w]
                vt = P.op("vector", lambda E, a=apc, tg=tgt, o=off, w=w:
                          E.tensor_mul(tg, a, abc[:, o:o + w]), inc="v")
            mul_tok[c] = vt

        def emit_add(c):
            nonlocal vt
            P.wait("vector", "v", mul_tok[c])
            ts = t_scr[c % 2] if affine is not None else None
            for (apc, off, w, isf32) in srcs_fn(c):
                if affine is None:
                    tgt = dst_fn(c)[:, off:off + w]
                    vt = P.op("vector", lambda E, tg=tgt, o=off, w=w:
                              E.tensor_add(tg, tg, bbc[:, o:o + w]), inc="v")
                else:
                    wcol, bcol = affine
                    vt = P.op("vector", lambda E, o=off, w=w, ts=ts:
                              E.tensor_add(ts[:, o:o + w], ts[:, o:o + w],
                                           bbc[:, o:o + w]), inc="v")
                    P.wait("vector", "v", vt)
                    vt = P.op("vector", lambda E, o=off, w=w, ts=ts, c=c:
                              E.tensor_scalar(ts[:, o:o + w], ts[:, o:o + w],
                                              wcol[:, c:c + 1], None,
                                              ALU.mult), inc="v")
                    P.wait("vector", "v", vt)
                    vt = P.op("vector", lambda E, c=c, o=off, w=w, ts=ts:
                              E.tensor_scalar(
                                  dst_fn(c)[:, o:o + w],
                                  ts[:, o:o + w], bcol[:, c:c + 1], None,
                                  ALU.add), inc="v")

        for c in range(nchunks):
            emit_mult(c)
            if c >= 1:
                emit_add(c - 1)
        emit_add(nchunks - 1)
        es.close()
        return vt, "v"

    def emit_ffn(idx, src_f32, target, upto=None, xtile=None, ext=None):
        name = "f%d" % idx
        if xtile is None:
            xtile = x_own
        es = contextlib.ExitStack()
        if ext is None:
            lnx = es.enter_context(
                nc.sbuf_tensor(name + "_lnx", [128, 16 * TOK], BF16))
            comb = es.enter_context(
                nc.sbuf_tensor(name + "_comb", [128, 16 * TOK], BF16))
        else:
            lnx, comb = ext

        es_x0 = contextlib.ExitStack()
        if src_f32 is None:
            x0p = es_x0.enter_context(
                nc.sbuf_tensor(name + "_x0p", [128, 16 * TOK], BF16))
            ltoks = []
            for c in range(16):
                ltoks.append(P.dma("gpsimd", x0p[:, c * TOK:(c + 1) * TOK],
                                   D["xT_part"].ap()[c], "ldp"))

            def srcs_fn(c):
                return [(x0p[:, c * TOK:(c + 1) * TOK], 0, TOK, False)]
            P.wait("vector", "ldp", ltoks[-1])
            P.wait("scalar", "ldp", ltoks[-1])
            P.wait("tensor", "ldp", ltoks[-1])
        else:
            def srcs_fn(c):
                return [(src_f32[:, c * TOK:(c + 1) * TOK], 0, TOK, True)]
            P.wait("vector", "ldx", CONST_S)
        P.wait("tensor", "ldc", CONST_G)
        lntok, lnsem = emit_ln(es, srcs_fn, TOK, lnx, name=name + "ln")
        P.wait("tensor", lnsem, lntok)
        P.wait("gpsimd", lnsem, lntok)
        fence_wait(mark_fence())
        es_x0.close()
        if upto == "ln":
            st = 0
            for c in range(16):
                st = P.dma("gpsimd", out_d.ap()[c],
                           lnx[:, c * TOK:(c + 1) * TOK], "st")
            P.wait("gpsimd", "st", st)
            es.close()
            return P.cnt["v"]

        es_gb = contextlib.ExitStack()
        gbc = es_gb.enter_context(
            nc.sbuf_tensor(name + "_gbc", [128, NE * TOK], BF16))
        es_r = contextlib.ExitStack()
        ps_r = es_r.enter_context(nc.psum_tensor(name + "_psr", [128, NE], F32))
        ps_tr = es_r.enter_context(nc.psum_tensor(name + "_pstr", [8, 128], F32))
        lg = es_r.enter_context(nc.sbuf_tensor(name + "_lg", [128, NE], F32))
        vm = es_r.enter_context(nc.sbuf_tensor(name + "_vm", [128, 8], F32))
        cols = es_r.enter_context(nc.sbuf_tensor(name + "_cols", [128, 28], F32))
        acols = es_r.enter_context(nc.sbuf_tensor(name + "_acols", [128, 4], F32))
        ia_t = es_r.enter_context(nc.sbuf_tensor(name + "_ia", [128, NE], F32))
        iab_t = es_r.enter_context(nc.sbuf_tensor(name + "_iab", [128, NE], F32))
        gtm = es_r.enter_context(nc.sbuf_tensor(name + "_gtm", [128, NE], F32))
        gfm = es_r.enter_context(nc.sbuf_tensor(name + "_gfm", [8, TOK], F32))
        ps_bc2 = es_r.enter_context(nc.psum_tensor(name + "_psb2", [128, TOK], F32))
        lg_done = 0
        gfm_done = 0
        tr_tok = 0
        for tm in range(8):
            P.wait("tensor", "v", lg_done)
            for k in range(16):
                P.op("tensor", lambda E, tm=tm, k=k: E.matmul(
                    ps_r[:, :], lnx[:, k * TOK + tm * 128:k * TOK + (tm + 1) * 128],
                    wr16[:, k * NE:(k + 1) * NE],
                    start=(k == 0), stop=False))
            tt = P.op("tensor", lambda E: E.matmul(
                ps_r[:, :], onesrow_f[:, :], rbias[:, :],
                start=False, stop=True), inc="t")
            P.wait("vector", "t", tt)
            vt = P.op("vector", lambda E: E.tensor_copy(lg[:, :], ps_r[:, :]),
                      inc="v")
            lg_done = vt
            P.wait("vector", "v", vt)
            vt = P.op("vector", lambda E: E.max(out=vm[:, :], in_=lg[:, :]),
                      inc="v")
            P.wait("vector", "v", vt)
            vt = P.op("vector", lambda E: E.tensor_sub(
                cols[:, 0:1], vm[:, 0:1], vm[:, 1:2]), inc="v")
            sub_tok = vt
            P.op("vector", lambda E: E.tensor_add(
                cols[:, 4:5], vm[:, 0:1], vm[:, 1:2]))
            vt = P.op("vector", lambda E: E.tensor_add(
                cols[:, 8:9], vm[:, 1:2], vm[:, 2:3]), inc="v")
            P.wait("vector", "v", vt)
            P.op("vector", lambda E: E.tensor_scalar(
                cols[:, 12:13], cols[:, 4:5], 0.5, None, ALU.mult))
            vt = P.op("vector", lambda E: E.tensor_scalar(
                cols[:, 16:17], cols[:, 8:9], 0.5, None, ALU.mult), inc="v")
            thr_tok = vt
            P.wait("scalar", "v", sub_tok)
            at = P.op("scalar", lambda E: E.activation(
                acols[:, 0:1], cols[:, 0:1], AF.Sigmoid), inc="a")
            P.wait("vector", "a", at)
            P.op("vector", lambda E: E.tensor_scalar(
                cols[:, 20:21], acols[:, 0:1], -1.0, 1.0, ALU.mult, ALU.add))
            vt = P.op("vector", lambda E: E.tensor_scalar(
                cols[:, 24:25], acols[:, 0:1], 2.0, -1.0, ALU.mult, ALU.add),
                inc="v")
            P.wait("vector", "v", vt)
            P.op("vector", lambda E: E.tensor_scalar(
                ia_t[:, :], lg[:, :], cols[:, 12:13], None, ALU.is_ge))
            vt = P.op("vector", lambda E: E.tensor_scalar(
                iab_t[:, :], lg[:, :], cols[:, 16:17], None, ALU.is_ge), inc="v")
            P.wait("vector", "v", vt)
            P.wait("vector", "t", tr_tok)
            vt = P.op("vector", lambda E: E.tensor_scalar(
                gtm[:, :], ia_t[:, :], cols[:, 24:25], None, ALU.mult), inc="v")
            P.wait("vector", "v", vt)
            vt = P.op("vector", lambda E: E.scalar_tensor_tensor(
                gtm[:, :], iab_t[:, :], cols[:, 20:21], gtm[:, :],
                ALU.mult, ALU.add), inc="v")
            P.wait("tensor", "v", vt)
            P.wait("tensor", "v", gfm_done)
            tt = P.op("tensor", lambda E, tm=tm: E.transpose(
                ps_tr[:, :], gtm[:, :], identf[:, :]), inc="t")
            tr_tok = tt
            P.wait("vector", "t", tt)
            vt = P.op("vector", lambda E, tm=tm: E.tensor_copy(
                gfm[:, tm * 128:(tm + 1) * 128], ps_tr[:, :]), inc="v")
            gfm_done = vt
        P.wait("tensor", "v", P.cnt["v"])
        bc_done = 0
        for e in range(NE):
            P.wait("tensor", "v", bc_done)
            for sl in range(0, TOK, 512):
                P.op("tensor", lambda E, e=e, sl=sl: E.matmul(
                    ps_bc2[:, sl:sl + 512], esel[:, e * 128:(e + 1) * 128],
                    gfm[:, sl:sl + 512], start=True, stop=True))
            tt = P.inc_last("tensor", "t")
            P.wait("vector", "t", tt)
            vt = P.op("vector", lambda E, e=e: E.tensor_copy(
                gbc[:, e * TOK:(e + 1) * TOK], ps_bc2[:, :]), inc="v")
            bc_done = vt
        GATES_V = P.cnt["v"]
        fence_wait(mark_fence())
        es_r.close()
        if upto == "router":
            st = 0
            for e in range(NE):
                st = P.dma("gpsimd", out_d.ap()[e],
                           gbc[:, e * TOK:(e + 1) * TOK], "st")
            P.wait("gpsimd", "st", st)
            es.close()
            return P.cnt["v"]

        es_e = contextlib.ExitStack()
        NBUF = 2
        wt = [es_e.enter_context(
            nc.sbuf_tensor(name + "_wt%d" % i, [128, 16 * 128], BF16))
            for i in range(NBUF)]
        hsb = [es_e.enter_context(
            nc.sbuf_tensor(name + "_h%d" % i, [128, TOK], BF16)) for i in range(2)]
        tsb = [es_e.enter_context(
            nc.sbuf_tensor(name + "_ts%d" % i, [128, TOK], BF16)) for i in range(2)]
        cm = [es_e.enter_context(
            nc.sbuf_tensor(name + "_cm%d" % i, [128, TOK], F32)) for i in range(2)]
        pse = [es_e.enter_context(
            nc.psum_tensor(name + "_pse%d" % i, [128, TOK], F32)) for i in range(3)]
        P.wait("vector", "ldc", CONST_G)

        wt_tok = [0] * NBUF
        wt_free = [0] * NBUF
        comb_tok = [0] * 16
        mul_t = 0
        add_t = 0
        evac_a = [0] * 3
        evac_v = {}
        piece = 0
        for m in range(16):
            for e in range(NE):
                bi = piece % NBUF
                si = piece % 3
                P.wait("gpsimd", "t", wt_free[bi])
                wt_tok[bi] = P.dma("gpsimd", wt[bi][:, :],
                                   D["eblk"].ap()[e, :, m, :, :], "wt%d" % bi)
                P.wait("tensor", "wt%d" % bi, wt_tok[bi])
                P.wait("tensor", "a", evac_a[si])
                for k in range(16):
                    for sl in (0, 512):
                        P.op("tensor", lambda E, bi=bi, si=si, k=k, sl=sl:
                             E.matmul(pse[si][:, sl:sl + 512],
                                      wt[bi][:, k * 128:(k + 1) * 128],
                                      lnx[:, k * TOK + sl:k * TOK + sl + 512],
                                      start=(k == 0), stop=(k == 15)))
                tt = P.inc_last("tensor", "t")
                wt_free[bi] = tt
                hb = hsb[piece % 2]
                P.wait("scalar", "t", tt)
                if piece >= 2:
                    P.wait("scalar", "v", evac_v.get(piece - 2, 0))
                at = P.op("scalar", lambda E, si=si, hb=hb, e=e, m=m:
                          E.activation(hb[:, :], pse[si][:, :],
                                       AF.Gelu_apprx_tanh,
                                       bias=ebias[:, e * 16 + m:e * 16 + m + 1]),
                          inc="a")
                evac_a[si] = at
                P.wait("vector", "a", at)
                if e == 0:
                    P.wait("vector", "v", GATES_V)
                    if m >= 2:
                        P.wait("vector", "v", comb_tok[m - 2])
                    vt = P.op("vector", lambda E, hb=hb, e=e, m=m: E.tensor_mul(
                        cm[m % 2][:, :], hb[:, :],
                        gbc[:, e * TOK:(e + 1) * TOK]), inc="v")
                    mul_t = vt
                    add_t = vt
                else:
                    vt = P.op("vector", lambda E, hb=hb, e=e: E.tensor_mul(
                        tsb[e % 2][:, :], hb[:, :],
                        gbc[:, e * TOK:(e + 1) * TOK]), inc="v")
                    prev_mul = mul_t
                    mul_t = vt
                    if e >= 2:
                        P.wait("vector", "v", prev_mul)
                        P.wait("vector", "v", add_t)
                        vt = P.op("vector", lambda E, m=m, e=e: E.tensor_add(
                            cm[m % 2][:, :], cm[m % 2][:, :],
                            tsb[(e - 1) % 2][:, :]), inc="v")
                        add_t = vt
                evac_v[piece] = vt
                piece += 1
            P.wait("vector", "v", mul_t)
            P.wait("vector", "v", add_t)
            vt = P.op("vector", lambda E, m=m: E.tensor_add(
                cm[m % 2][:, :], cm[m % 2][:, :], tsb[1][:, :]), inc="v")
            P.wait("vector", "v", vt)
            vt = P.op("vector", lambda E, m=m: E.tensor_copy(
                comb[:, m * TOK:(m + 1) * TOK], cm[m % 2][:, :]), inc="v")
            comb_tok[m] = vt
            evac_v[piece - 1] = vt
        COMB_V = P.cnt["v"]
        fence_wait(mark_fence())
        es_e.close()
        es_gb.close()
        if upto == "experts":
            es.close()
            return P.cnt["v"]

        es_o = contextlib.ExitStack()
        owt = [es_o.enter_context(
            nc.sbuf_tensor(name + "_ow%d" % i, [128, 2 * 16 * 128], BF16))
            for i in range(2)]
        pso = [es_o.enter_context(
            nc.psum_tensor(name + "_pso%d" % i, [128, TOK], F32))
            for i in range(4)]
        tev = [es_o.enter_context(
            nc.sbuf_tensor(name + "_te%d" % i, [128, TOK], F32)) for i in range(4)]
        if target == "part":
            xps = [es_o.enter_context(
                nc.sbuf_tensor(name + "_xp%d" % i, [128, TOK], F32))
                for i in range(2)]
            x1st = [es_o.enter_context(
                nc.sbuf_tensor(name + "_x1s%d" % i, [128, TOK], BF16))
                for i in range(2)]
        P.wait("vector", "v", COMB_V)
        ow_tok = [0, 0]
        ow_free = [0, 0]
        ev_done = [0, 0, 0, 0]
        xp_tok = [0, 0]
        xp_used = [0, 0]
        x1_st = [0, 0]
        pend = []

        def _emit_xadd(j_mp):
            j, pmp = j_mp
            P.wait("vector", "v", ev_done[j])
            if target == "own":
                P.op("vector", lambda E, j=j, mp=pmp: E.tensor_add(
                    xtile[:, mp * TOK:(mp + 1) * TOK],
                    xtile[:, mp * TOK:(mp + 1) * TOK], tev[j][:, :]), inc="v")
            else:
                bj = j % 2
                P.wait("vector", "xp%d" % bj, xp_tok[bj])
                P.wait("vector", "sx%d" % bj, x1_st[bj])
                vt = P.op(
                    "vector", lambda E, j=j, bj=bj: E.tensor_add(
                        x1st[bj][:, :], xps[bj][:, :], tev[j][:, :]), inc="v")
                xp_used[bj] = vt
                P.wait("sync", "v", vt)
                x1_st[bj] = P.dma("sync", x1p_dram.ap()[pmp],
                                  x1st[bj][:, :], "sx%d" % bj)

        for mg in range(8):
            bi = mg % 2
            P.wait("gpsimd", "t", ow_free[bi])
            ow_tok[bi] = P.dma("gpsimd", owt[bi][:, :],
                               D["owblk"].ap()[:, 2 * mg:2 * mg + 2, :, :],
                               "ow%d" % bi)
            P.wait("tensor", "ow%d" % bi, ow_tok[bi])
            P.wait("tensor", "v", COMB_V)
            for i in range(2):
                if len(pend) >= 2:
                    _emit_xadd(pend.pop(0))
                if target == "part":
                    P.wait("sync", "v", xp_used[i])
                    xp_tok[i] = P.dma("sync", xps[i][:, :],
                                      D["xT_part"].ap()[2 * mg + i], "xp%d" % i)
                ps = pso[2 * bi + i]
                P.wait("tensor", "v", ev_done[2 * bi + i])
                for k in range(16):
                    for sl in (0, 512):
                        P.op("tensor", lambda E, bi=bi, i=i, k=k, sl=sl, ps=ps:
                             E.matmul(ps[:, sl:sl + 512],
                                      owt[bi][:, (i * 16 + k) * 128:
                                              (i * 16 + k + 1) * 128],
                                      comb[:, k * TOK + sl:k * TOK + sl + 512],
                                      start=(k == 0), stop=(k == 15)))
                tt = P.inc_last("tensor", "t")
                ow_free[bi] = tt
                mp = 2 * mg + i
                j = 2 * bi + i
                P.wait("vector", "t", tt)
                at = P.op("vector", lambda E, ps=ps, j=j, mp=mp:
                          E.scalar_tensor_tensor(
                              tev[j][:, :], ps[:, :], 0.5,
                              obias2[:, mp:mp + 1].to_broadcast([128, TOK]),
                              ALU.mult, ALU.add), inc="v")
                ev_done[j] = at
                pend.append((j, mp))
        while pend:
            _emit_xadd(pend.pop(0))
        FFN_V = P.cnt["v"]
        fence_wait(mark_fence())
        es_o.close()
        es.close()
        return FFN_V

    if dev_stop in ("ln", "router", "experts"):
        v = emit_ffn(0, x_own, "own", upto=dev_stop)
        if dev_stop not in ("ln", "router"):
            _emit_store(P, nc, x_own, out_d, v)
        return _run(nc, P, top, in_maps)
    ffn1_v = emit_ffn(0, x_own, "own")

    if dev_stop == "ffn1own":
        _emit_store(P, nc, x_own, out_d, ffn1_v)
        return _run(nc, P, top, in_maps)

    ffn1p_v = emit_ffn(1, None, "part")
    X1P_ST = (P.cnt.get("sx0", 0), P.cnt.get("sx1", 0))

    if dev_stop == "ffn1":
        _emit_store(P, nc, x_own, out_d, ffn1_v)
        return _run(nc, P, top, in_maps)


    arena = x_own.ap().bitcast(BF16)

    es2 = contextlib.ExitStack()
    ln2x = es2.enter_context(nc.sbuf_tensor("ln2x", [128, 16 * S], BF16))
    x1p = es2.enter_context(nc.sbuf_tensor("x1p_sb", [128, 16 * TOK], BF16))
    P.wait("gpsimd", "sx0", X1P_ST[0])
    P.wait("gpsimd", "sx1", X1P_ST[1])
    lp2 = 0
    for c in range(16):
        lp2 = P.dma("gpsimd", x1p[:, c * TOK:(c + 1) * TOK],
                    x1p_dram.ap()[c], "ldp2")
    for eng in ("tensor", "vector", "scalar"):
        P.wait(eng, "ldp2", lp2)

    def ln2_srcs_own(c):
        return [(x_own[:, c * TOK:(c + 1) * TOK], 0, TOK, True)]

    def ln2_srcs_par(c):
        return [(x1p[:, c * TOK:(c + 1) * TOK], 0, TOK, False)]

    P.wait("vector", "v", ffn1p_v)
    emit_ln(None, ln2_srcs_own, TOK, None, name="ln2a",
            dst_fn=lambda c: ln2x[:, c * S:c * S + TOK])
    ln2tok, ln2sem = emit_ln(None, ln2_srcs_par, TOK, None, name="ln2b",
                             dst_fn=lambda c: ln2x[:, c * S + TOK:
                                                   (c + 1) * S])
    P.wait("tensor", ln2sem, ln2tok)
    P.wait("gpsimd", ln2sem, ln2tok)
    P.wait("sync", ln2sem, ln2tok)
    fence_wait(mark_fence())

    sp_tok = 0
    for c in range(16):
        sp_tok = P.dma("sync", xres_dram.ap()[c],
                       x_own[:, c * TOK:(c + 1) * TOK], "spx")
    qc_raw = arena[:, 0:12 * TOK]
    attn_all = arena[:, 12 * TOK:12 * TOK + 16 * TOK]

    es3 = contextlib.ExitStack()
    kvt = [es3.enter_context(nc.sbuf_tensor("kv_wt%d" % i, [128, 16 * 128], BF16))
           for i in range(2)]
    kevac = [es3.enter_context(nc.sbuf_tensor("kv_ev%d" % i, [128, S], BF16))
             for i in range(2)]
    ps_kv = [es3.enter_context(nc.psum_tensor("ps_kv%d" % i, [128, S], F32))
             for i in range(2)]
    kv_dma = [0, 0]
    kv_mm = [0, 0]
    kv_ev_a = [0, 0]
    kv_st = [0, 0]
    for m in range(24):
        bi = m % 2
        P.wait("gpsimd", "t", kv_mm[bi])
        kv_dma[bi] = P.dma("gpsimd", kvt[bi][:, :],
                           D["ckvblk"].ap()[:, m, :, :], "kvw%d" % bi)
        P.wait("tensor", "kvw%d" % bi, kv_dma[bi])
        P.wait("tensor", "a", kv_ev_a[bi])
        for k in range(16):
            for sl in range(0, S, 512):
                P.op("tensor", lambda E, bi=bi, k=k, sl=sl: E.matmul(
                    ps_kv[bi][:, sl:sl + 512], kvt[bi][:, k * 128:(k + 1) * 128],
                    ln2x[:, k * S + sl:k * S + sl + 512],
                    start=(k == 0), stop=(k == 15)))
        tt = P.inc_last("tensor", "t")
        kv_mm[bi] = tt
        P.wait("scalar", "t", tt)
        P.wait("scalar", "s%d" % bi, kv_st[bi])
        at = P.op("scalar", lambda E, bi=bi: E.activation(
            kevac[bi][:, :], ps_kv[bi][:, :], AF.Copy), inc="a")
        kv_ev_a[bi] = at
        P.wait("sync", "a", at)
        kv_st[bi] = P.dma("sync", kvc_dram.ap()[m], kevac[bi][:, :], "s%d" % bi)
    fence_wait(mark_fence())
    es3.close()

    es4 = contextlib.ExitStack()
    wip = es4.enter_context(nc.sbuf_tensor("wip", [128, 16 * 512], BF16))
    ievac = [es4.enter_context(nc.sbuf_tensor("ist_ev%d" % i, [128, 512], BF16))
             for i in range(2)]
    ps_ist = [es4.enter_context(nc.psum_tensor("ps_ist%d" % i, [128, 512], F32))
              for i in range(2)]
    half_mm = 0
    for half in range(2):
        P.wait("gpsimd", "t", half_mm)
        htok = 0
        for k in range(16):
            htok = P.dma("gpsimd", wip[:, k * 512:(k + 1) * 512],
                         D["ipblk"].ap()[k, :, half * 512:(half + 1) * 512],
                         "wiph")
        P.wait("tensor", "wiph", htok)
        i_mm = [0, 0]
        i_ev = [0, 0]
        i_st = [0, 0]
        for tokm in range(16):
            bi = tokm % 2
            P.wait("tensor", "v", i_ev[bi])
            for k in range(16):
                P.op("tensor", lambda E, bi=bi, k=k, tokm=tokm: E.matmul(
                    ps_ist[bi][:, :],
                    ln2x[:, k * S + tokm * 128:k * S + (tokm + 1) * 128],
                    wip[:, k * 512:(k + 1) * 512],
                    start=(k == 0), stop=(k == 15)))
            tt = P.inc_last("tensor", "t")
            i_mm[bi] = tt
            half_mm = tt
            P.wait("vector", "t", tt)
            P.wait("vector", "si%d" % bi, i_st[bi])
            vt = P.op("vector", lambda E, bi=bi: E.tensor_copy(
                ievac[bi][:, :], ps_ist[bi][:, :]), inc="v")
            i_ev[bi] = vt
            P.wait("sync", "v", vt)
            i_st[bi] = P.dma(
                "sync",
                ist_dram.ap()[tokm * 128:(tokm + 1) * 128,
                              half * 512:(half + 1) * 512],
                ievac[bi][:, :], "si%d" % bi)
    IST_ST = (P.cnt["si0"], P.cnt["si1"])
    fence_wait(mark_fence())
    es4.close()

    es5 = contextlib.ExitStack()
    ps_g = es5.enter_context(nc.psum_tensor("ps_g", [8, S], F32))
    gsp = es5.enter_context(nc.sbuf_tensor("gsp", [8, S], F32))
    svals = es5.enter_context(nc.sbuf_tensor("svals", [8, 8], F32))
    sidx = es5.enter_context(nc.sbuf_tensor("sidx", [32, 32], U32))
    sidxT = es5.enter_context(nc.sbuf_tensor("sidxT", [128, 32], U32))
    selraw = es5.enter_context(nc.sbuf_tensor("selraw", [8, 2 * 1024], BF16))
    selT = es5.enter_context(nc.sbuf_tensor("selT", [128, 64], BF16))
    sqT = es5.enter_context(nc.sbuf_tensor("sqT", [128, 64], BF16))
    skT = es5.enter_context(nc.sbuf_tensor("skT", [128, 64], BF16))
    sv2 = es5.enter_context(nc.sbuf_tensor("sv2", [8, 8 * 128], BF16))
    so2 = es5.enter_context(nc.sbuf_tensor("so2", [8, 8 * 128], BF16))
    soT = es5.enter_context(nc.sbuf_tensor("soT", [128, 64], BF16))
    srows = es5.enter_context(nc.sbuf_tensor("srows", [8, 24], F32))
    sarow = es5.enter_context(nc.sbuf_tensor("sarow", [8, 8], F32))
    spt = es5.enter_context(nc.sbuf_tensor("spt", [8, 2 * 8 * 8], BF16))
    wsq = es5.enter_context(nc.sbuf_tensor("sb_wsq", [128, 128], BF16))
    wsk = es5.enter_context(nc.sbuf_tensor("sb_wsk", [128, 128], BF16))
    wsv = es5.enter_context(nc.sbuf_tensor("sb_wsv", [128, 128], BF16))
    wio = es5.enter_context(nc.sbuf_tensor("sb_wio", [128, VD], BF16))
    ps_s1 = es5.enter_context(nc.psum_tensor("ps_s1", [128, 512], F32))
    ps_s2 = es5.enter_context(nc.psum_tensor("ps_s2", [8, 128], F32))
    ps_s2b = es5.enter_context(nc.psum_tensor("ps_s2b", [8, 128], BF16))
    ps_s3 = es5.enter_context(nc.psum_tensor("ps_s3", [128, 64], BF16))

    swt = 0
    for dst, src in ((wsq, "wsq"), (wsk, "wsk"), (wsv, "wsv"), (wio, "wio")):
        swt = P.dma("gpsimd", dst[:, :], D[src].ap(), "ldsw")
    P.wait("tensor", "ldsw", swt)
    P.op("vector", lambda E: E.memset(sidx[:, :], 0))
    vt = P.op("vector", lambda E: E.memset(sidxT[:, :], 0), inc="v")
    P.wait("vector", "v", vt)
    for k in range(16):
        for sl in range(0, S, 512):
            P.op("tensor", lambda E, k=k, sl=sl: E.matmul(
                ps_g[:, sl:sl + 512], wig16[:, k * IH:(k + 1) * IH],
                ln2x[:, k * S + sl:k * S + sl + 512],
                start=(k == 0), stop=(k == 15)))
    tt = P.inc_last("tensor", "t")
    P.wait("vector", "t", tt)
    vt = P.op("vector", lambda E: E.tensor_copy(gsp[:, :], ps_g[:, :]), inc="v")
    P.wait("vector", "v", vt)
    vt = P.op("vector", lambda E: E.max(out=svals[:, :], in_=gsp[:, :]), inc="v")
    P.wait("vector", "v", vt)
    vt = P.op("vector", lambda E: E.max_index(
        sidx[0:8, 0:8], svals[:, :], gsp[:, :]), inc="v")
    P.wait("vector", "v", vt)
    vt = P.op("vector", lambda E: E.transpose(sidxT[0:32, 0:32], sidx[:, :]),
              inc="v")
    P.wait("gpsimd", "v", vt)
    P.wait("gpsimd", "si0", IST_ST[0])
    P.wait("gpsimd", "si1", IST_ST[1])
    g_tok = [0, 0]
    tr_t = [0, 0]
    trt = 0
    for ih in range(IH):
        bi = ih % 2
        P.wait("gpsimd", "t", tr_t[bi])
        g_tok[bi] = P.idma(selraw[:, bi * 1024:(bi + 1) * 1024],
                           ist_dram.ap(),
                           bass.IndirectOffsetOnAxis(sidxT[0:8, ih:ih + 1], 0),
                           sem="sg%d" % bi)
        P.wait("tensor", "sg%d" % bi, g_tok[bi])
        P.wait("tensor", "v", trt)
        tt = P.op("tensor", lambda E, ih=ih, bi=bi: E.transpose(
            ps_s3[:, 0:8],
            selraw[0:8, bi * 1024 + ih * 128:bi * 1024 + (ih + 1) * 128],
            ident[0:8, 0:8]), inc="t")
        tr_t[bi] = tt
        P.wait("vector", "t", tt)
        trt = P.op("vector", lambda E, ih=ih: E.tensor_copy(
            selT[:, ih * 8:(ih + 1) * 8], ps_s3[:, 0:8]), inc="v")
    P.wait("tensor", "v", trt)
    tt = P.op("tensor", lambda E: E.matmul(
        ps_s1[:, 0:64], wsq[:, :], selT[:, :], start=True, stop=True), inc="t")
    P.wait("vector", "t", tt)
    vt = P.op("vector", lambda E: E.tensor_copy(sqT[:, :], ps_s1[:, 0:64]),
              inc="v")
    P.wait("tensor", "v", vt)
    tt = P.op("tensor", lambda E: E.matmul(
        ps_s1[:, 0:64], wsk[:, :], selT[:, :], start=True, stop=True), inc="t")
    P.wait("vector", "t", tt)
    vt = P.op("vector", lambda E: E.tensor_copy(skT[:, :], ps_s1[:, 0:64]),
              inc="v")
    P.wait("tensor", "v", vt)
    svt = 0
    for ih in range(IH):
        P.wait("tensor", "v", svt)
        tt = P.op("tensor", lambda E, ih=ih: E.matmul(
            ps_s2[:, :], selT[:, ih * 8:(ih + 1) * 8], wsv[:, :],
            start=True, stop=True), inc="t")
        P.wait("vector", "t", tt)
        svt = P.op("vector", lambda E, ih=ih: E.tensor_copy(
            sv2[:, ih * 128:(ih + 1) * 128], ps_s2[:, :]), inc="v")
    P.wait("tensor", "v", svt)
    sot = 0
    for ih in range(IH):
        P.wait("tensor", "v", sot)
        tt = P.op("tensor", lambda E, ih=ih: E.matmul(
            ps_s2[:, 0:8], sqT[:, ih * 8:(ih + 1) * 8],
            skT[:, ih * 8:(ih + 1) * 8], start=True, stop=True), inc="t")
        P.wait("vector", "t", tt)
        vt = P.op("vector", lambda E, ih=ih: E.reduce_max(
            srows[:, 0:1], ps_s2[:, 0:8], axis=AX.X), inc="v")
        P.wait("vector", "v", vt)
        vt = P.op("vector", lambda E: E.tensor_scalar(
            srows[:, 4:5], srows[:, 0:1], -SC_SP, None, ALU.mult), inc="v")
        P.wait("scalar", "v", vt)
        at = P.op("scalar", lambda E, ih=ih: E.activation(
            spt[:, ih * 8:(ih + 1) * 8], ps_s2[:, 0:8], AF.Exp,
            bias=srows[:, 4:5], scale=fc[0:8, 3:4],
            accum_out=sarow[:, 0:1]), inc="a")
        P.wait("vector", "a", at)
        vt = P.op("vector", lambda E: E.reciprocal(
            srows[:, 8:9], sarow[:, 0:1]), inc="v")
        P.wait("vector", "v", vt)
        vt = P.op("vector", lambda E, ih=ih: E.tensor_scalar(
            spt[:, ih * 8:(ih + 1) * 8], spt[:, ih * 8:(ih + 1) * 8],
            srows[:, 8:9], None, ALU.mult), inc="v")
        P.wait("tensor", "v", vt)
        tt = P.op("tensor", lambda E, ih=ih: E.transpose(
            ps_s2b[:, 0:8], spt[:, ih * 8:(ih + 1) * 8], ident[0:8, 0:8]),
            inc="t")
        P.wait("vector", "t", tt)
        vt = P.op("vector", lambda E, ih=ih: E.tensor_copy(
            spt[:, 64 + ih * 8:64 + (ih + 1) * 8], ps_s2b[:, 0:8]), inc="v")
        P.wait("tensor", "v", vt)
        tt = P.op("tensor", lambda E, ih=ih: E.matmul(
            ps_s2[:, :], spt[:, 64 + ih * 8:64 + (ih + 1) * 8],
            sv2[:, ih * 128:(ih + 1) * 128], start=True, stop=True), inc="t")
        P.wait("vector", "t", tt)
        sot = P.op("vector", lambda E, ih=ih: E.tensor_copy(
            so2[:, ih * 128:(ih + 1) * 128], ps_s2[:, :]), inc="v")
        P.wait("tensor", "v", sot)
        tt = P.op("tensor", lambda E, ih=ih: E.transpose(
            ps_s3[:, 0:8], so2[0:8, ih * 128:(ih + 1) * 128],
            ident[0:8, 0:8]), inc="t")
        P.wait("vector", "t", tt)
        vt = P.op("vector", lambda E, ih=ih: E.tensor_copy(
            soT[:, ih * 8:(ih + 1) * 8], ps_s3[:, 0:8]), inc="v")
        P.wait("tensor", "v", vt)
        tt = P.op("tensor", lambda E, ih=ih: E.matmul(
            ps_s1[:, 0:8], wio[:, :], soT[:, ih * 8:(ih + 1) * 8],
            start=True, stop=True), inc="t")
        P.wait("vector", "t", tt)
        vt = P.op("vector", lambda E, ih=ih: E.reduce_sum(
            iot[:, ih * 4 + 1:ih * 4 + 2], ps_s1[:, 0:8], axis=AX.X), inc="v")
        P.wait("vector", "v", vt)
        vt = P.op("vector", lambda E, ih=ih: E.tensor_scalar(
            iot[:, ih * 4:ih * 4 + 1], iot[:, ih * 4 + 1:ih * 4 + 2],
            0.125, None, ALU.mult), inc="v")
    IOT_V = P.cnt["v"]
    fence_wait(mark_fence())
    es5.close()


    es6 = contextlib.ExitStack()
    cqt = [es6.enter_context(nc.sbuf_tensor("cq_wt%d" % i, [128, 16 * 128], BF16))
           for i in range(2)]
    ps_qc = [es6.enter_context(nc.psum_tensor("ps_qc%d" % i, [128, TOK], F32))
             for i in range(2)]
    qc_dma = [0, 0]
    qc_mm = [0, 0]
    qc_ev = [0, 0]
    P.wait("vector", "spx", sp_tok)
    for m in range(12):
        bi = m % 2
        P.wait("gpsimd", "t", qc_mm[bi])
        qc_dma[bi] = P.dma("gpsimd", cqt[bi][:, :],
                           D["cqblk"].ap()[:, m, :, :], "cqw%d" % bi)
        P.wait("tensor", "cqw%d" % bi, qc_dma[bi])
        P.wait("tensor", "v", qc_ev[bi])
        for k in range(16):
            for sl in (0, 512):
                P.op("tensor", lambda E, bi=bi, k=k, sl=sl: E.matmul(
                    ps_qc[bi][:, sl:sl + 512], cqt[bi][:, k * 128:(k + 1) * 128],
                    ln2x[:, k * S + sl:k * S + sl + 512],
                    start=(k == 0), stop=(k == 15)))
        tt = P.inc_last("tensor", "t")
        qc_mm[bi] = tt
        P.wait("vector", "t", tt)
        vt = P.op("vector", lambda E, bi=bi, m=m: E.tensor_copy(
            qc_raw[:, m * TOK:(m + 1) * TOK], ps_qc[bi][:, :]), inc="v")
        qc_ev[bi] = vt
    fence_wait(mark_fence())
    es6.close()
    es2.close()

    def qc_srcs(c):
        return [(qc_raw[:, c * TOK:(c + 1) * TOK], 0, TOK, False)]

    qlntok, qlnsem = emit_ln(None, qc_srcs, TOK, qc_raw, affine=(qnw, qnb),
                             name="qcln", nchunks=12, nfeat=QLR)
    qcn = qc_raw
    fence_wait(mark_fence())

    if dev_stop == "sweeps":
        fence_wait(mark_fence())
        st = 0
        for c in range(16):
            st = P.dma("sync", out_d.ap()[c], xres_dram.ap()[c], "st")
        P.wait("sync", "st", st)
        return _run(nc, P, top, in_maps)

    es8 = contextlib.ExitStack()

    def sbt(nm, sh, dt):
        return es8.enter_context(nc.sbuf_tensor(nm, sh, dt))

    maskrow_t = sbt("maskrow_sb", [1, 36 * 128], BF16)
    maskrow = maskrow_t[0:1, :]

    tabq1 = sbt("sb_tabq1", [128, TOK], BF16)
    tabq2 = sbt("sb_tabq2", [128, TOK], BF16)
    tabk1 = sbt("sb_tabk1", [128, S], BF16)
    tabk2 = sbt("sb_tabk2", [128, S], BF16)
    tri = sbt("sb_tri", [128, 128], F32)
    sel16c = sbt("sel16c_sb", [128, 16 * 16], BF16)
    sel16w = sbt("sel16w_sb", [16, 16 * 128], F32)
    kraw = [sbt("kraw%d" % i, [128, S], BF16) for i in range(3)]
    kpe_rope = sbt("kpe_rope", [128, S], BF16)
    kvn = sbt("kvn", [128, S], BF16)
    Smat = sbt("Smat", [16, S], F32)
    Qmat = sbt("Qmat", [16, S], F32)
    knope = sbt("knope", [128, S], BF16)
    vtm = sbt("vtm", [128, 16 * 128], BF16)
    wq_sb = sbt("wq_sb", [128, 36 * 128], BF16)
    qp_sb = sbt("qp_sb", [128, 3 * TOK], BF16)
    qrope = sbt("qrope", [128, TOK], BF16)
    e_sb = sbt("e_sb", [128, 2048], BF16)
    diag = sbt("diag", [128, 128], F32)
    pT = [sbt("pT%d" % i, [128, 512], BF16) for i in range(16)]
    rsA = sbt("rsA", [128, S], BF16)
    rsB = sbt("rsB", [128, S], BF16)
    kvt1 = sbt("kvt1", [128, 512], BF16)
    recrow = sbt("recrow", [1, 512], F32)
    rb_sb = sbt("rb_sb", [128, 512], F32)
    avt = sbt("avt", [128, 512], F32)
    mxcol = sbt("mxcol", [128, 8], F32)
    acc_col = sbt("acc_col", [128, 8], F32)
    rec_col = sbt("rec_col", [128, 4], F32)

    P.wait("gpsimd", "spx", sp_tok)
    for dst, src in ((tabq1, "tabq1"), (tabq2, "tabq2"), (tabk1, "tabk1"),
                     (tabk2, "tabk2"), (tri, "tri"), (sel16c, "sel16c"),
                     (sel16w, "sel16w")):
        ctok = P.dma("gpsimd", dst[:, :], D[src].ap(), "ldc2")
    ctok = P.dma("gpsimd", maskrow_t[:, :], D["maskrow"].ap(), "ldc2")
    LDC2 = ctok
    for eng in ("vector", "scalar", "tensor", "sync"):
        P.wait(eng, "ldc2", LDC2)
        P.wait(eng, "spx", sp_tok)
    P.wait("vector", "v", IOT_V)

    es8a = contextlib.ExitStack()
    pp_tmp = es8a.enter_context(nc.sbuf_tensor("pp_tmp", [16, S], F32))
    ps_m = es8a.enter_context(nc.psum_tensor("ps_m", [16, S], F32))
    ps_q = es8a.enter_context(nc.psum_tensor("ps_q", [16, S], F32))
    sqb = [kvn, e_sb]
    ppd = [0, 0]
    pp_mm = [0, 0]
    sq_mm = [0, 0]
    for h2 in range(16):
        bi = h2 % 2
        buf = kraw[bi]
        P.wait("sync", "t", pp_mm[bi])
        ppd[bi] = P.dma("sync", buf[:, :], kvc_dram.ap()[8 + h2], "pp%d" % bi)
        P.wait("scalar", "pp%d" % bi, ppd[bi])
        P.wait("scalar", "t", sq_mm[bi])
        at = P.op("scalar", lambda E, buf=buf, bi=bi: E.activation(
            sqb[bi][:, :], buf[:, :], AF.Square), inc="a")
        P.wait("tensor", "pp%d" % bi, ppd[bi])
        for sl in range(0, S, 512):
            P.op("tensor", lambda E, h2=h2, sl=sl, buf=buf: E.matmul(
                ps_m[0:16, sl:sl + 512], sel16c[:, h2 * 16:(h2 + 1) * 16],
                buf[:, sl:sl + 512], start=(h2 == 0), stop=(h2 == 15)))
        pp_mm[bi] = P.inc_last("tensor", "t")
        P.wait("tensor", "a", at)
        for sl in range(0, S, 512):
            P.op("tensor", lambda E, h2=h2, sl=sl, bi=bi: E.matmul(
                ps_q[0:16, sl:sl + 512], sel16c[:, h2 * 16:(h2 + 1) * 16],
                sqb[bi][:, sl:sl + 512], start=(h2 == 0), stop=(h2 == 15)))
        sq_mm[bi] = P.inc_last("tensor", "t")
    PP_MM = list(pp_mm)
    SQ_MM = list(sq_mm)
    P.wait("vector", "t", sq_mm[1])
    P.wait("vector", "t", pp_mm[1])
    vt = P.op("vector", lambda E: E.tensor_scalar(
        Smat[0:16, :], ps_m[0:16, :], fc[0:16, 6:7], None, ALU.mult), inc="v")
    vt = P.op("vector", lambda E: E.tensor_scalar(
        pp_tmp[0:16, :], ps_q[0:16, :], fc[0:16, 6:7], None, ALU.mult),
        inc="v")
    PRE_PS_V = vt
    P.wait("vector", "v", vt)
    vt = P.op("vector", lambda E: E.tensor_mul(
        Qmat[0:16, :], Smat[0:16, :], Smat[0:16, :]), inc="v")
    P.wait("vector", "v", vt)
    vt = P.op("vector", lambda E: E.tensor_sub(
        Qmat[0:16, :], pp_tmp[0:16, :], Qmat[0:16, :]), inc="v")
    P.wait("scalar", "v", vt)
    at = P.op("scalar", lambda E: E.activation(
        pp_tmp[0:16, :], Qmat[0:16, :], AF.Sqrt, bias=fc[0:16, 0:1]),
        inc="a")
    P.wait("vector", "a", at)
    vt = P.op("vector", lambda E: E.reciprocal(
        Qmat[0:16, :], pp_tmp[0:16, :]), inc="v")
    P.wait("vector", "v", vt)
    vt = P.op("vector", lambda E: E.scalar_tensor_tensor(
        Smat[0:16, :], Smat[0:16, :], -1.0, Qmat[0:16, :],
        ALU.mult, ALU.mult), inc="v")
    PRE_V = vt
    P.wait("vector", "v", PRE_V)
    P.wait("tensor", "v", PRE_V)
    es8a.close()

    def emit_rope(src_pe, tab1, tab2, dst, Wd):
        mt = 0
        for j in range(4):
            b = 32 * j
            pb = b + 32 if j % 2 == 0 else b - 32
            tab = tab1 if j % 2 == 0 else tab2
            P.op("vector", lambda E, b=b, tab=tab: E.tensor_mul(
                rsA[b:b + 32, 0:Wd], src_pe[b:b + 32, 0:Wd],
                tab[b:b + 32, 0:Wd]))
            mt = P.op("vector", lambda E, b=b, pb=pb, tab=tab: E.tensor_mul(
                rsB[b:b + 32, 0:Wd], src_pe[pb:pb + 32, 0:Wd],
                tab[pb:pb + 32, 0:Wd]), inc="v")
        P.wait("vector", "v", mt)
        ft = 0
        for j in range(4):
            b = 32 * j
            op = ALU.subtract if j % 2 == 0 else ALU.add
            ft = P.op("vector", lambda E, b=b, op=op: E.tensor_tensor(
                dst[b:b + 32, 0:Wd], rsA[b:b + 32, 0:Wd],
                rsB[b:b + 32, 0:Wd], op), inc="v")
        P.wait("vector", "v", ft)
        return mt, ft

    kr_free_v = [0, 0, 0]
    kr_free_t = [PP_MM[0], PP_MM[1], 0]
    kvn_gate_t = SQ_MM[0]
    esb_gate_t = SQ_MM[1]
    wq_free_t = 0
    qp_gate_t = 0
    knope_gate_t = 0
    prev_exp_a = 0
    prev_max_v = 0
    prev_evac_v = PRE_PS_V
    av_mm_t = 0
    rb_mm_t = 0
    rbcp_v = 0
    avt_v = 0
    rr_v = 0
    pT_ev = 0

    for hp in range(8):
        P.wait("sync", "v", kr_free_v[0])
        P.wait("sync", "t", kr_free_t[0])
        k0 = P.dma("sync", kraw[0][:, :], kvc_dram.ap()[hp], "kv0")
        P.wait("sync", "v", kr_free_v[1])
        P.wait("sync", "t", kr_free_t[1])
        k1 = P.dma("sync", kraw[1][:, :], kvc_dram.ap()[8 + 2 * hp], "kv1")
        P.wait("sync", "v", kr_free_v[2])
        P.wait("sync", "t", kr_free_t[2])
        k2 = P.dma("sync", kraw[2][:, :], kvc_dram.ap()[9 + 2 * hp], "kv2")
        P.wait("gpsimd", "t", wq_free_t)
        wqt = P.dma("gpsimd", wq_sb[:, :], D["wqblk"].ap()[hp], "wq")

        es_qp = contextlib.ExitStack()
        ps_qp = [es_qp.enter_context(
            nc.psum_tensor("ps_qp%d_%d" % (i, hp), [128, TOK], F32))
            for i in range(3)]
        P.wait("tensor", "wq", wqt)
        P.wait("tensor", "a", prev_exp_a)
        P.wait("tensor", "v", prev_evac_v)
        for k in range(12):
            for m in range(3):
                for sl in (0, 512):
                    P.op("tensor", lambda E, k=k, m=m, sl=sl: E.matmul(
                        ps_qp[m][:, sl:sl + 512],
                        wq_sb[:, (m * 12 + k) * 128:(m * 12 + k + 1) * 128],
                        qcn[:, k * TOK + sl:k * TOK + sl + 512],
                        start=(k == 0), stop=(k == 11)))
        wq_free_t = P.inc_last("tensor", "t")
        P.wait("vector", "t", wq_free_t)
        P.wait("vector", "t", qp_gate_t)
        qp_ev = 0
        for m in range(3):
            qp_ev = P.op("vector", lambda E, m=m: E.tensor_copy(
                qp_sb[:, m * TOK:(m + 1) * TOK], ps_qp[m][:, :]), inc="v")
        P.wait("vector", "v", qp_ev)
        P.wait("tensor", "v", qp_ev)
        es_qp.close()

        P.wait("vector", "kv0", k0)
        P.wait("vector", "t", qp_gate_t)
        krmul, krv = emit_rope(kraw[0], tabk1, tabk2, kpe_rope, S)
        kr_free_v[0] = krmul
        kr_free_t[0] = 0
        qmul, qrv = emit_rope(qp_sb[:, 2 * TOK:3 * TOK], tabq1, tabq2,
                              qrope, TOK)
        ROPE_V = qrv
        P.wait("tensor", "v", ROPE_V)

        for par in range(2):
            h = 2 * hp + par
            kr = kraw[1 + par]
            ksem, ktok = ("kv1", k1) if par == 0 else ("kv2", k2)

            es_kv = contextlib.ExitStack()
            ps_a = es_kv.enter_context(
                nc.psum_tensor("ps_a%d" % h, [128, 512], F32))
            ps_b = es_kv.enter_context(
                nc.psum_tensor("ps_b%d" % h, [128, 512], F32))
            P.wait("vector", ksem, ktok)
            P.wait("vector", "t", kvn_gate_t)
            P.wait("tensor", "a", prev_exp_a)
            P.wait("tensor", "v", prev_evac_v)
            ab_read = 0
            t1_tok = 0
            for sl in (0, 512, 1024, 1536):
                P.wait("tensor", "v", ab_read)
                P.op("tensor", lambda E, h=h, sl=sl: E.matmul(
                    ps_a[:, :], sel16w[0:16, h * 128:(h + 1) * 128],
                    Qmat[0:16, sl:sl + 512], start=True, stop=True))
                mmt = P.op("tensor", lambda E, h=h, sl=sl: E.matmul(
                    ps_b[:, :], sel16w[0:16, h * 128:(h + 1) * 128],
                    Smat[0:16, sl:sl + 512], start=True, stop=True), inc="t")
                P.wait("vector", "t", mmt)
                t1_tok = P.op("vector", lambda E, kr=kr, sl=sl: E.tensor_mul(
                    kvt1[:, :], kr[:, sl:sl + 512], ps_a[:, :]), inc="v")
                P.wait("vector", "v", t1_tok)
                ab_read = P.op("vector", lambda E, sl=sl:
                               E.scalar_tensor_tensor(
                                   kvn[:, sl:sl + 512], ps_b[:, :],
                                   kvnb[:, 0:1], kvt1[:, :],
                                   ALU.add, ALU.add), inc="v")
            kr_free_v[1 + par] = t1_tok
            kr_free_t[1 + par] = 0
            KVN_V = ab_read
            P.wait("tensor", "v", KVN_V)
            es_kv.close()

            es_kp = contextlib.ExitStack()
            ps_k = [es_kp.enter_context(
                nc.psum_tensor("ps_k%d_%d" % (i, h), [128, 512], F32))
                for i in range(2)]
            P.wait("vector", "t", knope_gate_t)
            kp_ev = [0, 0]
            for i4, sl in enumerate((0, 512, 1024, 1536)):
                bi = i4 % 2
                P.wait("tensor", "v", kp_ev[bi])
                mmt = P.op("tensor", lambda E, sl=sl, bi=bi: E.matmul(
                    ps_k[bi][:, :], wkv[:, 0:128], kvn[:, sl:sl + 512],
                    start=True, stop=True), inc="t")
                P.wait("vector", "t", mmt)
                kp_ev[bi] = P.op("vector", lambda E, sl=sl, bi=bi:
                                 E.tensor_copy(knope[:, sl:sl + 512],
                                               ps_k[bi][:, :]), inc="v")
            for grp in range(4):
                bi = grp % 2
                P.wait("tensor", "v", kp_ev[bi])
                for tc4 in range(4):
                    tc = grp * 4 + tc4
                    P.op("tensor", lambda E, tc=tc, tc4=tc4, bi=bi: E.matmul(
                        ps_k[bi][:, tc4 * 128:(tc4 + 1) * 128],
                        kvn[:, tc * 128:(tc + 1) * 128], wkv[:, 128:256],
                        start=True, stop=True))
                mmt = P.inc_last("tensor", "t")
                kvn_gate_t = mmt
                P.wait("vector", "t", mmt)
                kp_ev[bi] = P.op("vector", lambda E, grp=grp, bi=bi:
                                 E.tensor_copy(
                                     vtm[:, grp * 512:(grp + 1) * 512],
                                     ps_k[bi][:, :]), inc="v")
            KVP_V = kp_ev[1]
            P.wait("tensor", "v", kp_ev[0])
            P.wait("tensor", "v", kp_ev[1])
            es_kp.close()

            es_sc = contextlib.ExitStack()
            ps_sc = es_sc.enter_context(
                nc.psum_tensor("ps_sc%d" % h, [128, 2048], F32))
            ps_tr = es_sc.enter_context(
                nc.psum_tensor("ps_tr%d" % h, [128, 512], BF16))
            av_ps = es_sc.enter_context(
                nc.psum_tensor("av_ps%d" % h, [128, 512], F32))
            ps_rb = es_sc.enter_context(
                nc.psum_tensor("ps_rb%d" % h, [128, 512], F32))
            ps_rr = es_sc.enter_context(
                nc.psum_tensor("ps_rr%d" % h, [1, 128], F32))
            exp_a = prev_exp_a
            max_v = prev_max_v
            sc_tt = 0
            tr_last_t = esb_gate_t
            for g in range(2):
                for p4 in range(4):
                    p = 4 * g + p4
                    Wp = (p + 1) * 128
                    moff = (p * (p + 1) // 2) * 128
                    qoff = par * TOK + p * 128
                    P.wait("tensor", "a", exp_a)
                    P.wait("tensor", "v", max_v)
                    for sl in range(0, Wp, 512):
                        ww = min(512, Wp - sl)
                        P.op("tensor", lambda E, qoff=qoff, sl=sl, ww=ww:
                             E.matmul(ps_sc[:, sl:sl + ww],
                                      qp_sb[:, qoff:qoff + 128],
                                      knope[:, sl:sl + ww],
                                      start=True, stop=False))
                        P.op("tensor", lambda E, par=par, p=p, sl=sl, ww=ww:
                             E.matmul(ps_sc[:, sl:sl + ww],
                                      qrope[par * 64:(par + 1) * 64,
                                            p * 128:(p + 1) * 128],
                                      kpe_rope[par * 64:(par + 1) * 64,
                                               sl:sl + ww],
                                      start=False, stop=True))
                    for sl in range(0, Wp, 512):
                        ww = min(512, Wp - sl)
                        P.op("tensor", lambda E, qoff=qoff, sl=sl, ww=ww:
                             E.matmul(ps_sc[:, 1024 + sl:1024 + sl + ww],
                                      qp_sb[:, qoff:qoff + 128],
                                      knope[:, 1024 + sl:1024 + sl + ww],
                                      start=True, stop=False))
                        P.op("tensor", lambda E, moff=moff, sl=sl, ww=ww:
                             E.matmul(ps_sc[:, 1024 + sl:1024 + sl + ww],
                                      onesrow_bf[:, :],
                                      maskrow[0:1, moff + sl:moff + sl + ww],
                                      start=False, stop=False))
                        P.op("tensor", lambda E, par=par, p=p, sl=sl, ww=ww:
                             E.matmul(ps_sc[:, 1024 + sl:1024 + sl + ww],
                                      qrope[par * 64:(par + 1) * 64,
                                            p * 128:(p + 1) * 128],
                                      kpe_rope[par * 64:(par + 1) * 64,
                                               1024 + sl:1024 + sl + ww],
                                      start=False, stop=True))
                    sc_tt = P.inc_last("tensor", "t")
                    P.wait("vector", "t", sc_tt)
                    P.op("vector", lambda E, p=p: E.tensor_add(
                        diag[:, :], ps_sc[:, p * 128:(p + 1) * 128],
                        tri[:, :]))
                    P.op("vector", lambda E, Wp=Wp: E.reduce_max(
                        mxcol[:, 0:1], ps_sc[:, 0:Wp], axis=AX.X))
                    vt = P.op("vector", lambda E, Wp=Wp: E.reduce_max(
                        mxcol[:, 1:2], ps_sc[:, 1024:1024 + Wp], axis=AX.X),
                        inc="v")
                    P.wait("vector", "v", vt)
                    vt = P.op("vector", lambda E: E.tensor_tensor(
                        mxcol[:, 2:3], mxcol[:, 0:1], mxcol[:, 1:2],
                        ALU.max), inc="v")
                    P.wait("vector", "v", vt)
                    max_v = P.op("vector", lambda E: E.tensor_scalar(
                        mxcol[:, 4:5], mxcol[:, 2:3], -SC_ATT, None,
                        ALU.mult), inc="v")
                    P.wait("scalar", "v", max_v)
                    P.wait("scalar", "t", sc_tt)
                    P.wait("scalar", "t", tr_last_t)
                    if p > 0:
                        P.op("scalar", lambda E, p=p: E.activation(
                            e_sb[:, 0:p * 128], ps_sc[:, 0:p * 128], AF.Exp,
                            bias=mxcol[:, 4:5], scale=fc[:, 2:3],
                            accum_out=acc_col[:, 0:1]), inc="a")
                    P.op("scalar", lambda E, p=p, Wp=Wp: E.activation(
                        e_sb[:, p * 128:Wp], diag[:, :], AF.Exp,
                        bias=mxcol[:, 4:5], scale=fc[:, 2:3],
                        accum_out=acc_col[:, 1:2]), inc="a")
                    exp_a = P.op("scalar", lambda E, Wp=Wp: E.activation(
                        e_sb[:, Wp:2 * Wp], ps_sc[:, 1024:1024 + Wp], AF.Exp,
                        bias=mxcol[:, 4:5], scale=fc[:, 2:3],
                        accum_out=acc_col[:, 2:3]), inc="a")
                    P.wait("vector", "a", exp_a)
                    if p > 0:
                        vt = P.op("vector", lambda E: E.tensor_add(
                            acc_col[:, 4:5], acc_col[:, 0:1],
                            acc_col[:, 1:2]), inc="v")
                        P.wait("vector", "v", vt)
                        vt = P.op("vector", lambda E: E.tensor_add(
                            acc_col[:, 4:5], acc_col[:, 4:5],
                            acc_col[:, 2:3]), inc="v")
                    else:
                        vt = P.op("vector", lambda E: E.tensor_add(
                            acc_col[:, 4:5], acc_col[:, 1:2],
                            acc_col[:, 2:3]), inc="v")
                    P.wait("vector", "v", vt)
                    rec_v = P.op("vector", lambda E: E.reciprocal(
                        rec_col[:, 0:1], acc_col[:, 4:5]), inc="v")
                    P.wait("tensor", "v", rec_v)
                    P.wait("tensor", "v", rr_v)
                    rrt = P.op("tensor", lambda E: E.transpose(
                        ps_rr[0:1, 0:128], rec_col[:, 0:1], identf[:, :]),
                        inc="t")
                    P.wait("vector", "t", rrt)
                    P.wait("vector", "t", rb_mm_t)
                    rr_v = P.op("vector", lambda E, p4=p4: E.tensor_copy(
                        recrow[0:1, p4 * 128:(p4 + 1) * 128],
                        ps_rr[0:1, 0:128]), inc="v")
                    P.wait("tensor", "a", exp_a)
                    for j0 in range(0, 2 * (p + 1), 4):
                        jn = min(j0 + 4, 2 * (p + 1))
                        P.wait("tensor", "v", pT_ev)
                        for j in range(j0, jn):
                            P.op("tensor", lambda E, j=j, j0=j0,
                                 st=(j == j0), sp=(j == jn - 1): E.matmul(
                                     ps_tr[:, (j - j0) * 128:
                                           (j - j0 + 1) * 128],
                                     e_sb[:, j * 128:(j + 1) * 128],
                                     ident[:, :], start=st, stop=sp,
                                     is_transpose=True))
                        tr_last_t = P.inc_last("tensor", "t")
                        P.wait("vector", "t", tr_last_t)
                        P.wait("vector", "t", av_mm_t)
                        for j in range(j0, jn):
                            kc = j if j <= p else 8 + (j - (p + 1))
                            pT_ev = P.op("vector", lambda E, kc=kc, p4=p4,
                                         j=j, j0=j0: E.tensor_copy(
                                             pT[kc][:, p4 * 128:
                                                    (p4 + 1) * 128],
                                             ps_tr[:, (j - j0) * 128:
                                                   (j - j0 + 1) * 128]),
                                         inc="v")
                P.wait("tensor", "v", pT_ev)
                P.wait("tensor", "v", avt_v)
                first = True
                for reg in range(2):
                    for kcl in range(4 * g + 4):
                        kc = kcl + 8 * reg
                        a0 = max(0, kcl - 4 * g) * 128
                        last = (reg == 1 and kcl == 4 * g + 3)
                        P.op("tensor", lambda E, kc=kc, a0=a0, st=first,
                             sp=last: E.matmul(
                                 av_ps[:, a0:512],
                                 vtm[:, kc * 128:(kc + 1) * 128],
                                 pT[kc][:, a0:512], start=st, stop=sp))
                        first = False
                av_mm_t = P.inc_last("tensor", "t")
                knope_gate_t = av_mm_t
                P.wait("tensor", "v", rr_v)
                P.wait("tensor", "v", rbcp_v)
                rb_mm_t = P.op("tensor", lambda E: E.matmul(
                    ps_rb[:, :], onesrow_f[:, :], recrow[0:1, 0:512],
                    start=True, stop=True), inc="t")
                P.wait("vector", "t", rb_mm_t)
                rbcp_v = P.op("vector", lambda E: E.tensor_copy(
                    rb_sb[:, :], ps_rb[:, :]), inc="v")
                P.wait("vector", "v", rbcp_v)
                P.wait("vector", "t", av_mm_t)
                avt_v = P.op("vector", lambda E: E.tensor_mul(
                    avt[:, :], av_ps[:, :], rb_sb[:, :]), inc="v")
                P.wait("vector", "v", avt_v)
                prev_evac_v = P.op("vector", lambda E, h=h, g=g:
                                   E.tensor_scalar(
                                       attn_all[:, h * TOK + g * 512:
                                                h * TOK + (g + 1) * 512],
                                       avt[:, :],
                                       iot[:, (h // 2) * 4:(h // 2) * 4 + 1],
                                       None, ALU.add), inc="v")
            prev_exp_a = exp_a
            prev_max_v = max_v
            esb_gate_t = tr_last_t
            es_sc.close()
        qp_gate_t = sc_tt
    fence_wait(mark_fence())
    es8.close()

    es9 = contextlib.ExitStack()
    es9w = contextlib.ExitStack()
    x2 = es9.enter_context(nc.sbuf_tensor("x_own2", [128, 16 * TOK], F32))
    owt2 = [es9w.enter_context(
        nc.sbuf_tensor("wo_t%d" % i, [128, 2 * 16 * 128], BF16))
        for i in range(2)]
    pso2 = [es9w.enter_context(
        nc.psum_tensor("pso2_%d" % i, [128, TOK], F32)) for i in range(4)]
    P.wait("sync", "spx", sp_tok)
    xre = 0
    for c in range(16):
        xre = P.dma("sync", x2[:, c * TOK:(c + 1) * TOK], xres_dram.ap()[c],
                    "xre")
    P.wait("vector", "xre", xre)
    wo_tok = [0, 0]
    wo_free = [0, 0]
    wo_ev = [0, 0, 0, 0]
    for mg in range(8):
        bi = mg % 2
        P.wait("gpsimd", "t", wo_free[bi])
        wo_tok[bi] = P.dma("gpsimd", owt2[bi][:, :],
                           D["woblk"].ap()[:, 2 * mg:2 * mg + 2, :, :],
                           "wo%d" % bi)
        P.wait("tensor", "wo%d" % bi, wo_tok[bi])
        for i in range(2):
            ps = pso2[2 * bi + i]
            P.wait("tensor", "v", wo_ev[2 * bi + i])
            for k in range(16):
                for sl in (0, 512):
                    P.op("tensor", lambda E, bi=bi, i=i, k=k, sl=sl, ps=ps:
                         E.matmul(ps[:, sl:sl + 512],
                                  owt2[bi][:, (i * 16 + k) * 128:
                                           (i * 16 + k + 1) * 128],
                                  attn_all[:, k * TOK + sl:k * TOK + sl + 512],
                                  start=(k == 0), stop=(k == 15)))
            tt = P.inc_last("tensor", "t")
            wo_free[bi] = tt
            mp = 2 * mg + i
            P.wait("vector", "t", tt)
            vt = P.op("vector", lambda E, ps=ps, mp=mp: E.tensor_add(
                x2[:, mp * TOK:(mp + 1) * TOK],
                x2[:, mp * TOK:(mp + 1) * TOK], ps[:, :]), inc="v")
            wo_ev[2 * bi + i] = vt
    WO_V = P.cnt["v"]
    fence_wait(mark_fence())
    es9w.close()

    if dev_stop == "attn":
        _emit_store(P, nc, x2, out_d, WO_V)
        return _run(nc, P, top, in_maps)

    lnx2 = arena[:, 0:16 * TOK]
    comb2 = arena[:, 16 * TOK:32 * TOK]
    P.wait("vector", "v", WO_V)
    ffn2_v = emit_ffn(2, x2, "own", xtile=x2, ext=(lnx2, comb2))
    _emit_store(P, nc, x2, out_d, ffn2_v)
    return _run(nc, P, top, in_maps)




def _emit_store(P, nc, x_own, out_d, wait_v):
    P.wait("sync", "v", wait_v)
    st = 0
    for c in range(16):
        st = P.dma("sync", out_d.ap()[c], x_own[:, c * TOK:(c + 1) * TOK], "st")
    P.wait("sync", "st", st)


def _run(nc, P, top, in_maps):
    with nc.Block() as block:
        @block.sync
        def _(E):
            for f in P.q["sync"]:
                f(E)

        @block.tensor
        def _(E):
            for f in P.q["tensor"]:
                f(E)

        @block.vector
        def _(E):
            for f in P.q["vector"]:
                f(E)

        @block.scalar
        def _(E):
            for f in P.q["scalar"]:
                f(E)

        @block.gpsimd
        def _(E):
            for f in P.q["gpsimd"]:
                f(E)
    top.close()
    res = run_bass_kernel_spmd(nc, in_maps, core_ids=list(range(NCORES)))
    return res


def kernel(**inputs):
    in_maps = host_inputs(inputs)
    res = build_and_run(in_maps, DEV_STOP)
    x = np.asarray(inputs["L0_attn_x"])
    out = np.empty((B, S, H), np.float32)
    for core in range(NCORES):
        b, z = core // 2, core % 2
        o = res.results[core]["out"]
        feat = o.reshape(16 * 128, TOK)
        toks = feat.T
        own = ZIG[z]
        for i, c in enumerate(own):
            out[b, c * 128:(c + 1) * 128, :] = toks[i * 128:(i + 1) * 128]
    return out


# version 02
# speedup vs baseline: 1.0065x; 1.0065x over previous
import math
import contextlib
import numpy as np
import concourse.bass as bass
import concourse.mybir as mybir
from concourse.bass_utils import run_bass_kernel_spmd

F32 = mybir.dt.float32
BF16 = mybir.dt.bfloat16
U32 = mybir.dt.uint32
AF = mybir.ActivationFunctionType
ALU = mybir.AluOpType
AX = mybir.AxisListType

NCORES = 8
B, S, H = 4, 2048, 2048
TOK = 1024
NH, NOPE, ROPE, VD, QLR = 16, 128, 64, 128, 1536
IH, ID = 8, 128
NE = 8
EPS = 1e-5
NEG = -1.0e30
ZIG = [[0, 3, 4, 7, 8, 11, 12, 15], [1, 2, 5, 6, 9, 10, 13, 14]]
SC_ATT = 1.0 / math.sqrt(NOPE + ROPE)
SC_SP = 1.0 / math.sqrt(ID)

ENGS = ("sync", "tensor", "vector", "scalar", "gpsimd")

DEV_STOP = "full"



def _block_kxm(w, KC, MC):
    return np.ascontiguousarray(w.reshape(KC, 128, MC, 128).transpose(1, 2, 0, 3))


def _ckv_col_perm():
    cols = []
    for pair in range(8):
        for h in (2 * pair, 2 * pair + 1):
            base = h * (ROPE + NOPE)
            cols.extend(base + np.arange(0, ROPE, 2))
            cols.extend(base + np.arange(1, ROPE, 2))
    for h in range(NH):
        base = h * (ROPE + NOPE)
        cols.extend(base + ROPE + np.arange(NOPE))
    return np.array(cols)


def _q_col_perm_pair(pair):
    a, b = 2 * pair, 2 * pair + 1
    cols = []
    for h in (a, b):
        cols.extend(h * (NOPE + ROPE) + np.arange(NOPE))
    for h in (a, b):
        base = h * (NOPE + ROPE) + NOPE
        cols.extend(base + np.arange(0, ROPE, 2))
        cols.extend(base + np.arange(1, ROPE, 2))
    return np.array(cols)


def _rope_tabs(positions):
    f = 1.0 / 10000.0 ** (np.arange(0, ROPE, 2, dtype=np.float64) / ROPE)
    ang = np.outer(f, positions.astype(np.float64))
    c, s = np.cos(ang), np.sin(ang)
    tab1 = np.concatenate([c, s, c, s], 0)
    tab2 = np.concatenate([s, c, s, c], 0)
    return (tab1.astype(np.float32).astype(np.dtype("float32")),
            tab2.astype(np.float32))


def host_inputs(ins):
    import ml_dtypes
    x = np.asarray(ins["L0_attn_x"], dtype=np.float32)
    sh = {}
    sh["eblk"] = np.stack([
        _block_kxm(np.asarray(ins["L0_ffn_expert_w"], np.float32)[e], 16, 16)
        for e in range(NE)])
    sh["owblk"] = _block_kxm(np.asarray(ins["L0_ffn_out_w"], np.float32), 16, 16)
    sh["cqblk"] = _block_kxm(np.asarray(ins["L0_attn_W_cq"], np.float32), 16, 12)
    ckv = np.asarray(ins["L0_attn_W_ckv"], np.float32)[:, _ckv_col_perm()]
    sh["ckvblk"] = _block_kxm(ckv, 16, 24)
    wq = np.asarray(ins["L0_attn_W_q"], np.float32)
    sh["wqblk"] = np.stack([
        _block_kxm(wq[:, _q_col_perm_pair(p)], 12, 3) for p in range(8)])
    sh["ipblk"] = np.ascontiguousarray(
        np.asarray(ins["L0_attn_W_ip"], np.float32).reshape(16, 128, IH * ID))
    sh["wrk"] = np.ascontiguousarray(
        np.asarray(ins["L0_ffn_router_w"], np.float32).reshape(16, 128, NE))
    sh["wigk"] = np.ascontiguousarray(
        np.asarray(ins["L0_attn_W_ig"], np.float32).reshape(16, 128, IH))
    sh["rb"] = np.asarray(ins["L0_ffn_router_b"], np.float32).reshape(1, NE)
    eb = np.asarray(ins["L0_ffn_expert_b"], np.float32)
    sh["ebias"] = np.ascontiguousarray(
        eb.reshape(NE, 16, 128).transpose(2, 0, 1).reshape(128, NE * 16))
    sh["obias"] = np.ascontiguousarray(
        np.asarray(ins["L0_ffn_out_b"], np.float32).reshape(16, 128).T)
    sh["woblk"] = _block_kxm(np.asarray(ins["L0_attn_W_o"], np.float32), 16, 16)
    sh["wkv"] = np.asarray(ins["L0_attn_W_kv"], np.float32)
    sh["kvnw"] = np.asarray(ins["L0_attn_kv_norm_w"], np.float32).reshape(128, 1)
    sh["kvnb"] = np.asarray(ins["L0_attn_kv_norm_b"], np.float32).reshape(128, 1)
    sh["qnw"] = np.ascontiguousarray(
        np.asarray(ins["L0_attn_q_norm_w"], np.float32).reshape(12, 128).T)
    sh["qnb"] = np.ascontiguousarray(
        np.asarray(ins["L0_attn_q_norm_b"], np.float32).reshape(12, 128).T)
    sh["wsq"] = np.asarray(ins["L0_attn_W_sq"], np.float32)
    sh["wsk"] = np.asarray(ins["L0_attn_W_sk"], np.float32)
    sh["wsv"] = np.asarray(ins["L0_attn_W_sv"], np.float32)
    sh["wio"] = np.ascontiguousarray(
        np.asarray(ins["L0_attn_W_io"], np.float32)[:, :VD])
    sh["ident"] = np.eye(128, dtype=np.float32)
    sh["ones"] = np.ones((128, 1), np.float32)
    sh["onesrow"] = np.ones((1, 128), np.float32)
    sh["tri"] = np.where(np.triu(np.ones((128, 128), bool), k=1),
                         np.float32(NEG), np.float32(0.0)).astype(np.float32)
    esel = np.zeros((8, NE * 128), np.float32)
    for e in range(NE):
        esel[e, e * 128:(e + 1) * 128] = 1.0
    sh["esel"] = esel
    fc = np.zeros((128, 8), np.float32)
    fc[:, 0] = EPS
    fc[:, 1] = 0.5
    fc[:, 2] = SC_ATT
    fc[:, 3] = SC_SP
    fc[:, 4] = 0.125
    fc[:, 5] = 1.0
    fc[:, 6] = 1.0 / NOPE
    fc[:, 7] = -1.0 / NOPE
    sh["fconsts"] = fc
    s16c = np.zeros((128, 16 * 16), np.float32)
    for hh in range(16):
        s16c[:, hh * 16 + hh] = 1.0
    sh["sel16c"] = s16c
    s16w = np.zeros((16, 16 * 128), np.float32)
    kvw = np.asarray(ins["L0_attn_kv_norm_w"], np.float32).reshape(128)
    for hh in range(16):
        s16w[hh, hh * 128:(hh + 1) * 128] = kvw
    sh["sel16w"] = s16w

    in_maps = []
    for core in range(NCORES):
        b, z = core // 2, core % 2
        own, part = ZIG[z], ZIG[1 - z]
        corder = np.concatenate(
            [np.arange(c * 128, (c + 1) * 128) for c in own + part])
        xc = x[b][corder]
        m = dict(sh)
        m["xT_own"] = np.ascontiguousarray(xc[:TOK].T.reshape(16, 128, TOK))
        m["xT_part"] = np.ascontiguousarray(xc[TOK:].T.reshape(16, 128, TOK))
        pos = np.concatenate([np.arange(c * 128, (c + 1) * 128) for c in own])
        t1, t2 = _rope_tabs(pos)
        m["tabq1"], m["tabq2"] = t1, t2
        posk = np.concatenate(
            [np.arange(c * 128, (c + 1) * 128) for c in own + part])
        t1k, t2k = _rope_tabs(posk)
        m["tabk1"], m["tabk2"] = t1k, t2k
        pm = np.zeros((1, 36 * 128), np.float32)
        off = 0
        for p in range(8):
            for mm in range(p + 1):
                if not (part[mm] < own[p]):
                    pm[:, (off + mm) * 128:(off + mm + 1) * 128] = NEG
            off += p + 1
        m["maskrow"] = pm
        in_maps.append(m)
    return in_maps



class Prog:
    def __init__(self, nc, top):
        self.nc = nc
        self.top = top
        self.sems = {}
        self.cnt = {}
        self.q = {e: [] for e in ENGS}

    def sem(self, name):
        if name not in self.sems:
            self.sems[name] = self.top.enter_context(
                self.nc.semaphore("sem_" + name))
            self.cnt[name] = 0
        return self.sems[name]

    def wait(self, eng, sem, tok):
        if tok is None or tok <= 0:
            return
        h = self.sem(sem)
        self.q[eng].append(lambda E, h=h, t=tok: E.wait_ge(h, t))

    def op(self, eng, fn, inc=None):
        if inc is not None:
            self.sem(inc)
            self.cnt[inc] += 1
            tok = self.cnt[inc]
            h = self.sems[inc]
            self.q[eng].append(lambda E, fn=fn, h=h: fn(E).then_inc(h, 1))
            return tok
        self.q[eng].append(fn)
        return None

    def inc_last(self, eng, sem):
        self.sem(sem)
        self.cnt[sem] += 1
        tok = self.cnt[sem]
        h = self.sems[sem]
        prev = self.q[eng].pop()
        self.q[eng].append(lambda E, f=prev, h=h: f(E).then_inc(h, 1))
        return tok

    def dma(self, eng, out, in_, sem):
        self.sem(sem)
        self.cnt[sem] += 16
        tok = self.cnt[sem]
        h = self.sems[sem]
        self.q[eng].append(
            lambda E, o=out, i=in_, h=h: E.dma_start(out=o, in_=i).then_inc(h, 16))
        return tok

    def idma(self, out, in_, in_offset, sem="g"):
        self.sem(sem)
        self.cnt[sem] += 16
        tok = self.cnt[sem]
        h = self.sems[sem]
        self.q["gpsimd"].append(
            lambda E, o=out, i=in_, io=in_offset, h=h:
            E.indirect_dma_start(out=o, out_offset=None, in_=i, in_offset=io)
            .then_inc(h, 16))
        return tok



def build_and_run(in_maps, dev_stop="full"):
    nc = bass.Bass("TRN2", target_bir_lowering=False, debug=False,
                   detect_race_conditions=False)
    top = contextlib.ExitStack()

    def din(name, shape, dt=F32):
        return nc.dram_tensor(name, list(shape), dt, kind="ExternalInput")

    D = {}
    D["xT_own"] = din("xT_own", [16, 128, TOK])
    D["xT_part"] = din("xT_part", [16, 128, TOK])
    D["eblk"] = din("eblk", [NE, 128, 16, 16, 128])
    D["owblk"] = din("owblk", [128, 16, 16, 128])
    D["cqblk"] = din("cqblk", [128, 12, 16, 128])
    D["ckvblk"] = din("ckvblk", [128, 24, 16, 128])
    D["wqblk"] = din("wqblk", [8, 128, 3, 12, 128])
    D["ipblk"] = din("ipblk", [16, 128, IH * ID])
    D["wrk"] = din("wrk", [16, 128, NE])
    D["wigk"] = din("wigk", [16, 128, IH])
    D["rb"] = din("rb", [1, NE])
    D["ebias"] = din("ebias", [128, NE * 16])
    D["obias"] = din("obias", [128, 16])
    D["woblk"] = din("woblk", [128, 16, 16, 128])
    D["wkv"] = din("wkv", [128, 256])
    D["kvnw"] = din("kvnw", [128, 1])
    D["kvnb"] = din("kvnb", [128, 1])
    D["qnw"] = din("qnw", [128, 12])
    D["qnb"] = din("qnb", [128, 12])
    D["wsq"] = din("wsq", [128, 128])
    D["wsk"] = din("wsk", [128, 128])
    D["wsv"] = din("wsv", [128, 128])
    D["wio"] = din("wio", [128, VD])
    D["ident"] = din("ident", [128, 128])
    D["ones"] = din("ones", [128, 1])
    D["onesrow"] = din("onesrow", [1, 128])
    D["tri"] = din("tri", [128, 128])
    D["esel"] = din("esel", [8, NE * 128])
    D["fconsts"] = din("fconsts", [128, 8])
    D["sel16c"] = din("sel16c", [128, 16 * 16])
    D["sel16w"] = din("sel16w", [16, 16 * 128])
    D["tabq1"] = din("tabq1", [128, TOK])
    D["tabq2"] = din("tabq2", [128, TOK])
    D["tabk1"] = din("tabk1", [128, S])
    D["tabk2"] = din("tabk2", [128, S])
    D["maskrow"] = din("maskrow", [1, 36 * 128])

    out_d = nc.dram_tensor("out", [16, 128, TOK], F32, kind="ExternalOutput")

    kvc_dram = nc.dram_tensor("kvc_sc", [24, 128, S], BF16, kind="Internal")
    ist_dram = nc.dram_tensor("ist_sc", [S, IH * ID], BF16, kind="Internal")
    xres_dram = nc.dram_tensor("xres_sc", [16, 128, TOK], F32, kind="Internal")
    x1p_dram = nc.dram_tensor("x1p_sc", [16, 128, TOK], BF16, kind="Internal")

    sb = nc.alloc_sbuf_tensor
    x_own = sb("x_own", [128, 16 * TOK], F32)
    ident = sb("sb_ident", [128, 128], BF16)
    ones_bf = sb("ones_bf", [128, 1], BF16)
    onesrow_f = sb("onesrow_f", [1, 128], F32)
    wkv = sb("sb_wkv", [128, 256], BF16)
    kvnw = sb("sb_kvnw", [128, 1], F32)
    kvnb = sb("sb_kvnb", [128, 1], F32)
    qnw = sb("sb_qnw", [128, 12], F32)
    qnb = sb("sb_qnb", [128, 12], F32)
    rbias = sb("rbias", [1, NE], F32)
    ebias = sb("sb_ebias", [128, NE * 16], F32)
    obias2 = sb("obias2", [128, 16], F32)
    wr16 = sb("wr16", [128, 16 * NE], BF16)
    wig16 = sb("wig16", [128, 16 * IH], BF16)
    esel = sb("sb_esel", [8, NE * 128], F32)
    fc = sb("sb_fc", [128, 8], F32)
    onesrow_bf = sb("onesrow_bf", [1, 128], BF16)
    iot = sb("iot", [128, 32], F32)

    P = Prog(nc, top)
    fence_scr = sb("fence_scr", [128, 4], F32)
    P.op("vector", lambda E: E.memset(fence_scr[:, :], 0.0))

    def mark_fence():
        tt = P.cnt["t"]
        vt = P.op("vector", lambda E: E.tensor_copy(
            fence_scr[:, 0:1], fence_scr[:, 1:2]), inc="v")
        at = P.op("scalar", lambda E: E.activation(
            fence_scr[:, 2:3], fence_scr[:, 3:4], AF.Copy), inc="a")
        return tt, vt, at

    def fence_wait(toks, engines=("tensor", "vector", "scalar", "gpsimd", "sync")):
        tt, vt, at = toks
        for eng in engines:
            P.wait(eng, "t", tt)
            P.wait(eng, "v", vt)
            P.wait(eng, "a", at)

    for dst, src in ((ident, "ident"), (ones_bf, "ones"), (onesrow_f, "onesrow"),
                     (wkv, "wkv"), (kvnw, "kvnw"), (kvnb, "kvnb"),
                     (qnw, "qnw"), (qnb, "qnb"), (rbias, "rb"),
                     (ebias, "ebias"), (esel, "esel"), (fc, "fconsts"),
                     (onesrow_bf, "onesrow")):
        ctok = P.dma("gpsimd", dst[:, :], D[src].ap(), "ldc")
    for k in range(16):
        ctok = P.dma("gpsimd", wr16[:, k * NE:(k + 1) * NE], D["wrk"].ap()[k], "ldc")
        ctok = P.dma("gpsimd", wig16[:, k * IH:(k + 1) * IH], D["wigk"].ap()[k], "ldc")
    obias_raw = sb("obias_raw", [128, 16], F32)
    ctok = P.dma("gpsimd", obias_raw[:, :], D["obias"].ap(), "ldc")
    identf = sb("sb_identf", [128, 128], F32)
    ctok = P.dma("gpsimd", identf[:, :], D["ident"].ap(), "ldc")
    xtok = 0
    for c in range(16):
        xtok = P.dma("sync", x_own[:, c * TOK:(c + 1) * TOK],
                     D["xT_own"].ap()[c], "ldx")
    CONST_G, CONST_S = ctok, xtok
    P.wait("vector", "ldc", CONST_G)
    P.wait("vector", "ldx", CONST_S)
    P.op("vector", lambda E: E.tensor_scalar_mul(obias2[:, :], obias_raw[:, :], 0.5),
         inc="v")

    def emit_ln(ctx, srcs_fn, T, dst, affine=None, name="ln", nchunks=16,
                nfeat=H, dst_fn=None):
        es = contextlib.ExitStack()
        ps_sum = es.enter_context(nc.psum_tensor(name + "_psum", [1, T], F32))
        ps_sq = es.enter_context(nc.psum_tensor(name + "_pssq", [1, T], F32))
        ps_bc = es.enter_context(nc.psum_tensor(name + "_psbc", [128, T], F32))
        sq_scr = [es.enter_context(nc.sbuf_tensor(name + "_sq%d" % i, [128, T], BF16))
                  for i in range(2)]
        cv_scr = [es.enter_context(nc.sbuf_tensor(name + "_cv%d" % i, [128, T], BF16))
                  for i in range(2)]
        rows = es.enter_context(nc.sbuf_tensor(name + "_rows", [1, 3 * T], F32))
        abc = es.enter_context(nc.sbuf_tensor(name + "_abc", [128, T], F32))
        bbc = es.enter_context(nc.sbuf_tensor(name + "_bbc", [128, T], F32))
        if affine is not None:
            t_scr = [es.enter_context(
                nc.sbuf_tensor(name + "_t%d" % i, [128, T], BF16))
                for i in range(2)]

        pe_tok = [0] * nchunks
        sq_tok = [0] * nchunks
        for c in range(nchunks):
            srcs = srcs_fn(c)
            cv = cv_scr[c % 2]
            sq = sq_scr[c % 2]
            bf_srcs = []
            for (apc, off, w, isf32) in srcs:
                if isf32:
                    if c >= 2:
                        P.wait("vector", "t", pe_tok[c - 2])
                        P.wait("vector", "a", sq_tok[c - 2])
                    vt = P.op("vector", lambda E, a=apc, s=cv, o=off, w=w:
                              E.tensor_copy(s[:, o:o + w], a), inc="v")
                    bf_srcs.append((cv[:, off:off + w], off, w, vt))
                else:
                    bf_srcs.append((apc, off, w, None))
            for i, (apc, off, w, vt) in enumerate(bf_srcs):
                if vt:
                    P.wait("scalar", "v", vt)
                if c >= 2:
                    P.wait("scalar", "t", pe_tok[c - 2])
                at = P.op("scalar", lambda E, a=apc, s=sq, o=off, w=w:
                          E.activation(s[:, o:o + w], a, AF.Square), inc="a")
                sq_tok[c] = at
                if vt:
                    P.wait("tensor", "v", vt)
                for sl in range(0, w, 512):
                    ee = min(sl + 512, w)
                    P.op("tensor", lambda E, a=apc, o=off, sl=sl, ee=ee,
                         st=(c == 0), sp=(c == nchunks - 1):
                         E.matmul(ps_sum[0:1, o + sl:o + ee], ones_bf[:, :],
                                  a[:, sl:ee], start=st, stop=sp))
                P.wait("tensor", "a", at)
                last = None
                for sl in range(0, w, 512):
                    ee = min(sl + 512, w)
                    last = P.op("tensor", lambda E, s=sq, o=off, sl=sl, ee=ee,
                                st=(c == 0), sp=(c == nchunks - 1):
                                E.matmul(ps_sq[0:1, o + sl:o + ee], ones_bf[:, :],
                                         s[:, sl:ee], start=st, stop=sp),
                                inc="t")
                pe_tok[c] = last
        tt = P.cnt["t"]
        P.wait("vector", "t", tt)
        mean = rows[0:1, 0:T]
        m2 = rows[0:1, T:2 * T]
        aux = rows[0:1, 2 * T:3 * T]
        rstd, brow = aux, mean
        P.op("vector", lambda E: E.tensor_scalar_mul(
            mean, ps_sum[0:1, :], 1.0 / nfeat))
        vt = P.op("vector", lambda E: E.tensor_scalar_mul(
            m2, ps_sq[0:1, :], 1.0 / nfeat), inc="v")
        P.wait("scalar", "v", vt)
        at = P.op("scalar", lambda E: E.activation(
            aux, mean, AF.Square), inc="a")
        P.wait("vector", "a", at)
        vt = P.op("vector", lambda E: E.tensor_sub(m2, m2, aux), inc="v")
        P.wait("scalar", "v", vt)
        at = P.op("scalar", lambda E: E.activation(
            aux, m2, AF.Sqrt, bias=fc[0:1, 0:1]), inc="a")
        P.wait("vector", "a", at)
        vt = P.op("vector", lambda E: E.reciprocal(rstd, rstd), inc="v")
        P.wait("vector", "v", vt)
        vt = P.op("vector", lambda E: E.scalar_tensor_tensor(
            brow, mean, -1.0, rstd, ALU.mult, ALU.mult), inc="v")
        P.wait("tensor", "v", vt)
        for sl in range(0, T, 512):
            ee = min(sl + 512, T)
            P.op("tensor", lambda E, sl=sl, ee=ee: E.matmul(
                ps_bc[:, sl:ee], onesrow_f[:, :], rstd[:, sl:ee],
                start=True, stop=True))
        tt = P.inc_last("tensor", "t")
        P.wait("vector", "t", tt)
        vt = P.op("vector", lambda E: E.tensor_copy(abc[:, :], ps_bc[:, :]),
                  inc="v")
        P.wait("tensor", "v", vt)
        for sl in range(0, T, 512):
            ee = min(sl + 512, T)
            P.op("tensor", lambda E, sl=sl, ee=ee: E.matmul(
                ps_bc[:, sl:ee], onesrow_f[:, :], brow[:, sl:ee],
                start=True, stop=True))
        tt = P.inc_last("tensor", "t")
        P.wait("vector", "t", tt)
        vt = P.op("vector", lambda E: E.tensor_copy(bbc[:, :], ps_bc[:, :]),
                  inc="v")
        P.wait("vector", "v", vt)
        mul_tok = [0] * (nchunks + 1)
        vt = 0
        if dst_fn is None:
            def dst_fn(c):
                return dst[:, c * T:(c + 1) * T]

        def emit_mult(c):
            nonlocal vt
            ts = t_scr[c % 2] if affine is not None else None
            for (apc, off, w, isf32) in srcs_fn(c):
                if affine is not None:
                    tgt = ts[:, off:off + w]
                else:
                    tgt = dst_fn(c)[:, off:off + w]
                vt = P.op("vector", lambda E, a=apc, tg=tgt, o=off, w=w:
                          E.tensor_mul(tg, a, abc[:, o:o + w]), inc="v")
            mul_tok[c] = vt

        def emit_add(c):
            nonlocal vt
            P.wait("vector", "v", mul_tok[c])
            ts = t_scr[c % 2] if affine is not None else None
            for (apc, off, w, isf32) in srcs_fn(c):
                if affine is None:
                    tgt = dst_fn(c)[:, off:off + w]
                    vt = P.op("vector", lambda E, tg=tgt, o=off, w=w:
                              E.tensor_add(tg, tg, bbc[:, o:o + w]), inc="v")
                else:
                    wcol, bcol = affine
                    vt = P.op("vector", lambda E, o=off, w=w, ts=ts:
                              E.tensor_add(ts[:, o:o + w], ts[:, o:o + w],
                                           bbc[:, o:o + w]), inc="v")
                    P.wait("vector", "v", vt)
                    vt = P.op("vector", lambda E, o=off, w=w, ts=ts, c=c:
                              E.tensor_scalar(ts[:, o:o + w], ts[:, o:o + w],
                                              wcol[:, c:c + 1], None,
                                              ALU.mult), inc="v")
                    P.wait("vector", "v", vt)
                    vt = P.op("vector", lambda E, c=c, o=off, w=w, ts=ts:
                              E.tensor_scalar(
                                  dst_fn(c)[:, o:o + w],
                                  ts[:, o:o + w], bcol[:, c:c + 1], None,
                                  ALU.add), inc="v")

        for c in range(nchunks):
            emit_mult(c)
            if c >= 1:
                emit_add(c - 1)
        emit_add(nchunks - 1)
        es.close()
        return vt, "v"

    def emit_ffn(idx, src_f32, target, upto=None, xtile=None, ext=None):
        name = "f%d" % idx
        if xtile is None:
            xtile = x_own
        es = contextlib.ExitStack()
        if ext is None:
            lnx = es.enter_context(
                nc.sbuf_tensor(name + "_lnx", [128, 16 * TOK], BF16))
            comb = es.enter_context(
                nc.sbuf_tensor(name + "_comb", [128, 16 * TOK], BF16))
        else:
            lnx, comb = ext

        es_x0 = contextlib.ExitStack()
        if src_f32 is None:
            x0p = es_x0.enter_context(
                nc.sbuf_tensor(name + "_x0p", [128, 16 * TOK], BF16))
            ltoks = []
            for c in range(16):
                ltoks.append(P.dma("gpsimd", x0p[:, c * TOK:(c + 1) * TOK],
                                   D["xT_part"].ap()[c], "ldp"))

            def srcs_fn(c):
                return [(x0p[:, c * TOK:(c + 1) * TOK], 0, TOK, False)]
            P.wait("vector", "ldp", ltoks[-1])
            P.wait("scalar", "ldp", ltoks[-1])
            P.wait("tensor", "ldp", ltoks[-1])
        else:
            def srcs_fn(c):
                return [(src_f32[:, c * TOK:(c + 1) * TOK], 0, TOK, True)]
            P.wait("vector", "ldx", CONST_S)
        P.wait("tensor", "ldc", CONST_G)
        lntok, lnsem = emit_ln(es, srcs_fn, TOK, lnx, name=name + "ln")
        P.wait("tensor", lnsem, lntok)
        P.wait("gpsimd", lnsem, lntok)
        fence_wait(mark_fence())
        es_x0.close()
        if upto == "ln":
            st = 0
            for c in range(16):
                st = P.dma("gpsimd", out_d.ap()[c],
                           lnx[:, c * TOK:(c + 1) * TOK], "st")
            P.wait("gpsimd", "st", st)
            es.close()
            return P.cnt["v"]

        es_gb = contextlib.ExitStack()
        gbc = es_gb.enter_context(
            nc.sbuf_tensor(name + "_gbc", [128, NE * TOK], BF16))
        es_r = contextlib.ExitStack()
        ps_r = es_r.enter_context(nc.psum_tensor(name + "_psr", [128, NE], F32))
        ps_tr = es_r.enter_context(nc.psum_tensor(name + "_pstr", [8, 128], F32))
        lg = es_r.enter_context(nc.sbuf_tensor(name + "_lg", [128, NE], F32))
        vm = es_r.enter_context(nc.sbuf_tensor(name + "_vm", [128, 8], F32))
        cols = es_r.enter_context(nc.sbuf_tensor(name + "_cols", [128, 28], F32))
        acols = es_r.enter_context(nc.sbuf_tensor(name + "_acols", [128, 4], F32))
        ia_t = es_r.enter_context(nc.sbuf_tensor(name + "_ia", [128, NE], F32))
        iab_t = es_r.enter_context(nc.sbuf_tensor(name + "_iab", [128, NE], F32))
        gtm = es_r.enter_context(nc.sbuf_tensor(name + "_gtm", [128, NE], F32))
        gfm = es_r.enter_context(nc.sbuf_tensor(name + "_gfm", [8, TOK], F32))
        ps_bc2 = es_r.enter_context(nc.psum_tensor(name + "_psb2", [128, TOK], F32))
        lg_done = 0
        gfm_done = 0
        tr_tok = 0
        for tm in range(8):
            P.wait("tensor", "v", lg_done)
            for k in range(16):
                P.op("tensor", lambda E, tm=tm, k=k: E.matmul(
                    ps_r[:, :], lnx[:, k * TOK + tm * 128:k * TOK + (tm + 1) * 128],
                    wr16[:, k * NE:(k + 1) * NE],
                    start=(k == 0), stop=False))
            tt = P.op("tensor", lambda E: E.matmul(
                ps_r[:, :], onesrow_f[:, :], rbias[:, :],
                start=False, stop=True), inc="t")
            P.wait("vector", "t", tt)
            vt = P.op("vector", lambda E: E.tensor_copy(lg[:, :], ps_r[:, :]),
                      inc="v")
            lg_done = vt
            P.wait("vector", "v", vt)
            vt = P.op("vector", lambda E: E.max(out=vm[:, :], in_=lg[:, :]),
                      inc="v")
            P.wait("vector", "v", vt)
            vt = P.op("vector", lambda E: E.tensor_sub(
                cols[:, 0:1], vm[:, 0:1], vm[:, 1:2]), inc="v")
            sub_tok = vt
            P.op("vector", lambda E: E.tensor_add(
                cols[:, 4:5], vm[:, 0:1], vm[:, 1:2]))
            vt = P.op("vector", lambda E: E.tensor_add(
                cols[:, 8:9], vm[:, 1:2], vm[:, 2:3]), inc="v")
            P.wait("vector", "v", vt)
            P.op("vector", lambda E: E.tensor_scalar(
                cols[:, 12:13], cols[:, 4:5], 0.5, None, ALU.mult))
            vt = P.op("vector", lambda E: E.tensor_scalar(
                cols[:, 16:17], cols[:, 8:9], 0.5, None, ALU.mult), inc="v")
            thr_tok = vt
            P.wait("scalar", "v", sub_tok)
            at = P.op("scalar", lambda E: E.activation(
                acols[:, 0:1], cols[:, 0:1], AF.Sigmoid), inc="a")
            P.wait("vector", "a", at)
            P.op("vector", lambda E: E.tensor_scalar(
                cols[:, 20:21], acols[:, 0:1], -1.0, 1.0, ALU.mult, ALU.add))
            vt = P.op("vector", lambda E: E.tensor_scalar(
                cols[:, 24:25], acols[:, 0:1], 2.0, -1.0, ALU.mult, ALU.add),
                inc="v")
            P.wait("vector", "v", vt)
            P.op("vector", lambda E: E.tensor_scalar(
                ia_t[:, :], lg[:, :], cols[:, 12:13], None, ALU.is_ge))
            vt = P.op("vector", lambda E: E.tensor_scalar(
                iab_t[:, :], lg[:, :], cols[:, 16:17], None, ALU.is_ge), inc="v")
            P.wait("vector", "v", vt)
            P.wait("vector", "t", tr_tok)
            vt = P.op("vector", lambda E: E.tensor_scalar(
                gtm[:, :], ia_t[:, :], cols[:, 24:25], None, ALU.mult), inc="v")
            P.wait("vector", "v", vt)
            vt = P.op("vector", lambda E: E.scalar_tensor_tensor(
                gtm[:, :], iab_t[:, :], cols[:, 20:21], gtm[:, :],
                ALU.mult, ALU.add), inc="v")
            P.wait("tensor", "v", vt)
            P.wait("tensor", "v", gfm_done)
            tt = P.op("tensor", lambda E, tm=tm: E.transpose(
                ps_tr[:, :], gtm[:, :], identf[:, :]), inc="t")
            tr_tok = tt
            P.wait("vector", "t", tt)
            vt = P.op("vector", lambda E, tm=tm: E.tensor_copy(
                gfm[:, tm * 128:(tm + 1) * 128], ps_tr[:, :]), inc="v")
            gfm_done = vt
        P.wait("tensor", "v", P.cnt["v"])
        bc_done = 0
        for e in range(NE):
            P.wait("tensor", "v", bc_done)
            for sl in range(0, TOK, 512):
                P.op("tensor", lambda E, e=e, sl=sl: E.matmul(
                    ps_bc2[:, sl:sl + 512], esel[:, e * 128:(e + 1) * 128],
                    gfm[:, sl:sl + 512], start=True, stop=True))
            tt = P.inc_last("tensor", "t")
            P.wait("vector", "t", tt)
            vt = P.op("vector", lambda E, e=e: E.tensor_copy(
                gbc[:, e * TOK:(e + 1) * TOK], ps_bc2[:, :]), inc="v")
            bc_done = vt
        GATES_V = P.cnt["v"]
        fence_wait(mark_fence())
        es_r.close()
        if upto == "router":
            st = 0
            for e in range(NE):
                st = P.dma("gpsimd", out_d.ap()[e],
                           gbc[:, e * TOK:(e + 1) * TOK], "st")
            P.wait("gpsimd", "st", st)
            es.close()
            return P.cnt["v"]

        es_e = contextlib.ExitStack()
        NBUF = 2
        wt = [es_e.enter_context(
            nc.sbuf_tensor(name + "_wt%d" % i, [128, 16 * 128], BF16))
            for i in range(NBUF)]
        hsb = [es_e.enter_context(
            nc.sbuf_tensor(name + "_h%d" % i, [128, TOK], BF16)) for i in range(2)]
        tsb = [es_e.enter_context(
            nc.sbuf_tensor(name + "_ts%d" % i, [128, TOK], BF16)) for i in range(2)]
        cm = [es_e.enter_context(
            nc.sbuf_tensor(name + "_cm%d" % i, [128, TOK], F32)) for i in range(2)]
        pse = [es_e.enter_context(
            nc.psum_tensor(name + "_pse%d" % i, [128, TOK], F32)) for i in range(3)]
        P.wait("vector", "ldc", CONST_G)

        wt_tok = [0] * NBUF
        wt_free = [0] * NBUF
        comb_tok = [0] * 16
        mul_t = 0
        add_t = 0
        evac_a = [0] * 3
        evac_v = {}
        piece = 0
        for m in range(16):
            for e in range(NE):
                bi = piece % NBUF
                si = piece % 3
                P.wait("gpsimd", "t", wt_free[bi])
                wt_tok[bi] = P.dma("gpsimd", wt[bi][:, :],
                                   D["eblk"].ap()[e, :, m, :, :], "wt%d" % bi)
                P.wait("tensor", "wt%d" % bi, wt_tok[bi])
                P.wait("tensor", "a", evac_a[si])
                for k in range(16):
                    for sl in (0, 512):
                        P.op("tensor", lambda E, bi=bi, si=si, k=k, sl=sl:
                             E.matmul(pse[si][:, sl:sl + 512],
                                      wt[bi][:, k * 128:(k + 1) * 128],
                                      lnx[:, k * TOK + sl:k * TOK + sl + 512],
                                      start=(k == 0), stop=(k == 15)))
                tt = P.inc_last("tensor", "t")
                wt_free[bi] = tt
                hb = hsb[piece % 2]
                P.wait("scalar", "t", tt)
                if piece >= 2:
                    P.wait("scalar", "v", evac_v.get(piece - 2, 0))
                at = P.op("scalar", lambda E, si=si, hb=hb, e=e, m=m:
                          E.activation(hb[:, :], pse[si][:, :],
                                       AF.Gelu_apprx_tanh,
                                       bias=ebias[:, e * 16 + m:e * 16 + m + 1]),
                          inc="a")
                evac_a[si] = at
                P.wait("vector", "a", at)
                if e == 0:
                    P.wait("vector", "v", GATES_V)
                    if m >= 2:
                        P.wait("vector", "v", comb_tok[m - 2])
                    vt = P.op("vector", lambda E, hb=hb, e=e, m=m: E.tensor_mul(
                        cm[m % 2][:, :], hb[:, :],
                        gbc[:, e * TOK:(e + 1) * TOK]), inc="v")
                    mul_t = vt
                    add_t = vt
                else:
                    vt = P.op("vector", lambda E, hb=hb, e=e: E.tensor_mul(
                        tsb[e % 2][:, :], hb[:, :],
                        gbc[:, e * TOK:(e + 1) * TOK]), inc="v")
                    prev_mul = mul_t
                    mul_t = vt
                    if e >= 2:
                        P.wait("vector", "v", prev_mul)
                        P.wait("vector", "v", add_t)
                        vt = P.op("vector", lambda E, m=m, e=e: E.tensor_add(
                            cm[m % 2][:, :], cm[m % 2][:, :],
                            tsb[(e - 1) % 2][:, :]), inc="v")
                        add_t = vt
                evac_v[piece] = vt
                piece += 1
            P.wait("vector", "v", mul_t)
            P.wait("vector", "v", add_t)
            vt = P.op("vector", lambda E, m=m: E.tensor_add(
                cm[m % 2][:, :], cm[m % 2][:, :], tsb[1][:, :]), inc="v")
            P.wait("vector", "v", vt)
            vt = P.op("vector", lambda E, m=m: E.tensor_copy(
                comb[:, m * TOK:(m + 1) * TOK], cm[m % 2][:, :]), inc="v")
            comb_tok[m] = vt
            evac_v[piece - 1] = vt
        COMB_V = P.cnt["v"]
        fence_wait(mark_fence())
        es_e.close()
        es_gb.close()
        if upto == "experts":
            es.close()
            return P.cnt["v"]

        es_o = contextlib.ExitStack()
        owt = [es_o.enter_context(
            nc.sbuf_tensor(name + "_ow%d" % i, [128, 2 * 16 * 128], BF16))
            for i in range(2)]
        pso = [es_o.enter_context(
            nc.psum_tensor(name + "_pso%d" % i, [128, TOK], F32))
            for i in range(4)]
        tev = [es_o.enter_context(
            nc.sbuf_tensor(name + "_te%d" % i, [128, TOK], F32)) for i in range(4)]
        if target == "part":
            xps = [es_o.enter_context(
                nc.sbuf_tensor(name + "_xp%d" % i, [128, TOK], F32))
                for i in range(2)]
            x1st = [es_o.enter_context(
                nc.sbuf_tensor(name + "_x1s%d" % i, [128, TOK], BF16))
                for i in range(2)]
        P.wait("vector", "v", COMB_V)
        ow_tok = [0, 0]
        ow_free = [0, 0]
        ev_done = [0, 0, 0, 0]
        xp_tok = [0, 0]
        xp_used = [0, 0]
        x1_st = [0, 0]
        pend = []

        def _emit_xadd(j_mp):
            j, pmp = j_mp
            P.wait("vector", "v", ev_done[j])
            if target == "own":
                P.op("vector", lambda E, j=j, mp=pmp: E.tensor_add(
                    xtile[:, mp * TOK:(mp + 1) * TOK],
                    xtile[:, mp * TOK:(mp + 1) * TOK], tev[j][:, :]), inc="v")
            else:
                bj = j % 2
                P.wait("vector", "xp%d" % bj, xp_tok[bj])
                P.wait("vector", "sx%d" % bj, x1_st[bj])
                vt = P.op(
                    "vector", lambda E, j=j, bj=bj: E.tensor_add(
                        x1st[bj][:, :], xps[bj][:, :], tev[j][:, :]), inc="v")
                xp_used[bj] = vt
                P.wait("sync", "v", vt)
                x1_st[bj] = P.dma("sync", x1p_dram.ap()[pmp],
                                  x1st[bj][:, :], "sx%d" % bj)

        for mg in range(8):
            bi = mg % 2
            P.wait("gpsimd", "t", ow_free[bi])
            ow_tok[bi] = P.dma("gpsimd", owt[bi][:, :],
                               D["owblk"].ap()[:, 2 * mg:2 * mg + 2, :, :],
                               "ow%d" % bi)
            P.wait("tensor", "ow%d" % bi, ow_tok[bi])
            P.wait("tensor", "v", COMB_V)
            for i in range(2):
                if len(pend) >= 2:
                    _emit_xadd(pend.pop(0))
                if target == "part":
                    P.wait("sync", "v", xp_used[i])
                    xp_tok[i] = P.dma("sync", xps[i][:, :],
                                      D["xT_part"].ap()[2 * mg + i], "xp%d" % i)
                ps = pso[2 * bi + i]
                P.wait("tensor", "v", ev_done[2 * bi + i])
                for k in range(16):
                    for sl in (0, 512):
                        P.op("tensor", lambda E, bi=bi, i=i, k=k, sl=sl, ps=ps:
                             E.matmul(ps[:, sl:sl + 512],
                                      owt[bi][:, (i * 16 + k) * 128:
                                              (i * 16 + k + 1) * 128],
                                      comb[:, k * TOK + sl:k * TOK + sl + 512],
                                      start=(k == 0), stop=(k == 15)))
                tt = P.inc_last("tensor", "t")
                ow_free[bi] = tt
                mp = 2 * mg + i
                j = 2 * bi + i
                P.wait("vector", "t", tt)
                at = P.op("vector", lambda E, ps=ps, j=j, mp=mp:
                          E.scalar_tensor_tensor(
                              tev[j][:, :], ps[:, :], 0.5,
                              obias2[:, mp:mp + 1].to_broadcast([128, TOK]),
                              ALU.mult, ALU.add), inc="v")
                ev_done[j] = at
                pend.append((j, mp))
        while pend:
            _emit_xadd(pend.pop(0))
        FFN_V = P.cnt["v"]
        fence_wait(mark_fence())
        es_o.close()
        es.close()
        return FFN_V

    if dev_stop in ("ln", "router", "experts"):
        v = emit_ffn(0, x_own, "own", upto=dev_stop)
        if dev_stop not in ("ln", "router"):
            _emit_store(P, nc, x_own, out_d, v)
        return _run(nc, P, top, in_maps)
    ffn1_v = emit_ffn(0, x_own, "own")

    if dev_stop == "ffn1own":
        _emit_store(P, nc, x_own, out_d, ffn1_v)
        return _run(nc, P, top, in_maps)

    ffn1p_v = emit_ffn(1, None, "part")
    X1P_ST = (P.cnt.get("sx0", 0), P.cnt.get("sx1", 0))

    if dev_stop == "ffn1":
        _emit_store(P, nc, x_own, out_d, ffn1_v)
        return _run(nc, P, top, in_maps)


    arena = x_own.ap().bitcast(BF16)

    es2 = contextlib.ExitStack()
    ln2x = es2.enter_context(nc.sbuf_tensor("ln2x", [128, 16 * S], BF16))
    x1p = es2.enter_context(nc.sbuf_tensor("x1p_sb", [128, 16 * TOK], BF16))
    P.wait("gpsimd", "sx0", X1P_ST[0])
    P.wait("gpsimd", "sx1", X1P_ST[1])
    lp2 = 0
    for c in range(16):
        lp2 = P.dma("gpsimd", x1p[:, c * TOK:(c + 1) * TOK],
                    x1p_dram.ap()[c], "ldp2")
    for eng in ("tensor", "vector", "scalar"):
        P.wait(eng, "ldp2", lp2)

    def ln2_srcs_own(c):
        return [(x_own[:, c * TOK:(c + 1) * TOK], 0, TOK, True)]

    def ln2_srcs_par(c):
        return [(x1p[:, c * TOK:(c + 1) * TOK], 0, TOK, False)]

    P.wait("vector", "v", ffn1p_v)
    emit_ln(None, ln2_srcs_own, TOK, None, name="ln2a",
            dst_fn=lambda c: ln2x[:, c * S:c * S + TOK])
    ln2tok, ln2sem = emit_ln(None, ln2_srcs_par, TOK, None, name="ln2b",
                             dst_fn=lambda c: ln2x[:, c * S + TOK:
                                                   (c + 1) * S])
    P.wait("tensor", ln2sem, ln2tok)
    P.wait("gpsimd", ln2sem, ln2tok)
    P.wait("sync", ln2sem, ln2tok)
    fence_wait(mark_fence())

    sp_tok = 0
    for c in range(16):
        sp_tok = P.dma("sync", xres_dram.ap()[c],
                       x_own[:, c * TOK:(c + 1) * TOK], "spx")
    qc_raw = arena[:, 0:12 * TOK]
    attn_all = arena[:, 12 * TOK:12 * TOK + 16 * TOK]

    es3 = contextlib.ExitStack()
    kvt = [es3.enter_context(nc.sbuf_tensor("kv_wt%d" % i, [128, 16 * 128], BF16))
           for i in range(2)]
    kevac = [es3.enter_context(nc.sbuf_tensor("kv_ev%d" % i, [128, S], BF16))
             for i in range(2)]
    ps_kv = [es3.enter_context(nc.psum_tensor("ps_kv%d" % i, [128, S], F32))
             for i in range(2)]
    kv_dma = [0, 0]
    kv_mm = [0, 0]
    kv_ev_a = [0, 0]
    kv_st = [0, 0]
    for m in range(24):
        bi = m % 2
        P.wait("gpsimd", "t", kv_mm[bi])
        kv_dma[bi] = P.dma("gpsimd", kvt[bi][:, :],
                           D["ckvblk"].ap()[:, m, :, :], "kvw%d" % bi)
        P.wait("tensor", "kvw%d" % bi, kv_dma[bi])
        P.wait("tensor", "a", kv_ev_a[bi])
        for k in range(16):
            for sl in range(0, S, 512):
                P.op("tensor", lambda E, bi=bi, k=k, sl=sl: E.matmul(
                    ps_kv[bi][:, sl:sl + 512], kvt[bi][:, k * 128:(k + 1) * 128],
                    ln2x[:, k * S + sl:k * S + sl + 512],
                    start=(k == 0), stop=(k == 15)))
        tt = P.inc_last("tensor", "t")
        kv_mm[bi] = tt
        P.wait("scalar", "t", tt)
        P.wait("scalar", "s%d" % bi, kv_st[bi])
        at = P.op("scalar", lambda E, bi=bi: E.activation(
            kevac[bi][:, :], ps_kv[bi][:, :], AF.Copy), inc="a")
        kv_ev_a[bi] = at
        P.wait("sync", "a", at)
        kv_st[bi] = P.dma("sync", kvc_dram.ap()[m], kevac[bi][:, :], "s%d" % bi)
    fence_wait(mark_fence())
    es3.close()

    es4 = contextlib.ExitStack()
    wip = es4.enter_context(nc.sbuf_tensor("wip", [128, 16 * 512], BF16))
    ievac = [es4.enter_context(nc.sbuf_tensor("ist_ev%d" % i, [128, 512], BF16))
             for i in range(2)]
    ps_ist = [es4.enter_context(nc.psum_tensor("ps_ist%d" % i, [128, 512], F32))
              for i in range(2)]
    half_mm = 0
    for half in range(2):
        P.wait("gpsimd", "t", half_mm)
        htok = 0
        for k in range(16):
            htok = P.dma("gpsimd", wip[:, k * 512:(k + 1) * 512],
                         D["ipblk"].ap()[k, :, half * 512:(half + 1) * 512],
                         "wiph")
        P.wait("tensor", "wiph", htok)
        i_mm = [0, 0]
        i_ev = [0, 0]
        i_st = [0, 0]
        for tokm in range(16):
            bi = tokm % 2
            P.wait("tensor", "v", i_ev[bi])
            for k in range(16):
                P.op("tensor", lambda E, bi=bi, k=k, tokm=tokm: E.matmul(
                    ps_ist[bi][:, :],
                    ln2x[:, k * S + tokm * 128:k * S + (tokm + 1) * 128],
                    wip[:, k * 512:(k + 1) * 512],
                    start=(k == 0), stop=(k == 15)))
            tt = P.inc_last("tensor", "t")
            i_mm[bi] = tt
            half_mm = tt
            P.wait("vector", "t", tt)
            P.wait("vector", "si%d" % bi, i_st[bi])
            vt = P.op("vector", lambda E, bi=bi: E.tensor_copy(
                ievac[bi][:, :], ps_ist[bi][:, :]), inc="v")
            i_ev[bi] = vt
            P.wait("sync", "v", vt)
            i_st[bi] = P.dma(
                "sync",
                ist_dram.ap()[tokm * 128:(tokm + 1) * 128,
                              half * 512:(half + 1) * 512],
                ievac[bi][:, :], "si%d" % bi)
    IST_ST = (P.cnt["si0"], P.cnt["si1"])
    fence_wait(mark_fence())
    es4.close()

    es5 = contextlib.ExitStack()
    ps_g = es5.enter_context(nc.psum_tensor("ps_g", [8, S], F32))
    gsp = es5.enter_context(nc.sbuf_tensor("gsp", [8, S], F32))
    svals = es5.enter_context(nc.sbuf_tensor("svals", [8, 8], F32))
    sidx = es5.enter_context(nc.sbuf_tensor("sidx", [32, 32], U32))
    sidxT = es5.enter_context(nc.sbuf_tensor("sidxT", [128, 32], U32))
    selraw = es5.enter_context(nc.sbuf_tensor("selraw", [8, 2 * 1024], BF16))
    selT = es5.enter_context(nc.sbuf_tensor("selT", [128, 64], BF16))
    sqT = es5.enter_context(nc.sbuf_tensor("sqT", [128, 64], BF16))
    skT = es5.enter_context(nc.sbuf_tensor("skT", [128, 64], BF16))
    sv2 = es5.enter_context(nc.sbuf_tensor("sv2", [8, 8 * 128], BF16))
    so2 = es5.enter_context(nc.sbuf_tensor("so2", [8, 8 * 128], BF16))
    soT = es5.enter_context(nc.sbuf_tensor("soT", [128, 64], BF16))
    srows = es5.enter_context(nc.sbuf_tensor("srows", [8, 24], F32))
    sarow = es5.enter_context(nc.sbuf_tensor("sarow", [8, 8], F32))
    spt = es5.enter_context(nc.sbuf_tensor("spt", [8, 2 * 8 * 8], BF16))
    wsq = es5.enter_context(nc.sbuf_tensor("sb_wsq", [128, 128], BF16))
    wsk = es5.enter_context(nc.sbuf_tensor("sb_wsk", [128, 128], BF16))
    wsv = es5.enter_context(nc.sbuf_tensor("sb_wsv", [128, 128], BF16))
    wio = es5.enter_context(nc.sbuf_tensor("sb_wio", [128, VD], BF16))
    ps_s1 = es5.enter_context(nc.psum_tensor("ps_s1", [128, 512], F32))
    ps_s2 = es5.enter_context(nc.psum_tensor("ps_s2", [8, 128], F32))
    ps_s2b = es5.enter_context(nc.psum_tensor("ps_s2b", [8, 128], BF16))
    ps_s3 = es5.enter_context(nc.psum_tensor("ps_s3", [128, 64], BF16))

    swt = 0
    for dst, src in ((wsq, "wsq"), (wsk, "wsk"), (wsv, "wsv"), (wio, "wio")):
        swt = P.dma("gpsimd", dst[:, :], D[src].ap(), "ldsw")
    P.wait("tensor", "ldsw", swt)
    P.op("vector", lambda E: E.memset(sidx[:, :], 0))
    vt = P.op("vector", lambda E: E.memset(sidxT[:, :], 0), inc="v")
    P.wait("vector", "v", vt)
    for k in range(16):
        for sl in range(0, S, 512):
            P.op("tensor", lambda E, k=k, sl=sl: E.matmul(
                ps_g[:, sl:sl + 512], wig16[:, k * IH:(k + 1) * IH],
                ln2x[:, k * S + sl:k * S + sl + 512],
                start=(k == 0), stop=(k == 15)))
    tt = P.inc_last("tensor", "t")
    P.wait("vector", "t", tt)
    vt = P.op("vector", lambda E: E.tensor_copy(gsp[:, :], ps_g[:, :]), inc="v")
    P.wait("vector", "v", vt)
    vt = P.op("vector", lambda E: E.max(out=svals[:, :], in_=gsp[:, :]), inc="v")
    P.wait("vector", "v", vt)
    vt = P.op("vector", lambda E: E.max_index(
        sidx[0:8, 0:8], svals[:, :], gsp[:, :]), inc="v")
    P.wait("vector", "v", vt)
    vt = P.op("vector", lambda E: E.transpose(sidxT[0:32, 0:32], sidx[:, :]),
              inc="v")
    P.wait("gpsimd", "v", vt)
    P.wait("gpsimd", "si0", IST_ST[0])
    P.wait("gpsimd", "si1", IST_ST[1])
    g_tok = [0, 0]
    tr_t = [0, 0]
    trt = 0
    for ih in range(IH):
        bi = ih % 2
        P.wait("gpsimd", "t", tr_t[bi])
        g_tok[bi] = P.idma(selraw[:, bi * 1024:(bi + 1) * 1024],
                           ist_dram.ap(),
                           bass.IndirectOffsetOnAxis(sidxT[0:8, ih:ih + 1], 0),
                           sem="sg%d" % bi)
        P.wait("tensor", "sg%d" % bi, g_tok[bi])
        P.wait("tensor", "v", trt)
        tt = P.op("tensor", lambda E, ih=ih, bi=bi: E.transpose(
            ps_s3[:, 0:8],
            selraw[0:8, bi * 1024 + ih * 128:bi * 1024 + (ih + 1) * 128],
            ident[0:8, 0:8]), inc="t")
        tr_t[bi] = tt
        P.wait("vector", "t", tt)
        trt = P.op("vector", lambda E, ih=ih: E.tensor_copy(
            selT[:, ih * 8:(ih + 1) * 8], ps_s3[:, 0:8]), inc="v")
    P.wait("tensor", "v", trt)
    tt = P.op("tensor", lambda E: E.matmul(
        ps_s1[:, 0:64], wsq[:, :], selT[:, :], start=True, stop=True), inc="t")
    P.wait("vector", "t", tt)
    vt = P.op("vector", lambda E: E.tensor_copy(sqT[:, :], ps_s1[:, 0:64]),
              inc="v")
    P.wait("tensor", "v", vt)
    tt = P.op("tensor", lambda E: E.matmul(
        ps_s1[:, 0:64], wsk[:, :], selT[:, :], start=True, stop=True), inc="t")
    P.wait("vector", "t", tt)
    vt = P.op("vector", lambda E: E.tensor_copy(skT[:, :], ps_s1[:, 0:64]),
              inc="v")
    P.wait("tensor", "v", vt)
    svt = 0
    for ih in range(IH):
        P.wait("tensor", "v", svt)
        tt = P.op("tensor", lambda E, ih=ih: E.matmul(
            ps_s2[:, :], selT[:, ih * 8:(ih + 1) * 8], wsv[:, :],
            start=True, stop=True), inc="t")
        P.wait("vector", "t", tt)
        svt = P.op("vector", lambda E, ih=ih: E.tensor_copy(
            sv2[:, ih * 128:(ih + 1) * 128], ps_s2[:, :]), inc="v")
    P.wait("tensor", "v", svt)
    sot = 0
    for ih in range(IH):
        P.wait("tensor", "v", sot)
        tt = P.op("tensor", lambda E, ih=ih: E.matmul(
            ps_s2[:, 0:8], sqT[:, ih * 8:(ih + 1) * 8],
            skT[:, ih * 8:(ih + 1) * 8], start=True, stop=True), inc="t")
        P.wait("vector", "t", tt)
        vt = P.op("vector", lambda E, ih=ih: E.reduce_max(
            srows[:, 0:1], ps_s2[:, 0:8], axis=AX.X), inc="v")
        P.wait("vector", "v", vt)
        vt = P.op("vector", lambda E: E.tensor_scalar(
            srows[:, 4:5], srows[:, 0:1], -SC_SP, None, ALU.mult), inc="v")
        P.wait("scalar", "v", vt)
        at = P.op("scalar", lambda E, ih=ih: E.activation(
            spt[:, ih * 8:(ih + 1) * 8], ps_s2[:, 0:8], AF.Exp,
            bias=srows[:, 4:5], scale=fc[0:8, 3:4],
            accum_out=sarow[:, 0:1]), inc="a")
        P.wait("vector", "a", at)
        vt = P.op("vector", lambda E: E.reciprocal(
            srows[:, 8:9], sarow[:, 0:1]), inc="v")
        P.wait("vector", "v", vt)
        vt = P.op("vector", lambda E, ih=ih: E.tensor_scalar(
            spt[:, ih * 8:(ih + 1) * 8], spt[:, ih * 8:(ih + 1) * 8],
            srows[:, 8:9], None, ALU.mult), inc="v")
        P.wait("tensor", "v", vt)
        tt = P.op("tensor", lambda E, ih=ih: E.transpose(
            ps_s2b[:, 0:8], spt[:, ih * 8:(ih + 1) * 8], ident[0:8, 0:8]),
            inc="t")
        P.wait("vector", "t", tt)
        vt = P.op("vector", lambda E, ih=ih: E.tensor_copy(
            spt[:, 64 + ih * 8:64 + (ih + 1) * 8], ps_s2b[:, 0:8]), inc="v")
        P.wait("tensor", "v", vt)
        tt = P.op("tensor", lambda E, ih=ih: E.matmul(
            ps_s2[:, :], spt[:, 64 + ih * 8:64 + (ih + 1) * 8],
            sv2[:, ih * 128:(ih + 1) * 128], start=True, stop=True), inc="t")
        P.wait("vector", "t", tt)
        sot = P.op("vector", lambda E, ih=ih: E.tensor_copy(
            so2[:, ih * 128:(ih + 1) * 128], ps_s2[:, :]), inc="v")
        P.wait("tensor", "v", sot)
        tt = P.op("tensor", lambda E, ih=ih: E.transpose(
            ps_s3[:, 0:8], so2[0:8, ih * 128:(ih + 1) * 128],
            ident[0:8, 0:8]), inc="t")
        P.wait("vector", "t", tt)
        vt = P.op("vector", lambda E, ih=ih: E.tensor_copy(
            soT[:, ih * 8:(ih + 1) * 8], ps_s3[:, 0:8]), inc="v")
        P.wait("tensor", "v", vt)
        tt = P.op("tensor", lambda E, ih=ih: E.matmul(
            ps_s1[:, 0:8], wio[:, :], soT[:, ih * 8:(ih + 1) * 8],
            start=True, stop=True), inc="t")
        P.wait("vector", "t", tt)
        vt = P.op("vector", lambda E, ih=ih: E.reduce_sum(
            iot[:, ih * 4 + 1:ih * 4 + 2], ps_s1[:, 0:8], axis=AX.X), inc="v")
        P.wait("vector", "v", vt)
        vt = P.op("vector", lambda E, ih=ih: E.tensor_scalar(
            iot[:, ih * 4:ih * 4 + 1], iot[:, ih * 4 + 1:ih * 4 + 2],
            0.125, None, ALU.mult), inc="v")
    IOT_V = P.cnt["v"]
    fence_wait(mark_fence())
    es5.close()


    es6 = contextlib.ExitStack()
    cqt = [es6.enter_context(nc.sbuf_tensor("cq_wt%d" % i, [128, 16 * 128], BF16))
           for i in range(2)]
    ps_qc = [es6.enter_context(nc.psum_tensor("ps_qc%d" % i, [128, TOK], F32))
             for i in range(2)]
    qc_dma = [0, 0]
    qc_mm = [0, 0]
    qc_ev = [0, 0]
    P.wait("vector", "spx", sp_tok)
    for m in range(12):
        bi = m % 2
        P.wait("gpsimd", "t", qc_mm[bi])
        qc_dma[bi] = P.dma("gpsimd", cqt[bi][:, :],
                           D["cqblk"].ap()[:, m, :, :], "cqw%d" % bi)
        P.wait("tensor", "cqw%d" % bi, qc_dma[bi])
        P.wait("tensor", "v", qc_ev[bi])
        for k in range(16):
            for sl in (0, 512):
                P.op("tensor", lambda E, bi=bi, k=k, sl=sl: E.matmul(
                    ps_qc[bi][:, sl:sl + 512], cqt[bi][:, k * 128:(k + 1) * 128],
                    ln2x[:, k * S + sl:k * S + sl + 512],
                    start=(k == 0), stop=(k == 15)))
        tt = P.inc_last("tensor", "t")
        qc_mm[bi] = tt
        P.wait("vector", "t", tt)
        vt = P.op("vector", lambda E, bi=bi, m=m: E.tensor_copy(
            qc_raw[:, m * TOK:(m + 1) * TOK], ps_qc[bi][:, :]), inc="v")
        qc_ev[bi] = vt
    fence_wait(mark_fence())
    es6.close()
    es2.close()

    def qc_srcs(c):
        return [(qc_raw[:, c * TOK:(c + 1) * TOK], 0, TOK, False)]

    qlntok, qlnsem = emit_ln(None, qc_srcs, TOK, qc_raw, affine=(qnw, qnb),
                             name="qcln", nchunks=12, nfeat=QLR)
    qcn = qc_raw
    fence_wait(mark_fence())

    if dev_stop == "sweeps":
        fence_wait(mark_fence())
        st = 0
        for c in range(16):
            st = P.dma("sync", out_d.ap()[c], xres_dram.ap()[c], "st")
        P.wait("sync", "st", st)
        return _run(nc, P, top, in_maps)

    es8 = contextlib.ExitStack()

    def sbt(nm, sh, dt):
        return es8.enter_context(nc.sbuf_tensor(nm, sh, dt))

    maskrow_t = sbt("maskrow_sb", [1, 36 * 128], BF16)
    maskrow = maskrow_t[0:1, :]

    tabk1 = sbt("sb_tabk1", [128, S], BF16)
    tabk2 = sbt("sb_tabk2", [128, S], BF16)
    tabq1 = tabk1[:, 0:TOK]
    tabq2 = tabk2[:, 0:TOK]
    tri = sbt("sb_tri", [128, 128], F32)
    sel16c = sbt("sel16c_sb", [128, 16 * 16], BF16)
    sel16w = sbt("sel16w_sb", [16, 16 * 128], BF16)
    Qb = sbt("Qb", [16, S], BF16)
    Sb = sbt("Sb", [16, S], BF16)
    kraw = [sbt("kraw%d" % i, [128, S], BF16) for i in range(3)]
    kpe_rope = sbt("kpe_rope", [128, S], BF16)
    kvn = sbt("kvn", [128, S], BF16)
    Smat = sbt("Smat", [16, S], F32)
    Qmat = sbt("Qmat", [16, S], F32)
    knope = sbt("knope", [128, S], BF16)
    vtm = sbt("vtm", [128, 16 * 128], BF16)
    wq_sb = sbt("wq_sb", [128, 36 * 128], BF16)
    qp_sb = sbt("qp_sb", [128, 3 * TOK], BF16)
    qrope = sbt("qrope", [128, TOK], BF16)
    e_sb = sbt("e_sb", [128, 2048], BF16)
    diag = sbt("diag", [128, 128], F32)
    pT = [sbt("pT%d" % i, [128, 512], BF16) for i in range(16)]
    rsA = sbt("rsA", [128, S], BF16)
    rsB = sbt("rsB", [128, S], BF16)
    kvt1 = sbt("kvt1", [128, 512], BF16)
    recrow = sbt("recrow", [1, 512], F32)
    rb_sb = sbt("rb_sb", [128, 512], F32)
    avt = sbt("avt", [128, 512], F32)
    mxcol = sbt("mxcol", [128, 8], F32)
    acc_col = sbt("acc_col", [128, 8], F32)
    rec_col = sbt("rec_col", [128, 4], F32)

    P.wait("gpsimd", "spx", sp_tok)
    for dst, src in ((tabk1, "tabk1"), (tabk2, "tabk2"), (tri, "tri"),
                     (sel16c, "sel16c"), (sel16w, "sel16w")):
        ctok = P.dma("gpsimd", dst[:, :], D[src].ap(), "ldc2")
    ctok = P.dma("gpsimd", maskrow_t[:, :], D["maskrow"].ap(), "ldc2")
    LDC2 = ctok
    for eng in ("vector", "scalar", "tensor", "sync"):
        P.wait(eng, "ldc2", LDC2)
        P.wait(eng, "spx", sp_tok)
    P.wait("vector", "v", IOT_V)

    es8a = contextlib.ExitStack()
    ps_m = es8a.enter_context(nc.psum_tensor("ps_m", [16, S], F32))
    ps_q = es8a.enter_context(nc.psum_tensor("ps_q", [16, S], F32))
    sqb = [kvn, e_sb]
    ppd = [0, 0]
    pp_mm = [0, 0]
    sq_mm = [0, 0]
    for h2 in range(16):
        bi = h2 % 2
        buf = kraw[bi]
        P.wait("sync", "t", pp_mm[bi])
        ppd[bi] = P.dma("sync", buf[:, :], kvc_dram.ap()[8 + h2], "pp%d" % bi)
        P.wait("scalar", "pp%d" % bi, ppd[bi])
        P.wait("scalar", "t", sq_mm[bi])
        at = P.op("scalar", lambda E, buf=buf, bi=bi: E.activation(
            sqb[bi][:, :], buf[:, :], AF.Square), inc="a")
        P.wait("tensor", "pp%d" % bi, ppd[bi])
        for sl in range(0, S, 512):
            P.op("tensor", lambda E, h2=h2, sl=sl, buf=buf: E.matmul(
                ps_m[0:16, sl:sl + 512], sel16c[:, h2 * 16:(h2 + 1) * 16],
                buf[:, sl:sl + 512], start=(h2 == 0), stop=(h2 == 15)))
        pp_mm[bi] = P.inc_last("tensor", "t")
        P.wait("tensor", "a", at)
        for sl in range(0, S, 512):
            P.op("tensor", lambda E, h2=h2, sl=sl, bi=bi: E.matmul(
                ps_q[0:16, sl:sl + 512], sel16c[:, h2 * 16:(h2 + 1) * 16],
                sqb[bi][:, sl:sl + 512], start=(h2 == 0), stop=(h2 == 15)))
        sq_mm[bi] = P.inc_last("tensor", "t")
    PP_MM = list(pp_mm)
    SQ_MM = list(sq_mm)
    P.wait("vector", "t", sq_mm[1])
    P.wait("vector", "t", pp_mm[1])
    vt = P.op("vector", lambda E: E.tensor_scalar(
        Smat[0:16, :], ps_m[0:16, :], fc[0:16, 6:7], None, ALU.mult), inc="v")
    vt = P.op("vector", lambda E: E.tensor_scalar(
        Qmat[0:16, :], ps_q[0:16, :], fc[0:16, 6:7], None, ALU.mult),
        inc="v")
    PRE_PS_V = vt
    P.wait("vector", "v", vt)
    vt = P.op("vector", lambda E: E.tensor_mul(
        ps_m[0:16, :], Smat[0:16, :], Smat[0:16, :]), inc="v")
    P.wait("vector", "v", vt)
    vt = P.op("vector", lambda E: E.tensor_sub(
        Qmat[0:16, :], Qmat[0:16, :], ps_m[0:16, :]), inc="v")
    P.wait("scalar", "v", vt)
    at = P.op("scalar", lambda E: E.activation(
        ps_m[0:16, :], Qmat[0:16, :], AF.Sqrt, bias=fc[0:16, 0:1]),
        inc="a")
    P.wait("vector", "a", at)
    vt = P.op("vector", lambda E: E.reciprocal(
        Qmat[0:16, :], ps_m[0:16, :]), inc="v")
    P.wait("vector", "v", vt)
    vt = P.op("vector", lambda E: E.scalar_tensor_tensor(
        Smat[0:16, :], Smat[0:16, :], -1.0, Qmat[0:16, :],
        ALU.mult, ALU.mult), inc="v")
    P.wait("vector", "v", vt)
    P.op("vector", lambda E: E.tensor_copy(Qb[0:16, :], Qmat[0:16, :]))
    vt = P.op("vector", lambda E: E.tensor_copy(Sb[0:16, :], Smat[0:16, :]),
              inc="v")
    PRE_V = vt
    P.wait("vector", "v", PRE_V)
    P.wait("tensor", "v", PRE_V)
    es8a.close()

    def emit_rope(src_pe, tab1, tab2, dst, Wd):
        mt = 0
        for j in range(4):
            b = 32 * j
            pb = b + 32 if j % 2 == 0 else b - 32
            tab = tab1 if j % 2 == 0 else tab2
            P.op("vector", lambda E, b=b, tab=tab: E.tensor_mul(
                rsA[b:b + 32, 0:Wd], src_pe[b:b + 32, 0:Wd],
                tab[b:b + 32, 0:Wd]))
            mt = P.op("vector", lambda E, b=b, pb=pb, tab=tab: E.tensor_mul(
                rsB[b:b + 32, 0:Wd], src_pe[pb:pb + 32, 0:Wd],
                tab[pb:pb + 32, 0:Wd]), inc="v")
        P.wait("vector", "v", mt)
        ft = 0
        for j in range(4):
            b = 32 * j
            op = ALU.subtract if j % 2 == 0 else ALU.add
            ft = P.op("vector", lambda E, b=b, op=op: E.tensor_tensor(
                dst[b:b + 32, 0:Wd], rsA[b:b + 32, 0:Wd],
                rsB[b:b + 32, 0:Wd], op), inc="v")
        P.wait("vector", "v", ft)
        return mt, ft

    kr_free_v = [0, 0, 0]
    kr_free_t = [PP_MM[0], PP_MM[1], 0]
    kvn_gate_t = SQ_MM[0]
    esb_gate_t = SQ_MM[1]
    wq_free_t = 0
    qp_gate_t = 0
    knope_gate_t = 0
    prev_exp_a = 0
    prev_max_v = 0
    prev_evac_v = PRE_PS_V
    av_mm_t = 0
    rb_mm_t = 0
    rbcp_v = 0
    avt_v = 0
    rr_v = 0
    pT_ev = 0

    for hp in range(8):
        P.wait("sync", "v", kr_free_v[0])
        P.wait("sync", "t", kr_free_t[0])
        k0 = P.dma("sync", kraw[0][:, :], kvc_dram.ap()[hp], "kv0")
        P.wait("sync", "v", kr_free_v[1])
        P.wait("sync", "t", kr_free_t[1])
        k1 = P.dma("sync", kraw[1][:, :], kvc_dram.ap()[8 + 2 * hp], "kv1")
        P.wait("sync", "v", kr_free_v[2])
        P.wait("sync", "t", kr_free_t[2])
        k2 = P.dma("sync", kraw[2][:, :], kvc_dram.ap()[9 + 2 * hp], "kv2")
        P.wait("gpsimd", "t", wq_free_t)
        wqt = P.dma("gpsimd", wq_sb[:, :], D["wqblk"].ap()[hp], "wq")

        es_qp = contextlib.ExitStack()
        ps_qp = [es_qp.enter_context(
            nc.psum_tensor("ps_qp%d_%d" % (i, hp), [128, TOK], F32))
            for i in range(3)]
        P.wait("tensor", "wq", wqt)
        P.wait("tensor", "a", prev_exp_a)
        P.wait("tensor", "v", prev_evac_v)
        for k in range(12):
            for m in range(3):
                for sl in (0, 512):
                    P.op("tensor", lambda E, k=k, m=m, sl=sl: E.matmul(
                        ps_qp[m][:, sl:sl + 512],
                        wq_sb[:, (m * 12 + k) * 128:(m * 12 + k + 1) * 128],
                        qcn[:, k * TOK + sl:k * TOK + sl + 512],
                        start=(k == 0), stop=(k == 11)))
        wq_free_t = P.inc_last("tensor", "t")
        P.wait("vector", "t", wq_free_t)
        P.wait("vector", "t", qp_gate_t)
        qp_ev = 0
        for m in range(3):
            qp_ev = P.op("vector", lambda E, m=m: E.tensor_copy(
                qp_sb[:, m * TOK:(m + 1) * TOK], ps_qp[m][:, :]), inc="v")
        P.wait("vector", "v", qp_ev)
        P.wait("tensor", "v", qp_ev)
        es_qp.close()

        for par in range(2):
            h = 2 * hp + par
            kr = kraw[1 + par]
            ksem, ktok = ("kv1", k1) if par == 0 else ("kv2", k2)

            es_kv = contextlib.ExitStack()
            ps_a = es_kv.enter_context(
                nc.psum_tensor("ps_a%d" % h, [128, 512], F32))
            ps_b = es_kv.enter_context(
                nc.psum_tensor("ps_b%d" % h, [128, 512], F32))
            P.wait("vector", ksem, ktok)
            P.wait("vector", "t", kvn_gate_t)
            P.wait("tensor", "a", prev_exp_a)
            P.wait("tensor", "v", prev_evac_v)
            ab_read = 0
            t1_tok = 0
            for sl in (0, 512, 1024, 1536):
                P.wait("tensor", "v", ab_read)
                P.op("tensor", lambda E, h=h, sl=sl: E.matmul(
                    ps_a[:, :], sel16w[0:16, h * 128:(h + 1) * 128],
                    Qb[0:16, sl:sl + 512], start=True, stop=True))
                mmt = P.op("tensor", lambda E, h=h, sl=sl: E.matmul(
                    ps_b[:, :], sel16w[0:16, h * 128:(h + 1) * 128],
                    Sb[0:16, sl:sl + 512], start=True, stop=True), inc="t")
                P.wait("vector", "t", mmt)
                t1_tok = P.op("vector", lambda E, kr=kr, sl=sl: E.tensor_mul(
                    kvt1[:, :], kr[:, sl:sl + 512], ps_a[:, :]), inc="v")
                P.wait("vector", "v", t1_tok)
                ab_read = P.op("vector", lambda E, sl=sl:
                               E.scalar_tensor_tensor(
                                   kvn[:, sl:sl + 512], ps_b[:, :],
                                   kvnb[:, 0:1], kvt1[:, :],
                                   ALU.add, ALU.add), inc="v")
            kr_free_v[1 + par] = t1_tok
            kr_free_t[1 + par] = 0
            KVN_V = ab_read
            P.wait("tensor", "v", KVN_V)
            es_kv.close()

            es_kp = contextlib.ExitStack()
            ps_k = [es_kp.enter_context(
                nc.psum_tensor("ps_k%d_%d" % (i, h), [128, 512], F32))
                for i in range(2)]
            P.wait("vector", "t", knope_gate_t)
            kp_ev = [0, 0]
            for i4, sl in enumerate((0, 512, 1024, 1536)):
                bi = i4 % 2
                P.wait("tensor", "v", kp_ev[bi])
                mmt = P.op("tensor", lambda E, sl=sl, bi=bi: E.matmul(
                    ps_k[bi][:, :], wkv[:, 0:128], kvn[:, sl:sl + 512],
                    start=True, stop=True), inc="t")
                P.wait("vector", "t", mmt)
                kp_ev[bi] = P.op("vector", lambda E, sl=sl, bi=bi:
                                 E.tensor_copy(knope[:, sl:sl + 512],
                                               ps_k[bi][:, :]), inc="v")
            for grp in range(4):
                bi = grp % 2
                P.wait("tensor", "v", kp_ev[bi])
                for tc4 in range(4):
                    tc = grp * 4 + tc4
                    P.op("tensor", lambda E, tc=tc, tc4=tc4, bi=bi: E.matmul(
                        ps_k[bi][:, tc4 * 128:(tc4 + 1) * 128],
                        kvn[:, tc * 128:(tc + 1) * 128], wkv[:, 128:256],
                        start=True, stop=True))
                mmt = P.inc_last("tensor", "t")
                kvn_gate_t = mmt
                P.wait("vector", "t", mmt)
                kp_ev[bi] = P.op("vector", lambda E, grp=grp, bi=bi:
                                 E.tensor_copy(
                                     vtm[:, grp * 512:(grp + 1) * 512],
                                     ps_k[bi][:, :]), inc="v")
            KVP_V = kp_ev[1]
            P.wait("tensor", "v", kp_ev[0])
            P.wait("tensor", "v", kp_ev[1])
            es_kp.close()

            if par == 0:
                P.wait("vector", "kv0", k0)
                P.wait("vector", "t", qp_gate_t)
                krmul, krv = emit_rope(kraw[0], tabk1, tabk2, kpe_rope, S)
                kr_free_v[0] = krmul
                kr_free_t[0] = 0
                qmul, qrv = emit_rope(qp_sb[:, 2 * TOK:3 * TOK], tabq1,
                                      tabq2, qrope, TOK)
                ROPE_V = qrv
                P.wait("tensor", "v", ROPE_V)

            es_sc = contextlib.ExitStack()
            ps_sc = es_sc.enter_context(
                nc.psum_tensor("ps_sc%d" % h, [128, 2048], F32))
            ps_tr = es_sc.enter_context(
                nc.psum_tensor("ps_tr%d" % h, [128, 512], BF16))
            av_ps = es_sc.enter_context(
                nc.psum_tensor("av_ps%d" % h, [128, 512], F32))
            ps_rb = es_sc.enter_context(
                nc.psum_tensor("ps_rb%d" % h, [128, 512], F32))
            ps_rr = es_sc.enter_context(
                nc.psum_tensor("ps_rr%d" % h, [1, 128], F32))
            exp_a = prev_exp_a
            max_v = prev_max_v
            sc_tt = 0
            tr_last_t = esb_gate_t
            for g in range(2):
                for p4 in range(4):
                    p = 4 * g + p4
                    Wp = (p + 1) * 128
                    moff = (p * (p + 1) // 2) * 128
                    qoff = par * TOK + p * 128
                    P.wait("tensor", "a", exp_a)
                    P.wait("tensor", "v", max_v)
                    for sl in range(0, Wp, 512):
                        ww = min(512, Wp - sl)
                        P.op("tensor", lambda E, qoff=qoff, sl=sl, ww=ww:
                             E.matmul(ps_sc[:, sl:sl + ww],
                                      qp_sb[:, qoff:qoff + 128],
                                      knope[:, sl:sl + ww],
                                      start=True, stop=False))
                        P.op("tensor", lambda E, par=par, p=p, sl=sl, ww=ww:
                             E.matmul(ps_sc[:, sl:sl + ww],
                                      qrope[par * 64:(par + 1) * 64,
                                            p * 128:(p + 1) * 128],
                                      kpe_rope[par * 64:(par + 1) * 64,
                                               sl:sl + ww],
                                      start=False, stop=True))
                    for sl in range(0, Wp, 512):
                        ww = min(512, Wp - sl)
                        P.op("tensor", lambda E, qoff=qoff, sl=sl, ww=ww:
                             E.matmul(ps_sc[:, 1024 + sl:1024 + sl + ww],
                                      qp_sb[:, qoff:qoff + 128],
                                      knope[:, 1024 + sl:1024 + sl + ww],
                                      start=True, stop=False))
                        P.op("tensor", lambda E, moff=moff, sl=sl, ww=ww:
                             E.matmul(ps_sc[:, 1024 + sl:1024 + sl + ww],
                                      onesrow_bf[:, :],
                                      maskrow[0:1, moff + sl:moff + sl + ww],
                                      start=False, stop=False))
                        P.op("tensor", lambda E, par=par, p=p, sl=sl, ww=ww:
                             E.matmul(ps_sc[:, 1024 + sl:1024 + sl + ww],
                                      qrope[par * 64:(par + 1) * 64,
                                            p * 128:(p + 1) * 128],
                                      kpe_rope[par * 64:(par + 1) * 64,
                                               1024 + sl:1024 + sl + ww],
                                      start=False, stop=True))
                    sc_tt = P.inc_last("tensor", "t")
                    P.wait("vector", "t", sc_tt)
                    P.op("vector", lambda E, p=p: E.tensor_add(
                        diag[:, :], ps_sc[:, p * 128:(p + 1) * 128],
                        tri[:, :]))
                    P.op("vector", lambda E, Wp=Wp: E.reduce_max(
                        mxcol[:, 0:1], ps_sc[:, 0:Wp], axis=AX.X))
                    vt = P.op("vector", lambda E, Wp=Wp: E.reduce_max(
                        mxcol[:, 1:2], ps_sc[:, 1024:1024 + Wp], axis=AX.X),
                        inc="v")
                    P.wait("vector", "v", vt)
                    vt = P.op("vector", lambda E: E.tensor_tensor(
                        mxcol[:, 2:3], mxcol[:, 0:1], mxcol[:, 1:2],
                        ALU.max), inc="v")
                    P.wait("vector", "v", vt)
                    max_v = P.op("vector", lambda E: E.tensor_scalar(
                        mxcol[:, 4:5], mxcol[:, 2:3], -SC_ATT, None,
                        ALU.mult), inc="v")
                    P.wait("scalar", "v", max_v)
                    P.wait("scalar", "t", sc_tt)
                    P.wait("scalar", "t", tr_last_t)
                    if p > 0:
                        P.op("scalar", lambda E, p=p: E.activation(
                            e_sb[:, 0:p * 128], ps_sc[:, 0:p * 128], AF.Exp,
                            bias=mxcol[:, 4:5], scale=fc[:, 2:3],
                            accum_out=acc_col[:, 0:1]), inc="a")
                    P.op("scalar", lambda E, p=p, Wp=Wp: E.activation(
                        e_sb[:, p * 128:Wp], diag[:, :], AF.Exp,
                        bias=mxcol[:, 4:5], scale=fc[:, 2:3],
                        accum_out=acc_col[:, 1:2]), inc="a")
                    exp_a = P.op("scalar", lambda E, Wp=Wp: E.activation(
                        e_sb[:, Wp:2 * Wp], ps_sc[:, 1024:1024 + Wp], AF.Exp,
                        bias=mxcol[:, 4:5], scale=fc[:, 2:3],
                        accum_out=acc_col[:, 2:3]), inc="a")
                    P.wait("vector", "a", exp_a)
                    if p > 0:
                        vt = P.op("vector", lambda E: E.tensor_add(
                            acc_col[:, 4:5], acc_col[:, 0:1],
                            acc_col[:, 1:2]), inc="v")
                        P.wait("vector", "v", vt)
                        vt = P.op("vector", lambda E: E.tensor_add(
                            acc_col[:, 4:5], acc_col[:, 4:5],
                            acc_col[:, 2:3]), inc="v")
                    else:
                        vt = P.op("vector", lambda E: E.tensor_add(
                            acc_col[:, 4:5], acc_col[:, 1:2],
                            acc_col[:, 2:3]), inc="v")
                    P.wait("vector", "v", vt)
                    rec_v = P.op("vector", lambda E: E.reciprocal(
                        rec_col[:, 0:1], acc_col[:, 4:5]), inc="v")
                    P.wait("tensor", "v", rec_v)
                    P.wait("tensor", "v", rr_v)
                    rrt = P.op("tensor", lambda E: E.transpose(
                        ps_rr[0:1, 0:128], rec_col[:, 0:1], identf[:, :]),
                        inc="t")
                    P.wait("vector", "t", rrt)
                    P.wait("vector", "t", rb_mm_t)
                    rr_v = P.op("vector", lambda E, p4=p4: E.tensor_copy(
                        recrow[0:1, p4 * 128:(p4 + 1) * 128],
                        ps_rr[0:1, 0:128]), inc="v")
                    P.wait("tensor", "a", exp_a)
                    for j0 in range(0, 2 * (p + 1), 4):
                        jn = min(j0 + 4, 2 * (p + 1))
                        P.wait("tensor", "v", pT_ev)
                        for j in range(j0, jn):
                            P.op("tensor", lambda E, j=j, j0=j0,
                                 st=(j == j0), sp=(j == jn - 1): E.matmul(
                                     ps_tr[:, (j - j0) * 128:
                                           (j - j0 + 1) * 128],
                                     e_sb[:, j * 128:(j + 1) * 128],
                                     ident[:, :], start=st, stop=sp,
                                     is_transpose=True))
                        tr_last_t = P.inc_last("tensor", "t")
                        P.wait("vector", "t", tr_last_t)
                        P.wait("vector", "t", av_mm_t)
                        for j in range(j0, jn):
                            kc = j if j <= p else 8 + (j - (p + 1))
                            pT_ev = P.op("vector", lambda E, kc=kc, p4=p4,
                                         j=j, j0=j0: E.tensor_copy(
                                             pT[kc][:, p4 * 128:
                                                    (p4 + 1) * 128],
                                             ps_tr[:, (j - j0) * 128:
                                                   (j - j0 + 1) * 128]),
                                         inc="v")
                P.wait("tensor", "v", pT_ev)
                P.wait("tensor", "v", avt_v)
                first = True
                for reg in range(2):
                    for kcl in range(4 * g + 4):
                        kc = kcl + 8 * reg
                        a0 = max(0, kcl - 4 * g) * 128
                        last = (reg == 1 and kcl == 4 * g + 3)
                        P.op("tensor", lambda E, kc=kc, a0=a0, st=first,
                             sp=last: E.matmul(
                                 av_ps[:, a0:512],
                                 vtm[:, kc * 128:(kc + 1) * 128],
                                 pT[kc][:, a0:512], start=st, stop=sp))
                        first = False
                av_mm_t = P.inc_last("tensor", "t")
                knope_gate_t = av_mm_t
                P.wait("tensor", "v", rr_v)
                P.wait("tensor", "v", rbcp_v)
                rb_mm_t = P.op("tensor", lambda E: E.matmul(
                    ps_rb[:, :], onesrow_f[:, :], recrow[0:1, 0:512],
                    start=True, stop=True), inc="t")
                P.wait("vector", "t", rb_mm_t)
                rbcp_v = P.op("vector", lambda E: E.tensor_copy(
                    rb_sb[:, :], ps_rb[:, :]), inc="v")
                P.wait("vector", "v", rbcp_v)
                P.wait("vector", "t", av_mm_t)
                avt_v = P.op("vector", lambda E: E.tensor_mul(
                    avt[:, :], av_ps[:, :], rb_sb[:, :]), inc="v")
                P.wait("vector", "v", avt_v)
                prev_evac_v = P.op("vector", lambda E, h=h, g=g:
                                   E.tensor_scalar(
                                       attn_all[:, h * TOK + g * 512:
                                                h * TOK + (g + 1) * 512],
                                       avt[:, :],
                                       iot[:, (h // 2) * 4:(h // 2) * 4 + 1],
                                       None, ALU.add), inc="v")
            prev_exp_a = exp_a
            prev_max_v = max_v
            esb_gate_t = tr_last_t
            es_sc.close()
        qp_gate_t = sc_tt
    fence_wait(mark_fence())
    es8.close()

    es9 = contextlib.ExitStack()
    es9w = contextlib.ExitStack()
    x2 = es9.enter_context(nc.sbuf_tensor("x_own2", [128, 16 * TOK], F32))
    owt2 = [es9w.enter_context(
        nc.sbuf_tensor("wo_t%d" % i, [128, 2 * 16 * 128], BF16))
        for i in range(2)]
    pso2 = [es9w.enter_context(
        nc.psum_tensor("pso2_%d" % i, [128, TOK], F32)) for i in range(4)]
    P.wait("sync", "spx", sp_tok)
    xre = 0
    for c in range(16):
        xre = P.dma("sync", x2[:, c * TOK:(c + 1) * TOK], xres_dram.ap()[c],
                    "xre")
    P.wait("vector", "xre", xre)
    wo_tok = [0, 0]
    wo_free = [0, 0]
    wo_ev = [0, 0, 0, 0]
    for mg in range(8):
        bi = mg % 2
        P.wait("gpsimd", "t", wo_free[bi])
        wo_tok[bi] = P.dma("gpsimd", owt2[bi][:, :],
                           D["woblk"].ap()[:, 2 * mg:2 * mg + 2, :, :],
                           "wo%d" % bi)
        P.wait("tensor", "wo%d" % bi, wo_tok[bi])
        for i in range(2):
            ps = pso2[2 * bi + i]
            P.wait("tensor", "v", wo_ev[2 * bi + i])
            for k in range(16):
                for sl in (0, 512):
                    P.op("tensor", lambda E, bi=bi, i=i, k=k, sl=sl, ps=ps:
                         E.matmul(ps[:, sl:sl + 512],
                                  owt2[bi][:, (i * 16 + k) * 128:
                                           (i * 16 + k + 1) * 128],
                                  attn_all[:, k * TOK + sl:k * TOK + sl + 512],
                                  start=(k == 0), stop=(k == 15)))
            tt = P.inc_last("tensor", "t")
            wo_free[bi] = tt
            mp = 2 * mg + i
            P.wait("vector", "t", tt)
            vt = P.op("vector", lambda E, ps=ps, mp=mp: E.tensor_add(
                x2[:, mp * TOK:(mp + 1) * TOK],
                x2[:, mp * TOK:(mp + 1) * TOK], ps[:, :]), inc="v")
            wo_ev[2 * bi + i] = vt
    WO_V = P.cnt["v"]
    fence_wait(mark_fence())
    es9w.close()

    if dev_stop == "attn":
        _emit_store(P, nc, x2, out_d, WO_V)
        return _run(nc, P, top, in_maps)

    lnx2 = arena[:, 0:16 * TOK]
    comb2 = arena[:, 16 * TOK:32 * TOK]
    P.wait("vector", "v", WO_V)
    ffn2_v = emit_ffn(2, x2, "own", xtile=x2, ext=(lnx2, comb2))
    _emit_store(P, nc, x2, out_d, ffn2_v)
    return _run(nc, P, top, in_maps)




def _emit_store(P, nc, x_own, out_d, wait_v):
    P.wait("sync", "v", wait_v)
    st = 0
    for c in range(16):
        st = P.dma("sync", out_d.ap()[c], x_own[:, c * TOK:(c + 1) * TOK], "st")
    P.wait("sync", "st", st)


def _run(nc, P, top, in_maps):
    with nc.Block() as block:
        @block.sync
        def _(E):
            for f in P.q["sync"]:
                f(E)

        @block.tensor
        def _(E):
            for f in P.q["tensor"]:
                f(E)

        @block.vector
        def _(E):
            for f in P.q["vector"]:
                f(E)

        @block.scalar
        def _(E):
            for f in P.q["scalar"]:
                f(E)

        @block.gpsimd
        def _(E):
            for f in P.q["gpsimd"]:
                f(E)
    top.close()
    res = run_bass_kernel_spmd(nc, in_maps, core_ids=list(range(NCORES)))
    return res


def kernel(**inputs):
    in_maps = host_inputs(inputs)
    res = build_and_run(in_maps, DEV_STOP)
    x = np.asarray(inputs["L0_attn_x"])
    out = np.empty((B, S, H), np.float32)
    for core in range(NCORES):
        b, z = core // 2, core % 2
        o = res.results[core]["out"]
        feat = o.reshape(16 * 128, TOK)
        toks = feat.T
        own = ZIG[z]
        for i, c in enumerate(own):
            out[b, c * 128:(c + 1) * 128, :] = toks[i * 128:(i + 1) * 128]
    return out


# version 04
# speedup vs baseline: 1.0136x; 1.0070x over previous
import math
import contextlib
import numpy as np
import concourse.bass as bass
import concourse.mybir as mybir
from concourse.bass_utils import run_bass_kernel_spmd

F32 = mybir.dt.float32
BF16 = mybir.dt.bfloat16
U32 = mybir.dt.uint32
AF = mybir.ActivationFunctionType
ALU = mybir.AluOpType
AX = mybir.AxisListType

NCORES = 8
B, S, H = 4, 2048, 2048
TOK = 1024
NH, NOPE, ROPE, VD, QLR = 16, 128, 64, 128, 1536
IH, ID = 8, 128
NE = 8
EPS = 1e-5
NEG = -1.0e30
ZIG = [[0, 3, 4, 7, 8, 11, 12, 15], [1, 2, 5, 6, 9, 10, 13, 14]]
SC_ATT = 1.0 / math.sqrt(NOPE + ROPE)
SC_SP = 1.0 / math.sqrt(ID)

ENGS = ("sync", "tensor", "vector", "scalar", "gpsimd")

DEV_STOP = "full"



def _block_kxm(w, KC, MC):
    return np.ascontiguousarray(w.reshape(KC, 128, MC, 128).transpose(1, 2, 0, 3))


def _ckv_col_perm():
    cols = []
    for pair in range(8):
        for h in (2 * pair, 2 * pair + 1):
            base = h * (ROPE + NOPE)
            cols.extend(base + np.arange(0, ROPE, 2))
            cols.extend(base + np.arange(1, ROPE, 2))
    for h in range(NH):
        base = h * (ROPE + NOPE)
        cols.extend(base + ROPE + np.arange(NOPE))
    return np.array(cols)


def _q_col_perm_pair(pair):
    a, b = 2 * pair, 2 * pair + 1
    cols = []
    for h in (a, b):
        cols.extend(h * (NOPE + ROPE) + np.arange(NOPE))
    for h in (a, b):
        base = h * (NOPE + ROPE) + NOPE
        cols.extend(base + np.arange(0, ROPE, 2))
        cols.extend(base + np.arange(1, ROPE, 2))
    return np.array(cols)


def _rope_tabs(positions):
    f = 1.0 / 10000.0 ** (np.arange(0, ROPE, 2, dtype=np.float64) / ROPE)
    ang = np.outer(f, positions.astype(np.float64))
    c, s = np.cos(ang), np.sin(ang)
    tab1 = np.concatenate([c, c, c, c], 0)
    tab2 = np.concatenate([s, s, s, s], 0)
    return (tab1.astype(np.float32).astype(np.dtype("float32")),
            tab2.astype(np.float32))


def host_inputs(ins):
    import ml_dtypes
    x = np.asarray(ins["L0_attn_x"], dtype=np.float32)
    sh = {}
    sh["eblk"] = np.stack([
        _block_kxm(np.asarray(ins["L0_ffn_expert_w"], np.float32)[e], 16, 16)
        for e in range(NE)])
    sh["owblk"] = _block_kxm(np.asarray(ins["L0_ffn_out_w"], np.float32), 16, 16)
    sh["cqblk"] = _block_kxm(np.asarray(ins["L0_attn_W_cq"], np.float32), 16, 12)
    ckv = np.asarray(ins["L0_attn_W_ckv"], np.float32)[:, _ckv_col_perm()]
    sh["ckvblk"] = _block_kxm(ckv, 16, 24)
    wq = np.asarray(ins["L0_attn_W_q"], np.float32)
    sh["wqblk"] = np.stack([
        _block_kxm(wq[:, _q_col_perm_pair(p)], 12, 3) for p in range(8)])
    sh["ipblk"] = np.ascontiguousarray(
        np.asarray(ins["L0_attn_W_ip"], np.float32).reshape(16, 128, IH * ID))
    sh["wrk"] = np.ascontiguousarray(
        np.asarray(ins["L0_ffn_router_w"], np.float32).reshape(16, 128, NE))
    sh["wigk"] = np.ascontiguousarray(
        np.asarray(ins["L0_attn_W_ig"], np.float32).reshape(16, 128, IH))
    sh["rb"] = np.asarray(ins["L0_ffn_router_b"], np.float32).reshape(1, NE)
    eb = np.asarray(ins["L0_ffn_expert_b"], np.float32)
    sh["ebias"] = np.ascontiguousarray(
        eb.reshape(NE, 16, 128).transpose(2, 0, 1).reshape(128, NE * 16))
    sh["obias"] = np.ascontiguousarray(
        np.asarray(ins["L0_ffn_out_b"], np.float32).reshape(16, 128).T)
    sh["woblk"] = _block_kxm(np.asarray(ins["L0_attn_W_o"], np.float32), 16, 16)
    sh["wkv"] = np.asarray(ins["L0_attn_W_kv"], np.float32)
    sh["kvnw"] = np.asarray(ins["L0_attn_kv_norm_w"], np.float32).reshape(128, 1)
    sh["kvnb"] = np.asarray(ins["L0_attn_kv_norm_b"], np.float32).reshape(128, 1)
    sh["qnw"] = np.ascontiguousarray(
        np.asarray(ins["L0_attn_q_norm_w"], np.float32).reshape(12, 128).T)
    sh["qnb"] = np.ascontiguousarray(
        np.asarray(ins["L0_attn_q_norm_b"], np.float32).reshape(12, 128).T)
    sh["wsq"] = np.asarray(ins["L0_attn_W_sq"], np.float32)
    sh["wsk"] = np.asarray(ins["L0_attn_W_sk"], np.float32)
    sh["wsv"] = np.asarray(ins["L0_attn_W_sv"], np.float32)
    sh["wio"] = np.ascontiguousarray(
        np.asarray(ins["L0_attn_W_io"], np.float32)[:, :VD])
    sh["ident"] = np.eye(128, dtype=np.float32)
    sh["ones"] = np.ones((128, 1), np.float32)
    sh["onesrow"] = np.ones((1, 128), np.float32)
    sh["tri"] = np.where(np.triu(np.ones((128, 128), bool), k=1),
                         np.float32(NEG), np.float32(0.0)).astype(np.float32)
    esel = np.zeros((8, NE * 128), np.float32)
    for e in range(NE):
        esel[e, e * 128:(e + 1) * 128] = 1.0
    sh["esel"] = esel
    fc = np.zeros((128, 8), np.float32)
    fc[:, 0] = EPS
    fc[:, 1] = 0.5
    fc[:, 2] = SC_ATT
    fc[:, 3] = SC_SP
    fc[:, 4] = 0.125
    fc[:, 5] = 1.0
    fc[:, 6] = 1.0 / NOPE
    fc[:, 7] = -1.0 / NOPE
    sh["fconsts"] = fc
    s16c = np.zeros((128, 16 * 16), np.float32)
    for hh in range(16):
        s16c[:, hh * 16 + hh] = 1.0
    sh["sel16c"] = s16c
    s16w = np.zeros((16, 16 * 128), np.float32)
    kvw = np.asarray(ins["L0_attn_kv_norm_w"], np.float32).reshape(128)
    for hh in range(16):
        s16w[hh, hh * 128:(hh + 1) * 128] = kvw
    sh["sel16w"] = s16w

    in_maps = []
    for core in range(NCORES):
        b, z = core // 2, core % 2
        own, part = ZIG[z], ZIG[1 - z]
        corder = np.concatenate(
            [np.arange(c * 128, (c + 1) * 128) for c in own + part])
        xc = x[b][corder]
        m = dict(sh)
        m["xT_own"] = np.ascontiguousarray(xc[:TOK].T.reshape(16, 128, TOK))
        m["xT_part"] = np.ascontiguousarray(xc[TOK:].T.reshape(16, 128, TOK))
        pos = np.concatenate([np.arange(c * 128, (c + 1) * 128) for c in own])
        t1, t2 = _rope_tabs(pos)
        m["tabq1"], m["tabq2"] = t1, t2
        posk = np.concatenate(
            [np.arange(c * 128, (c + 1) * 128) for c in own + part])
        t1k, t2k = _rope_tabs(posk)
        m["tabk1"], m["tabk2"] = t1k, t2k
        pm = np.zeros((1, 36 * 128), np.float32)
        off = 0
        for p in range(8):
            for mm in range(p + 1):
                if not (part[mm] < own[p]):
                    pm[:, (off + mm) * 128:(off + mm + 1) * 128] = NEG
            off += p + 1
        m["maskrow"] = pm
        in_maps.append(m)
    return in_maps



class Prog:
    def __init__(self, nc, top):
        self.nc = nc
        self.top = top
        self.sems = {}
        self.cnt = {}
        self.q = {e: [] for e in ENGS}

    def sem(self, name):
        if name not in self.sems:
            self.sems[name] = self.top.enter_context(
                self.nc.semaphore("sem_" + name))
            self.cnt[name] = 0
        return self.sems[name]

    def wait(self, eng, sem, tok):
        if tok is None or tok <= 0:
            return
        h = self.sem(sem)
        self.q[eng].append(lambda E, h=h, t=tok: E.wait_ge(h, t))

    def op(self, eng, fn, inc=None):
        if inc is not None:
            self.sem(inc)
            self.cnt[inc] += 1
            tok = self.cnt[inc]
            h = self.sems[inc]
            self.q[eng].append(lambda E, fn=fn, h=h: fn(E).then_inc(h, 1))
            return tok
        self.q[eng].append(fn)
        return None

    def inc_last(self, eng, sem):
        self.sem(sem)
        self.cnt[sem] += 1
        tok = self.cnt[sem]
        h = self.sems[sem]
        prev = self.q[eng].pop()
        self.q[eng].append(lambda E, f=prev, h=h: f(E).then_inc(h, 1))
        return tok

    def dma(self, eng, out, in_, sem):
        self.sem(sem)
        self.cnt[sem] += 16
        tok = self.cnt[sem]
        h = self.sems[sem]
        self.q[eng].append(
            lambda E, o=out, i=in_, h=h: E.dma_start(out=o, in_=i).then_inc(h, 16))
        return tok

    def idma(self, out, in_, in_offset, sem="g"):
        self.sem(sem)
        self.cnt[sem] += 16
        tok = self.cnt[sem]
        h = self.sems[sem]
        self.q["gpsimd"].append(
            lambda E, o=out, i=in_, io=in_offset, h=h:
            E.indirect_dma_start(out=o, out_offset=None, in_=i, in_offset=io)
            .then_inc(h, 16))
        return tok



def build_and_run(in_maps, dev_stop="full"):
    nc = bass.Bass("TRN2", target_bir_lowering=False, debug=False,
                   detect_race_conditions=False)
    top = contextlib.ExitStack()

    def din(name, shape, dt=F32):
        return nc.dram_tensor(name, list(shape), dt, kind="ExternalInput")

    D = {}
    D["xT_own"] = din("xT_own", [16, 128, TOK])
    D["xT_part"] = din("xT_part", [16, 128, TOK])
    D["eblk"] = din("eblk", [NE, 128, 16, 16, 128])
    D["owblk"] = din("owblk", [128, 16, 16, 128])
    D["cqblk"] = din("cqblk", [128, 12, 16, 128])
    D["ckvblk"] = din("ckvblk", [128, 24, 16, 128])
    D["wqblk"] = din("wqblk", [8, 128, 3, 12, 128])
    D["ipblk"] = din("ipblk", [16, 128, IH * ID])
    D["wrk"] = din("wrk", [16, 128, NE])
    D["wigk"] = din("wigk", [16, 128, IH])
    D["rb"] = din("rb", [1, NE])
    D["ebias"] = din("ebias", [128, NE * 16])
    D["obias"] = din("obias", [128, 16])
    D["woblk"] = din("woblk", [128, 16, 16, 128])
    D["wkv"] = din("wkv", [128, 256])
    D["kvnw"] = din("kvnw", [128, 1])
    D["kvnb"] = din("kvnb", [128, 1])
    D["qnw"] = din("qnw", [128, 12])
    D["qnb"] = din("qnb", [128, 12])
    D["wsq"] = din("wsq", [128, 128])
    D["wsk"] = din("wsk", [128, 128])
    D["wsv"] = din("wsv", [128, 128])
    D["wio"] = din("wio", [128, VD])
    D["ident"] = din("ident", [128, 128])
    D["ones"] = din("ones", [128, 1])
    D["onesrow"] = din("onesrow", [1, 128])
    D["tri"] = din("tri", [128, 128])
    D["esel"] = din("esel", [8, NE * 128])
    D["fconsts"] = din("fconsts", [128, 8])
    D["sel16c"] = din("sel16c", [128, 16 * 16])
    D["sel16w"] = din("sel16w", [16, 16 * 128])
    D["tabq1"] = din("tabq1", [128, TOK])
    D["tabq2"] = din("tabq2", [128, TOK])
    D["tabk1"] = din("tabk1", [128, S])
    D["tabk2"] = din("tabk2", [128, S])
    D["maskrow"] = din("maskrow", [1, 36 * 128])

    out_d = nc.dram_tensor("out", [16, 128, TOK], F32, kind="ExternalOutput")

    kvc_dram = nc.dram_tensor("kvc_sc", [24, 128, S], BF16, kind="Internal")
    ist_dram = nc.dram_tensor("ist_sc", [S, IH * ID], BF16, kind="Internal")
    xres_dram = nc.dram_tensor("xres_sc", [16, 128, TOK], F32, kind="Internal")
    x1p_dram = nc.dram_tensor("x1p_sc", [16, 128, TOK], BF16, kind="Internal")

    sb = nc.alloc_sbuf_tensor
    x_own = sb("x_own", [128, 16 * TOK], F32)
    ident = sb("sb_ident", [128, 128], BF16)
    ones_bf = sb("ones_bf", [128, 1], BF16)
    onesrow_f = sb("onesrow_f", [1, 128], F32)
    wkv = sb("sb_wkv", [128, 256], BF16)
    kvnw = sb("sb_kvnw", [128, 1], F32)
    kvnb = sb("sb_kvnb", [128, 1], F32)
    qnw = sb("sb_qnw", [128, 12], F32)
    qnb = sb("sb_qnb", [128, 12], F32)
    rbias = sb("rbias", [1, NE], F32)
    ebias = sb("sb_ebias", [128, NE * 16], F32)
    obias2 = sb("obias2", [128, 16], F32)
    wr16 = sb("wr16", [128, 16 * NE], BF16)
    wig16 = sb("wig16", [128, 16 * IH], BF16)
    esel = sb("sb_esel", [8, NE * 128], F32)
    fc = sb("sb_fc", [128, 8], F32)
    onesrow_bf = sb("onesrow_bf", [1, 128], BF16)
    iot = sb("iot", [128, 32], F32)

    P = Prog(nc, top)
    fence_scr = sb("fence_scr", [128, 4], F32)
    P.op("vector", lambda E: E.memset(fence_scr[:, :], 0.0))

    def mark_fence():
        tt = P.cnt["t"]
        vt = P.op("vector", lambda E: E.tensor_copy(
            fence_scr[:, 0:1], fence_scr[:, 1:2]), inc="v")
        at = P.op("scalar", lambda E: E.activation(
            fence_scr[:, 2:3], fence_scr[:, 3:4], AF.Copy), inc="a")
        return tt, vt, at

    def fence_wait(toks, engines=("tensor", "vector", "scalar", "gpsimd", "sync")):
        tt, vt, at = toks
        for eng in engines:
            P.wait(eng, "t", tt)
            P.wait(eng, "v", vt)
            P.wait(eng, "a", at)

    for dst, src in ((ident, "ident"), (ones_bf, "ones"), (onesrow_f, "onesrow"),
                     (wkv, "wkv"), (kvnw, "kvnw"), (kvnb, "kvnb"),
                     (qnw, "qnw"), (qnb, "qnb"), (rbias, "rb"),
                     (ebias, "ebias"), (esel, "esel"), (fc, "fconsts"),
                     (onesrow_bf, "onesrow")):
        ctok = P.dma("gpsimd", dst[:, :], D[src].ap(), "ldc")
    for k in range(16):
        ctok = P.dma("gpsimd", wr16[:, k * NE:(k + 1) * NE], D["wrk"].ap()[k], "ldc")
        ctok = P.dma("gpsimd", wig16[:, k * IH:(k + 1) * IH], D["wigk"].ap()[k], "ldc")
    obias_raw = sb("obias_raw", [128, 16], F32)
    ctok = P.dma("gpsimd", obias_raw[:, :], D["obias"].ap(), "ldc")
    identf = sb("sb_identf", [128, 128], F32)
    ctok = P.dma("gpsimd", identf[:, :], D["ident"].ap(), "ldc")
    xtok = 0
    for c in range(16):
        xtok = P.dma("sync", x_own[:, c * TOK:(c + 1) * TOK],
                     D["xT_own"].ap()[c], "ldx")
    CONST_G, CONST_S = ctok, xtok
    P.wait("vector", "ldc", CONST_G)
    P.wait("vector", "ldx", CONST_S)
    P.op("vector", lambda E: E.tensor_scalar_mul(obias2[:, :], obias_raw[:, :], 0.5),
         inc="v")

    def emit_ln(ctx, srcs_fn, T, dst, affine=None, name="ln", nchunks=16,
                nfeat=H, dst_fn=None):
        es = contextlib.ExitStack()
        ps_sum = es.enter_context(nc.psum_tensor(name + "_psum", [1, T], F32))
        ps_sq = es.enter_context(nc.psum_tensor(name + "_pssq", [1, T], F32))
        ps_bc = es.enter_context(nc.psum_tensor(name + "_psbc", [128, T], F32))
        ps_bc2 = es.enter_context(nc.psum_tensor(name + "_psbc2", [128, T], F32))
        sq_scr = [es.enter_context(nc.sbuf_tensor(name + "_sq%d" % i, [128, T], BF16))
                  for i in range(2)]
        cv_scr = [es.enter_context(nc.sbuf_tensor(name + "_cv%d" % i, [128, T], BF16))
                  for i in range(2)]
        rows = es.enter_context(nc.sbuf_tensor(name + "_rows", [1, 3 * T], F32))
        abc = es.enter_context(nc.sbuf_tensor(name + "_abc", [128, T], F32))
        bbc = es.enter_context(nc.sbuf_tensor(name + "_bbc", [128, T], F32))
        if affine is not None:
            t_scr = [es.enter_context(
                nc.sbuf_tensor(name + "_t%d" % i, [128, T], BF16))
                for i in range(2)]

        pe_tok = [0] * nchunks
        sq_tok = [0] * nchunks
        for c in range(nchunks):
            srcs = srcs_fn(c)
            cv = cv_scr[c % 2]
            sq = sq_scr[c % 2]
            bf_srcs = []
            for (apc, off, w, isf32) in srcs:
                if isf32:
                    if c >= 2:
                        P.wait("vector", "t", pe_tok[c - 2])
                        P.wait("vector", "a", sq_tok[c - 2])
                    vt = P.op("vector", lambda E, a=apc, s=cv, o=off, w=w:
                              E.tensor_copy(s[:, o:o + w], a), inc="v")
                    bf_srcs.append((cv[:, off:off + w], off, w, vt))
                else:
                    bf_srcs.append((apc, off, w, None))
            for i, (apc, off, w, vt) in enumerate(bf_srcs):
                if vt:
                    P.wait("scalar", "v", vt)
                if c >= 2:
                    P.wait("scalar", "t", pe_tok[c - 2])
                at = P.op("scalar", lambda E, a=apc, s=sq, o=off, w=w:
                          E.activation(s[:, o:o + w], a, AF.Square), inc="a")
                sq_tok[c] = at
                if vt:
                    P.wait("tensor", "v", vt)
                for sl in range(0, w, 512):
                    ee = min(sl + 512, w)
                    P.op("tensor", lambda E, a=apc, o=off, sl=sl, ee=ee,
                         st=(c == 0), sp=(c == nchunks - 1):
                         E.matmul(ps_sum[0:1, o + sl:o + ee], ones_bf[:, :],
                                  a[:, sl:ee], start=st, stop=sp))
                P.wait("tensor", "a", at)
                last = None
                for sl in range(0, w, 512):
                    ee = min(sl + 512, w)
                    last = P.op("tensor", lambda E, s=sq, o=off, sl=sl, ee=ee,
                                st=(c == 0), sp=(c == nchunks - 1):
                                E.matmul(ps_sq[0:1, o + sl:o + ee], ones_bf[:, :],
                                         s[:, sl:ee], start=st, stop=sp),
                                inc="t")
                pe_tok[c] = last
        tt = P.cnt["t"]
        P.wait("vector", "t", tt)
        mean = rows[0:1, 0:T]
        m2 = rows[0:1, T:2 * T]
        aux = rows[0:1, 2 * T:3 * T]
        rstd, brow = aux, mean
        P.op("vector", lambda E: E.tensor_scalar_mul(
            mean, ps_sum[0:1, :], 1.0 / nfeat))
        vt = P.op("vector", lambda E: E.tensor_scalar_mul(
            m2, ps_sq[0:1, :], 1.0 / nfeat), inc="v")
        P.wait("scalar", "v", vt)
        at = P.op("scalar", lambda E: E.activation(
            aux, mean, AF.Square), inc="a")
        P.wait("vector", "a", at)
        vt = P.op("vector", lambda E: E.tensor_sub(m2, m2, aux), inc="v")
        P.wait("scalar", "v", vt)
        at = P.op("scalar", lambda E: E.activation(
            aux, m2, AF.Sqrt, bias=fc[0:1, 0:1]), inc="a")
        P.wait("vector", "a", at)
        vt = P.op("vector", lambda E: E.reciprocal(rstd, rstd), inc="v")
        P.wait("vector", "v", vt)
        vt = P.op("vector", lambda E: E.scalar_tensor_tensor(
            brow, mean, -1.0, rstd, ALU.mult, ALU.mult), inc="v")
        P.wait("tensor", "v", vt)
        for sl in range(0, T, 512):
            ee = min(sl + 512, T)
            P.op("tensor", lambda E, sl=sl, ee=ee: E.matmul(
                ps_bc[:, sl:ee], onesrow_f[:, :], rstd[:, sl:ee],
                start=True, stop=True))
        tt = P.inc_last("tensor", "t")
        P.wait("vector", "t", tt)
        vt = P.op("vector", lambda E: E.tensor_copy(abc[:, :], ps_bc[:, :]),
                  inc="v")
        for sl in range(0, T, 512):
            ee = min(sl + 512, T)
            P.op("tensor", lambda E, sl=sl, ee=ee: E.matmul(
                ps_bc2[:, sl:ee], onesrow_f[:, :], brow[:, sl:ee],
                start=True, stop=True))
        tt = P.inc_last("tensor", "t")
        P.wait("vector", "t", tt)
        vt = P.op("vector", lambda E: E.tensor_copy(bbc[:, :], ps_bc2[:, :]),
                  inc="v")
        P.wait("vector", "v", vt)
        mul_tok = [0] * (nchunks + 1)
        vt = 0
        if dst_fn is None:
            def dst_fn(c):
                return dst[:, c * T:(c + 1) * T]

        def emit_mult(c):
            nonlocal vt
            ts = t_scr[c % 2] if affine is not None else None
            for (apc, off, w, isf32) in srcs_fn(c):
                if affine is not None:
                    tgt = ts[:, off:off + w]
                else:
                    tgt = dst_fn(c)[:, off:off + w]
                vt = P.op("vector", lambda E, a=apc, tg=tgt, o=off, w=w:
                          E.tensor_mul(tg, a, abc[:, o:o + w]), inc="v")
            mul_tok[c] = vt

        def emit_add(c):
            nonlocal vt
            P.wait("vector", "v", mul_tok[c])
            ts = t_scr[c % 2] if affine is not None else None
            for (apc, off, w, isf32) in srcs_fn(c):
                if affine is None:
                    tgt = dst_fn(c)[:, off:off + w]
                    vt = P.op("vector", lambda E, tg=tgt, o=off, w=w:
                              E.tensor_add(tg, tg, bbc[:, o:o + w]), inc="v")
                else:
                    wcol, bcol = affine
                    vt = P.op("vector", lambda E, o=off, w=w, ts=ts:
                              E.tensor_add(ts[:, o:o + w], ts[:, o:o + w],
                                           bbc[:, o:o + w]), inc="v")
                    P.wait("vector", "v", vt)
                    vt = P.op("vector", lambda E, o=off, w=w, ts=ts, c=c:
                              E.tensor_scalar(ts[:, o:o + w], ts[:, o:o + w],
                                              wcol[:, c:c + 1], None,
                                              ALU.mult), inc="v")
                    P.wait("vector", "v", vt)
                    vt = P.op("vector", lambda E, c=c, o=off, w=w, ts=ts:
                              E.tensor_scalar(
                                  dst_fn(c)[:, o:o + w],
                                  ts[:, o:o + w], bcol[:, c:c + 1], None,
                                  ALU.add), inc="v")

        for c in range(nchunks):
            emit_mult(c)
            if c >= 1:
                emit_add(c - 1)
        emit_add(nchunks - 1)
        es.close()
        return vt, "v"

    def emit_ffn(idx, src_f32, target, upto=None, xtile=None, ext=None):
        name = "f%d" % idx
        if xtile is None:
            xtile = x_own
        es = contextlib.ExitStack()
        if ext is None:
            lnx = es.enter_context(
                nc.sbuf_tensor(name + "_lnx", [128, 16 * TOK], BF16))
            comb = es.enter_context(
                nc.sbuf_tensor(name + "_comb", [128, 16 * TOK], BF16))
        else:
            lnx, comb = ext

        es_x0 = contextlib.ExitStack()
        if src_f32 is None:
            x0p = es_x0.enter_context(
                nc.sbuf_tensor(name + "_x0p", [128, 16 * TOK], BF16))
            ltoks = []
            for c in range(16):
                ltoks.append(P.dma("gpsimd", x0p[:, c * TOK:(c + 1) * TOK],
                                   D["xT_part"].ap()[c], "ldp"))

            def srcs_fn(c):
                return [(x0p[:, c * TOK:(c + 1) * TOK], 0, TOK, False)]
            P.wait("vector", "ldp", ltoks[-1])
            P.wait("scalar", "ldp", ltoks[-1])
            P.wait("tensor", "ldp", ltoks[-1])
        else:
            def srcs_fn(c):
                return [(src_f32[:, c * TOK:(c + 1) * TOK], 0, TOK, True)]
            P.wait("vector", "ldx", CONST_S)
        P.wait("tensor", "ldc", CONST_G)
        lntok, lnsem = emit_ln(es, srcs_fn, TOK, lnx, name=name + "ln")
        P.wait("tensor", lnsem, lntok)
        P.wait("gpsimd", lnsem, lntok)
        fence_wait(mark_fence())
        es_x0.close()
        if upto == "ln":
            st = 0
            for c in range(16):
                st = P.dma("gpsimd", out_d.ap()[c],
                           lnx[:, c * TOK:(c + 1) * TOK], "st")
            P.wait("gpsimd", "st", st)
            es.close()
            return P.cnt["v"]

        es_gb = contextlib.ExitStack()
        gbc = es_gb.enter_context(
            nc.sbuf_tensor(name + "_gbc", [128, NE * TOK], BF16))
        es_r = contextlib.ExitStack()
        ps_r = es_r.enter_context(nc.psum_tensor(name + "_psr", [128, NE], F32))
        ps_tr = es_r.enter_context(nc.psum_tensor(name + "_pstr", [8, 128], F32))
        lg = es_r.enter_context(nc.sbuf_tensor(name + "_lg", [128, NE], F32))
        vm = es_r.enter_context(nc.sbuf_tensor(name + "_vm", [128, 8], F32))
        cols = es_r.enter_context(nc.sbuf_tensor(name + "_cols", [128, 28], F32))
        acols = es_r.enter_context(nc.sbuf_tensor(name + "_acols", [128, 4], F32))
        ia_t = es_r.enter_context(nc.sbuf_tensor(name + "_ia", [128, NE], F32))
        iab_t = es_r.enter_context(nc.sbuf_tensor(name + "_iab", [128, NE], F32))
        gtm = es_r.enter_context(nc.sbuf_tensor(name + "_gtm", [128, NE], F32))
        gfm = es_r.enter_context(nc.sbuf_tensor(name + "_gfm", [8, TOK], F32))
        ps_bc2 = es_r.enter_context(nc.psum_tensor(name + "_psb2", [128, TOK], F32))
        lg_done = 0
        gfm_done = 0
        tr_tok = 0
        for tm in range(8):
            P.wait("tensor", "v", lg_done)
            for k in range(16):
                P.op("tensor", lambda E, tm=tm, k=k: E.matmul(
                    ps_r[:, :], lnx[:, k * TOK + tm * 128:k * TOK + (tm + 1) * 128],
                    wr16[:, k * NE:(k + 1) * NE],
                    start=(k == 0), stop=False))
            tt = P.op("tensor", lambda E: E.matmul(
                ps_r[:, :], onesrow_f[:, :], rbias[:, :],
                start=False, stop=True), inc="t")
            P.wait("vector", "t", tt)
            vt = P.op("vector", lambda E: E.tensor_copy(lg[:, :], ps_r[:, :]),
                      inc="v")
            lg_done = vt
            P.wait("vector", "v", vt)
            vt = P.op("vector", lambda E: E.max(out=vm[:, :], in_=lg[:, :]),
                      inc="v")
            P.wait("vector", "v", vt)
            vt = P.op("vector", lambda E: E.tensor_sub(
                cols[:, 0:1], vm[:, 0:1], vm[:, 1:2]), inc="v")
            sub_tok = vt
            P.op("vector", lambda E: E.tensor_add(
                cols[:, 4:5], vm[:, 0:1], vm[:, 1:2]))
            vt = P.op("vector", lambda E: E.tensor_add(
                cols[:, 8:9], vm[:, 1:2], vm[:, 2:3]), inc="v")
            P.wait("vector", "v", vt)
            P.op("vector", lambda E: E.tensor_scalar(
                cols[:, 12:13], cols[:, 4:5], 0.5, None, ALU.mult))
            vt = P.op("vector", lambda E: E.tensor_scalar(
                cols[:, 16:17], cols[:, 8:9], 0.5, None, ALU.mult), inc="v")
            thr_tok = vt
            P.wait("scalar", "v", sub_tok)
            at = P.op("scalar", lambda E: E.activation(
                acols[:, 0:1], cols[:, 0:1], AF.Sigmoid), inc="a")
            P.wait("vector", "a", at)
            P.op("vector", lambda E: E.tensor_scalar(
                cols[:, 20:21], acols[:, 0:1], -1.0, 1.0, ALU.mult, ALU.add))
            vt = P.op("vector", lambda E: E.tensor_scalar(
                cols[:, 24:25], acols[:, 0:1], 2.0, -1.0, ALU.mult, ALU.add),
                inc="v")
            P.wait("vector", "v", vt)
            P.op("vector", lambda E: E.tensor_scalar(
                ia_t[:, :], lg[:, :], cols[:, 12:13], None, ALU.is_ge))
            vt = P.op("vector", lambda E: E.tensor_scalar(
                iab_t[:, :], lg[:, :], cols[:, 16:17], None, ALU.is_ge), inc="v")
            P.wait("vector", "v", vt)
            P.wait("vector", "t", tr_tok)
            vt = P.op("vector", lambda E: E.tensor_scalar(
                gtm[:, :], ia_t[:, :], cols[:, 24:25], None, ALU.mult), inc="v")
            P.wait("vector", "v", vt)
            vt = P.op("vector", lambda E: E.scalar_tensor_tensor(
                gtm[:, :], iab_t[:, :], cols[:, 20:21], gtm[:, :],
                ALU.mult, ALU.add), inc="v")
            P.wait("tensor", "v", vt)
            P.wait("tensor", "v", gfm_done)
            tt = P.op("tensor", lambda E, tm=tm: E.transpose(
                ps_tr[:, :], gtm[:, :], identf[:, :]), inc="t")
            tr_tok = tt
            P.wait("vector", "t", tt)
            vt = P.op("vector", lambda E, tm=tm: E.tensor_copy(
                gfm[:, tm * 128:(tm + 1) * 128], ps_tr[:, :]), inc="v")
            gfm_done = vt
        P.wait("tensor", "v", P.cnt["v"])
        bc_done = 0
        for e in range(NE):
            P.wait("tensor", "v", bc_done)
            for sl in range(0, TOK, 512):
                P.op("tensor", lambda E, e=e, sl=sl: E.matmul(
                    ps_bc2[:, sl:sl + 512], esel[:, e * 128:(e + 1) * 128],
                    gfm[:, sl:sl + 512], start=True, stop=True))
            tt = P.inc_last("tensor", "t")
            P.wait("vector", "t", tt)
            vt = P.op("vector", lambda E, e=e: E.tensor_copy(
                gbc[:, e * TOK:(e + 1) * TOK], ps_bc2[:, :]), inc="v")
            bc_done = vt
        GATES_V = P.cnt["v"]
        fence_wait(mark_fence())
        es_r.close()
        if upto == "router":
            st = 0
            for e in range(NE):
                st = P.dma("gpsimd", out_d.ap()[e],
                           gbc[:, e * TOK:(e + 1) * TOK], "st")
            P.wait("gpsimd", "st", st)
            es.close()
            return P.cnt["v"]

        es_e = contextlib.ExitStack()
        NBUF = 2
        wt = [es_e.enter_context(
            nc.sbuf_tensor(name + "_wt%d" % i, [128, 16 * 128], BF16))
            for i in range(NBUF)]
        hsb = [es_e.enter_context(
            nc.sbuf_tensor(name + "_h%d" % i, [128, TOK], BF16)) for i in range(2)]
        tsb = [es_e.enter_context(
            nc.sbuf_tensor(name + "_ts%d" % i, [128, TOK], BF16)) for i in range(2)]
        cm = [es_e.enter_context(
            nc.sbuf_tensor(name + "_cm%d" % i, [128, TOK], F32)) for i in range(2)]
        pse = [es_e.enter_context(
            nc.psum_tensor(name + "_pse%d" % i, [128, TOK], F32)) for i in range(3)]
        P.wait("vector", "ldc", CONST_G)

        wt_tok = [0] * NBUF
        wt_free = [0] * NBUF
        comb_tok = [0] * 16
        mul_t = 0
        add_t = 0
        evac_a = [0] * 3
        evac_v = {}
        piece = 0
        for m in range(16):
            for e in range(NE):
                bi = piece % NBUF
                si = piece % 3
                P.wait("gpsimd", "t", wt_free[bi])
                wt_tok[bi] = P.dma("gpsimd", wt[bi][:, :],
                                   D["eblk"].ap()[e, :, m, :, :], "wt%d" % bi)
                P.wait("tensor", "wt%d" % bi, wt_tok[bi])
                P.wait("tensor", "a", evac_a[si])
                for k in range(16):
                    for sl in (0, 512):
                        P.op("tensor", lambda E, bi=bi, si=si, k=k, sl=sl:
                             E.matmul(pse[si][:, sl:sl + 512],
                                      wt[bi][:, k * 128:(k + 1) * 128],
                                      lnx[:, k * TOK + sl:k * TOK + sl + 512],
                                      start=(k == 0), stop=(k == 15)))
                tt = P.inc_last("tensor", "t")
                wt_free[bi] = tt
                hb = hsb[piece % 2]
                P.wait("scalar", "t", tt)
                if piece >= 2:
                    P.wait("scalar", "v", evac_v.get(piece - 2, 0))
                at = P.op("scalar", lambda E, si=si, hb=hb, e=e, m=m:
                          E.activation(hb[:, :], pse[si][:, :],
                                       AF.Gelu_apprx_tanh,
                                       bias=ebias[:, e * 16 + m:e * 16 + m + 1]),
                          inc="a")
                evac_a[si] = at
                P.wait("vector", "a", at)
                if e == 0:
                    P.wait("vector", "v", GATES_V)
                    if m >= 2:
                        P.wait("vector", "v", comb_tok[m - 2])
                    vt = P.op("vector", lambda E, hb=hb, e=e, m=m: E.tensor_mul(
                        cm[m % 2][:, :], hb[:, :],
                        gbc[:, e * TOK:(e + 1) * TOK]), inc="v")
                    mul_t = vt
                    add_t = vt
                else:
                    vt = P.op("vector", lambda E, hb=hb, e=e: E.tensor_mul(
                        tsb[e % 2][:, :], hb[:, :],
                        gbc[:, e * TOK:(e + 1) * TOK]), inc="v")
                    prev_mul = mul_t
                    mul_t = vt
                    if e >= 2:
                        P.wait("vector", "v", prev_mul)
                        P.wait("vector", "v", add_t)
                        vt = P.op("vector", lambda E, m=m, e=e: E.tensor_add(
                            cm[m % 2][:, :], cm[m % 2][:, :],
                            tsb[(e - 1) % 2][:, :]), inc="v")
                        add_t = vt
                evac_v[piece] = vt
                piece += 1
            P.wait("vector", "v", mul_t)
            P.wait("vector", "v", add_t)
            vt = P.op("vector", lambda E, m=m: E.tensor_add(
                cm[m % 2][:, :], cm[m % 2][:, :], tsb[1][:, :]), inc="v")
            P.wait("vector", "v", vt)
            vt = P.op("vector", lambda E, m=m: E.tensor_copy(
                comb[:, m * TOK:(m + 1) * TOK], cm[m % 2][:, :]), inc="v")
            comb_tok[m] = vt
            evac_v[piece - 1] = vt
        COMB_V = P.cnt["v"]
        fence_wait(mark_fence())
        es_e.close()
        es_gb.close()
        if upto == "experts":
            es.close()
            return P.cnt["v"]

        es_o = contextlib.ExitStack()
        owt = [es_o.enter_context(
            nc.sbuf_tensor(name + "_ow%d" % i, [128, 2 * 16 * 128], BF16))
            for i in range(2)]
        pso = [es_o.enter_context(
            nc.psum_tensor(name + "_pso%d" % i, [128, TOK], F32))
            for i in range(4)]
        tev = [es_o.enter_context(
            nc.sbuf_tensor(name + "_te%d" % i, [128, TOK], F32)) for i in range(4)]
        if target == "part":
            xps = [es_o.enter_context(
                nc.sbuf_tensor(name + "_xp%d" % i, [128, TOK], F32))
                for i in range(2)]
            x1st = [es_o.enter_context(
                nc.sbuf_tensor(name + "_x1s%d" % i, [128, TOK], BF16))
                for i in range(2)]
        P.wait("vector", "v", COMB_V)
        ow_tok = [0, 0]
        ow_free = [0, 0]
        ev_done = [0, 0, 0, 0]
        xp_tok = [0, 0]
        xp_used = [0, 0]
        x1_st = [0, 0]
        pend = []

        def _emit_xadd(j_mp):
            j, pmp = j_mp
            P.wait("vector", "v", ev_done[j])
            if target == "own":
                P.op("vector", lambda E, j=j, mp=pmp: E.tensor_add(
                    xtile[:, mp * TOK:(mp + 1) * TOK],
                    xtile[:, mp * TOK:(mp + 1) * TOK], tev[j][:, :]), inc="v")
            else:
                bj = j % 2
                P.wait("vector", "xp%d" % bj, xp_tok[bj])
                P.wait("vector", "sx%d" % bj, x1_st[bj])
                vt = P.op(
                    "vector", lambda E, j=j, bj=bj: E.tensor_add(
                        x1st[bj][:, :], xps[bj][:, :], tev[j][:, :]), inc="v")
                xp_used[bj] = vt
                P.wait("sync", "v", vt)
                x1_st[bj] = P.dma("sync", x1p_dram.ap()[pmp],
                                  x1st[bj][:, :], "sx%d" % bj)

        for mg in range(8):
            bi = mg % 2
            P.wait("gpsimd", "t", ow_free[bi])
            ow_tok[bi] = P.dma("gpsimd", owt[bi][:, :],
                               D["owblk"].ap()[:, 2 * mg:2 * mg + 2, :, :],
                               "ow%d" % bi)
            P.wait("tensor", "ow%d" % bi, ow_tok[bi])
            P.wait("tensor", "v", COMB_V)
            for i in range(2):
                if len(pend) >= 2:
                    _emit_xadd(pend.pop(0))
                if target == "part":
                    P.wait("sync", "v", xp_used[i])
                    xp_tok[i] = P.dma("sync", xps[i][:, :],
                                      D["xT_part"].ap()[2 * mg + i], "xp%d" % i)
                ps = pso[2 * bi + i]
                P.wait("tensor", "v", ev_done[2 * bi + i])
                for k in range(16):
                    for sl in (0, 512):
                        P.op("tensor", lambda E, bi=bi, i=i, k=k, sl=sl, ps=ps:
                             E.matmul(ps[:, sl:sl + 512],
                                      owt[bi][:, (i * 16 + k) * 128:
                                              (i * 16 + k + 1) * 128],
                                      comb[:, k * TOK + sl:k * TOK + sl + 512],
                                      start=(k == 0), stop=(k == 15)))
                tt = P.inc_last("tensor", "t")
                ow_free[bi] = tt
                mp = 2 * mg + i
                j = 2 * bi + i
                P.wait("vector", "t", tt)
                at = P.op("vector", lambda E, ps=ps, j=j, mp=mp:
                          E.scalar_tensor_tensor(
                              tev[j][:, :], ps[:, :], 0.5,
                              obias2[:, mp:mp + 1].to_broadcast([128, TOK]),
                              ALU.mult, ALU.add), inc="v")
                ev_done[j] = at
                pend.append((j, mp))
        while pend:
            _emit_xadd(pend.pop(0))
        FFN_V = P.cnt["v"]
        fence_wait(mark_fence())
        es_o.close()
        es.close()
        return FFN_V

    if dev_stop in ("ln", "router", "experts"):
        v = emit_ffn(0, x_own, "own", upto=dev_stop)
        if dev_stop not in ("ln", "router"):
            _emit_store(P, nc, x_own, out_d, v)
        return _run(nc, P, top, in_maps)
    ffn1_v = emit_ffn(0, x_own, "own")

    if dev_stop == "ffn1own":
        _emit_store(P, nc, x_own, out_d, ffn1_v)
        return _run(nc, P, top, in_maps)

    ffn1p_v = emit_ffn(1, None, "part")
    X1P_ST = (P.cnt.get("sx0", 0), P.cnt.get("sx1", 0))

    if dev_stop == "ffn1":
        _emit_store(P, nc, x_own, out_d, ffn1_v)
        return _run(nc, P, top, in_maps)


    arena = x_own.ap().bitcast(BF16)

    es2 = contextlib.ExitStack()
    ln2x = es2.enter_context(nc.sbuf_tensor("ln2x", [128, 16 * S], BF16))
    x1p = es2.enter_context(nc.sbuf_tensor("x1p_sb", [128, 16 * TOK], BF16))
    P.wait("gpsimd", "sx0", X1P_ST[0])
    P.wait("gpsimd", "sx1", X1P_ST[1])
    lp2 = 0
    for c in range(16):
        lp2 = P.dma("gpsimd", x1p[:, c * TOK:(c + 1) * TOK],
                    x1p_dram.ap()[c], "ldp2")
    for eng in ("tensor", "vector", "scalar"):
        P.wait(eng, "ldp2", lp2)

    def ln2_srcs_own(c):
        return [(x_own[:, c * TOK:(c + 1) * TOK], 0, TOK, True)]

    def ln2_srcs_par(c):
        return [(x1p[:, c * TOK:(c + 1) * TOK], 0, TOK, False)]

    P.wait("vector", "v", ffn1p_v)
    emit_ln(None, ln2_srcs_own, TOK, None, name="ln2a",
            dst_fn=lambda c: ln2x[:, c * S:c * S + TOK])
    ln2tok, ln2sem = emit_ln(None, ln2_srcs_par, TOK, None, name="ln2b",
                             dst_fn=lambda c: ln2x[:, c * S + TOK:
                                                   (c + 1) * S])
    P.wait("tensor", ln2sem, ln2tok)
    P.wait("gpsimd", ln2sem, ln2tok)
    P.wait("sync", ln2sem, ln2tok)
    fence_wait(mark_fence())

    sp_tok = 0
    for c in range(16):
        sp_tok = P.dma("sync", xres_dram.ap()[c],
                       x_own[:, c * TOK:(c + 1) * TOK], "spx")
    qc_raw = arena[:, 0:12 * TOK]
    attn_all = arena[:, 12 * TOK:12 * TOK + 16 * TOK]

    es3 = contextlib.ExitStack()
    kvt = [es3.enter_context(nc.sbuf_tensor("kv_wt%d" % i, [128, 16 * 128], BF16))
           for i in range(2)]
    kevac = [es3.enter_context(nc.sbuf_tensor("kv_ev%d" % i, [128, S], BF16))
             for i in range(2)]
    ps_kv = [es3.enter_context(nc.psum_tensor("ps_kv%d" % i, [128, S], F32))
             for i in range(2)]
    kv_dma = [0, 0]
    kv_mm = [0, 0]
    kv_ev_a = [0, 0]
    kv_st = [0, 0]
    for m in range(24):
        bi = m % 2
        P.wait("gpsimd", "t", kv_mm[bi])
        kv_dma[bi] = P.dma("gpsimd", kvt[bi][:, :],
                           D["ckvblk"].ap()[:, m, :, :], "kvw%d" % bi)
        P.wait("tensor", "kvw%d" % bi, kv_dma[bi])
        P.wait("tensor", "a", kv_ev_a[bi])
        for k in range(16):
            for sl in range(0, S, 512):
                P.op("tensor", lambda E, bi=bi, k=k, sl=sl: E.matmul(
                    ps_kv[bi][:, sl:sl + 512], kvt[bi][:, k * 128:(k + 1) * 128],
                    ln2x[:, k * S + sl:k * S + sl + 512],
                    start=(k == 0), stop=(k == 15)))
        tt = P.inc_last("tensor", "t")
        kv_mm[bi] = tt
        P.wait("scalar", "t", tt)
        P.wait("scalar", "s%d" % bi, kv_st[bi])
        at = P.op("scalar", lambda E, bi=bi: E.activation(
            kevac[bi][:, :], ps_kv[bi][:, :], AF.Copy), inc="a")
        kv_ev_a[bi] = at
        P.wait("sync", "a", at)
        kv_st[bi] = P.dma("sync", kvc_dram.ap()[m], kevac[bi][:, :], "s%d" % bi)
    fence_wait(mark_fence())
    es3.close()

    es4 = contextlib.ExitStack()
    wip = es4.enter_context(nc.sbuf_tensor("wip", [128, 16 * 512], BF16))
    ievac = [es4.enter_context(nc.sbuf_tensor("ist_ev%d" % i, [128, 512], BF16))
             for i in range(2)]
    ps_ist = [es4.enter_context(nc.psum_tensor("ps_ist%d" % i, [128, 512], F32))
              for i in range(2)]
    half_mm = 0
    for half in range(2):
        P.wait("gpsimd", "t", half_mm)
        htok = 0
        for k in range(16):
            htok = P.dma("gpsimd", wip[:, k * 512:(k + 1) * 512],
                         D["ipblk"].ap()[k, :, half * 512:(half + 1) * 512],
                         "wiph")
        P.wait("tensor", "wiph", htok)
        i_mm = [0, 0]
        i_ev = [0, 0]
        i_st = [0, 0]
        for tokm in range(16):
            bi = tokm % 2
            P.wait("tensor", "v", i_ev[bi])
            for k in range(16):
                P.op("tensor", lambda E, bi=bi, k=k, tokm=tokm: E.matmul(
                    ps_ist[bi][:, :],
                    ln2x[:, k * S + tokm * 128:k * S + (tokm + 1) * 128],
                    wip[:, k * 512:(k + 1) * 512],
                    start=(k == 0), stop=(k == 15)))
            tt = P.inc_last("tensor", "t")
            i_mm[bi] = tt
            half_mm = tt
            P.wait("vector", "t", tt)
            P.wait("vector", "si%d" % bi, i_st[bi])
            vt = P.op("vector", lambda E, bi=bi: E.tensor_copy(
                ievac[bi][:, :], ps_ist[bi][:, :]), inc="v")
            i_ev[bi] = vt
            P.wait("sync", "v", vt)
            i_st[bi] = P.dma(
                "sync",
                ist_dram.ap()[tokm * 128:(tokm + 1) * 128,
                              half * 512:(half + 1) * 512],
                ievac[bi][:, :], "si%d" % bi)
    IST_ST = (P.cnt["si0"], P.cnt["si1"])
    fence_wait(mark_fence())
    es4.close()

    es5 = contextlib.ExitStack()
    ps_g = es5.enter_context(nc.psum_tensor("ps_g", [8, S], F32))
    gsp = es5.enter_context(nc.sbuf_tensor("gsp", [8, S], F32))
    svals = es5.enter_context(nc.sbuf_tensor("svals", [8, 8], F32))
    sidx = es5.enter_context(nc.sbuf_tensor("sidx", [32, 32], U32))
    sidxT = es5.enter_context(nc.sbuf_tensor("sidxT", [128, 32], U32))
    selraw = es5.enter_context(nc.sbuf_tensor("selraw", [8, 2 * 1024], BF16))
    selT = es5.enter_context(nc.sbuf_tensor("selT", [128, 64], BF16))
    sqT = es5.enter_context(nc.sbuf_tensor("sqT", [128, 64], BF16))
    skT = es5.enter_context(nc.sbuf_tensor("skT", [128, 64], BF16))
    sv2 = es5.enter_context(nc.sbuf_tensor("sv2", [8, 8 * 128], BF16))
    so2 = es5.enter_context(nc.sbuf_tensor("so2", [8, 8 * 128], BF16))
    soT = es5.enter_context(nc.sbuf_tensor("soT", [128, 64], BF16))
    srows = es5.enter_context(nc.sbuf_tensor("srows", [8, 24], F32))
    sarow = es5.enter_context(nc.sbuf_tensor("sarow", [8, 8], F32))
    spt = es5.enter_context(nc.sbuf_tensor("spt", [8, 2 * 8 * 8], BF16))
    wsq = es5.enter_context(nc.sbuf_tensor("sb_wsq", [128, 128], BF16))
    wsk = es5.enter_context(nc.sbuf_tensor("sb_wsk", [128, 128], BF16))
    wsv = es5.enter_context(nc.sbuf_tensor("sb_wsv", [128, 128], BF16))
    wio = es5.enter_context(nc.sbuf_tensor("sb_wio", [128, VD], BF16))
    ps_s1 = es5.enter_context(nc.psum_tensor("ps_s1", [128, 512], F32))
    ps_s2 = es5.enter_context(nc.psum_tensor("ps_s2", [8, 128], F32))
    ps_s2b = es5.enter_context(nc.psum_tensor("ps_s2b", [8, 128], BF16))
    ps_s3 = es5.enter_context(nc.psum_tensor("ps_s3", [128, 64], BF16))

    swt = 0
    for dst, src in ((wsq, "wsq"), (wsk, "wsk"), (wsv, "wsv"), (wio, "wio")):
        swt = P.dma("gpsimd", dst[:, :], D[src].ap(), "ldsw")
    P.wait("tensor", "ldsw", swt)
    P.op("vector", lambda E: E.memset(sidx[:, :], 0))
    vt = P.op("vector", lambda E: E.memset(sidxT[:, :], 0), inc="v")
    P.wait("vector", "v", vt)
    for k in range(16):
        for sl in range(0, S, 512):
            P.op("tensor", lambda E, k=k, sl=sl: E.matmul(
                ps_g[:, sl:sl + 512], wig16[:, k * IH:(k + 1) * IH],
                ln2x[:, k * S + sl:k * S + sl + 512],
                start=(k == 0), stop=(k == 15)))
    tt = P.inc_last("tensor", "t")
    P.wait("vector", "t", tt)
    vt = P.op("vector", lambda E: E.tensor_copy(gsp[:, :], ps_g[:, :]), inc="v")
    P.wait("vector", "v", vt)
    vt = P.op("vector", lambda E: E.max(out=svals[:, :], in_=gsp[:, :]), inc="v")
    P.wait("vector", "v", vt)
    vt = P.op("vector", lambda E: E.max_index(
        sidx[0:8, 0:8], svals[:, :], gsp[:, :]), inc="v")
    P.wait("vector", "v", vt)
    vt = P.op("vector", lambda E: E.transpose(sidxT[0:32, 0:32], sidx[:, :]),
              inc="v")
    P.wait("gpsimd", "v", vt)
    P.wait("gpsimd", "si0", IST_ST[0])
    P.wait("gpsimd", "si1", IST_ST[1])
    g_tok = [0, 0]
    tr_t = [0, 0]
    trt = 0
    for ih in range(IH):
        bi = ih % 2
        P.wait("gpsimd", "t", tr_t[bi])
        g_tok[bi] = P.idma(selraw[:, bi * 1024:(bi + 1) * 1024],
                           ist_dram.ap(),
                           bass.IndirectOffsetOnAxis(sidxT[0:8, ih:ih + 1], 0),
                           sem="sg%d" % bi)
        P.wait("tensor", "sg%d" % bi, g_tok[bi])
        P.wait("tensor", "v", trt)
        tt = P.op("tensor", lambda E, ih=ih, bi=bi: E.transpose(
            ps_s3[:, 0:8],
            selraw[0:8, bi * 1024 + ih * 128:bi * 1024 + (ih + 1) * 128],
            ident[0:8, 0:8]), inc="t")
        tr_t[bi] = tt
        P.wait("vector", "t", tt)
        trt = P.op("vector", lambda E, ih=ih: E.tensor_copy(
            selT[:, ih * 8:(ih + 1) * 8], ps_s3[:, 0:8]), inc="v")
    P.wait("tensor", "v", trt)
    tt = P.op("tensor", lambda E: E.matmul(
        ps_s1[:, 0:64], wsq[:, :], selT[:, :], start=True, stop=True), inc="t")
    P.wait("vector", "t", tt)
    vt = P.op("vector", lambda E: E.tensor_copy(sqT[:, :], ps_s1[:, 0:64]),
              inc="v")
    P.wait("tensor", "v", vt)
    tt = P.op("tensor", lambda E: E.matmul(
        ps_s1[:, 0:64], wsk[:, :], selT[:, :], start=True, stop=True), inc="t")
    P.wait("vector", "t", tt)
    vt = P.op("vector", lambda E: E.tensor_copy(skT[:, :], ps_s1[:, 0:64]),
              inc="v")
    P.wait("tensor", "v", vt)
    svt = 0
    for ih in range(IH):
        P.wait("tensor", "v", svt)
        tt = P.op("tensor", lambda E, ih=ih: E.matmul(
            ps_s2[:, :], selT[:, ih * 8:(ih + 1) * 8], wsv[:, :],
            start=True, stop=True), inc="t")
        P.wait("vector", "t", tt)
        svt = P.op("vector", lambda E, ih=ih: E.tensor_copy(
            sv2[:, ih * 128:(ih + 1) * 128], ps_s2[:, :]), inc="v")
    P.wait("tensor", "v", svt)
    sot = 0
    for ih in range(IH):
        P.wait("tensor", "v", sot)
        tt = P.op("tensor", lambda E, ih=ih: E.matmul(
            ps_s2[:, 0:8], sqT[:, ih * 8:(ih + 1) * 8],
            skT[:, ih * 8:(ih + 1) * 8], start=True, stop=True), inc="t")
        P.wait("vector", "t", tt)
        vt = P.op("vector", lambda E, ih=ih: E.reduce_max(
            srows[:, 0:1], ps_s2[:, 0:8], axis=AX.X), inc="v")
        P.wait("vector", "v", vt)
        vt = P.op("vector", lambda E: E.tensor_scalar(
            srows[:, 4:5], srows[:, 0:1], -SC_SP, None, ALU.mult), inc="v")
        P.wait("scalar", "v", vt)
        at = P.op("scalar", lambda E, ih=ih: E.activation(
            spt[:, ih * 8:(ih + 1) * 8], ps_s2[:, 0:8], AF.Exp,
            bias=srows[:, 4:5], scale=fc[0:8, 3:4],
            accum_out=sarow[:, 0:1]), inc="a")
        P.wait("vector", "a", at)
        vt = P.op("vector", lambda E: E.reciprocal(
            srows[:, 8:9], sarow[:, 0:1]), inc="v")
        P.wait("vector", "v", vt)
        vt = P.op("vector", lambda E, ih=ih: E.tensor_scalar(
            spt[:, ih * 8:(ih + 1) * 8], spt[:, ih * 8:(ih + 1) * 8],
            srows[:, 8:9], None, ALU.mult), inc="v")
        P.wait("tensor", "v", vt)
        tt = P.op("tensor", lambda E, ih=ih: E.transpose(
            ps_s2b[:, 0:8], spt[:, ih * 8:(ih + 1) * 8], ident[0:8, 0:8]),
            inc="t")
        P.wait("vector", "t", tt)
        vt = P.op("vector", lambda E, ih=ih: E.tensor_copy(
            spt[:, 64 + ih * 8:64 + (ih + 1) * 8], ps_s2b[:, 0:8]), inc="v")
        P.wait("tensor", "v", vt)
        tt = P.op("tensor", lambda E, ih=ih: E.matmul(
            ps_s2[:, :], spt[:, 64 + ih * 8:64 + (ih + 1) * 8],
            sv2[:, ih * 128:(ih + 1) * 128], start=True, stop=True), inc="t")
        P.wait("vector", "t", tt)
        sot = P.op("vector", lambda E, ih=ih: E.tensor_copy(
            so2[:, ih * 128:(ih + 1) * 128], ps_s2[:, :]), inc="v")
        P.wait("tensor", "v", sot)
        tt = P.op("tensor", lambda E, ih=ih: E.transpose(
            ps_s3[:, 0:8], so2[0:8, ih * 128:(ih + 1) * 128],
            ident[0:8, 0:8]), inc="t")
        P.wait("vector", "t", tt)
        vt = P.op("vector", lambda E, ih=ih: E.tensor_copy(
            soT[:, ih * 8:(ih + 1) * 8], ps_s3[:, 0:8]), inc="v")
        P.wait("tensor", "v", vt)
        tt = P.op("tensor", lambda E, ih=ih: E.matmul(
            ps_s1[:, 0:8], wio[:, :], soT[:, ih * 8:(ih + 1) * 8],
            start=True, stop=True), inc="t")
        P.wait("vector", "t", tt)
        vt = P.op("vector", lambda E, ih=ih: E.reduce_sum(
            iot[:, ih * 4 + 1:ih * 4 + 2], ps_s1[:, 0:8], axis=AX.X), inc="v")
        P.wait("vector", "v", vt)
        vt = P.op("vector", lambda E, ih=ih: E.tensor_scalar(
            iot[:, ih * 4:ih * 4 + 1], iot[:, ih * 4 + 1:ih * 4 + 2],
            0.125, None, ALU.mult), inc="v")
    IOT_V = P.cnt["v"]
    fence_wait(mark_fence())
    es5.close()


    es6 = contextlib.ExitStack()
    cqt = [es6.enter_context(nc.sbuf_tensor("cq_wt%d" % i, [128, 16 * 128], BF16))
           for i in range(2)]
    ps_qc = [es6.enter_context(nc.psum_tensor("ps_qc%d" % i, [128, TOK], F32))
             for i in range(2)]
    qc_dma = [0, 0]
    qc_mm = [0, 0]
    qc_ev = [0, 0]
    P.wait("vector", "spx", sp_tok)
    for m in range(12):
        bi = m % 2
        P.wait("gpsimd", "t", qc_mm[bi])
        qc_dma[bi] = P.dma("gpsimd", cqt[bi][:, :],
                           D["cqblk"].ap()[:, m, :, :], "cqw%d" % bi)
        P.wait("tensor", "cqw%d" % bi, qc_dma[bi])
        P.wait("tensor", "v", qc_ev[bi])
        for k in range(16):
            for sl in (0, 512):
                P.op("tensor", lambda E, bi=bi, k=k, sl=sl: E.matmul(
                    ps_qc[bi][:, sl:sl + 512], cqt[bi][:, k * 128:(k + 1) * 128],
                    ln2x[:, k * S + sl:k * S + sl + 512],
                    start=(k == 0), stop=(k == 15)))
        tt = P.inc_last("tensor", "t")
        qc_mm[bi] = tt
        P.wait("vector", "t", tt)
        vt = P.op("vector", lambda E, bi=bi, m=m: E.tensor_copy(
            qc_raw[:, m * TOK:(m + 1) * TOK], ps_qc[bi][:, :]), inc="v")
        qc_ev[bi] = vt
    fence_wait(mark_fence())
    es6.close()
    es2.close()

    def qc_srcs(c):
        return [(qc_raw[:, c * TOK:(c + 1) * TOK], 0, TOK, False)]

    qlntok, qlnsem = emit_ln(None, qc_srcs, TOK, qc_raw, affine=(qnw, qnb),
                             name="qcln", nchunks=12, nfeat=QLR)
    qcn = qc_raw
    fence_wait(mark_fence())

    if dev_stop == "sweeps":
        fence_wait(mark_fence())
        st = 0
        for c in range(16):
            st = P.dma("sync", out_d.ap()[c], xres_dram.ap()[c], "st")
        P.wait("sync", "st", st)
        return _run(nc, P, top, in_maps)

    es8 = contextlib.ExitStack()

    def sbt(nm, sh, dt):
        return es8.enter_context(nc.sbuf_tensor(nm, sh, dt))

    maskrow_t = sbt("maskrow_sb", [1, 36 * 128], BF16)
    maskrow = maskrow_t[0:1, :]

    tabk1 = sbt("sb_tabk1", [128, S], BF16)
    tabk2 = sbt("sb_tabk2", [128, S], BF16)
    tabq1 = tabk1[:, 0:TOK]
    tabq2 = tabk2[:, 0:TOK]
    tri = sbt("sb_tri", [128, 128], F32)
    sel16c = sbt("sel16c_sb", [128, 16 * 16], BF16)
    sel16w = sbt("sel16w_sb", [16, 16 * 128], BF16)
    Qb = sbt("Qb", [16, S], BF16)
    Sb = sbt("Sb", [16, S], BF16)
    kraw = [sbt("kraw%d" % i, [128, S], BF16) for i in range(3)]
    kpe_rope = sbt("kpe_rope", [128, S], BF16)
    kvn = sbt("kvn", [128, S], BF16)
    Smat = sbt("Smat", [16, S], F32)
    Qmat = sbt("Qmat", [16, S], F32)
    knope = sbt("knope", [128, S], BF16)
    vtm = sbt("vtm", [128, 16 * 128], BF16)
    wq_sb = sbt("wq_sb", [128, 36 * 128], BF16)
    qp_sb = sbt("qp_sb", [128, 3 * TOK], BF16)
    qrope = sbt("qrope", [128, TOK], BF16)
    e_sb = sbt("e_sb", [128, 2048], BF16)
    diag = sbt("diag", [128, 128], F32)
    pT = [sbt("pT%d" % i, [128, 512], BF16) for i in range(16)]
    rsA = sbt("rsA", [128, S], BF16)
    rsB = sbt("rsB", [128, S], BF16)
    kvt1 = sbt("kvt1", [128, 512], BF16)
    recrow = sbt("recrow", [1, 512], F32)
    rb_sb = sbt("rb_sb", [128, 512], F32)
    avt = sbt("avt", [128, 512], F32)
    mxcol = sbt("mxcol", [128, 8], F32)
    acc_col = sbt("acc_col", [128, 8], F32)
    rec_col = sbt("rec_col", [128, 4], F32)

    P.wait("gpsimd", "spx", sp_tok)
    for dst, src in ((tabk1, "tabk1"), (tabk2, "tabk2"), (tri, "tri"),
                     (sel16c, "sel16c"), (sel16w, "sel16w")):
        ctok = P.dma("gpsimd", dst[:, :], D[src].ap(), "ldc2")
    ctok = P.dma("gpsimd", maskrow_t[:, :], D["maskrow"].ap(), "ldc2")
    LDC2 = ctok
    for eng in ("vector", "scalar", "tensor", "sync"):
        P.wait(eng, "ldc2", LDC2)
        P.wait(eng, "spx", sp_tok)
    P.wait("vector", "v", IOT_V)

    es8a = contextlib.ExitStack()
    ps_m = es8a.enter_context(nc.psum_tensor("ps_m", [16, S], F32))
    ps_q = es8a.enter_context(nc.psum_tensor("ps_q", [16, S], F32))
    sqb = [kvn, e_sb]
    ppd = [0, 0]
    pp_mm = [0, 0]
    sq_mm = [0, 0]
    for h2 in range(16):
        bi = h2 % 2
        buf = kraw[bi]
        P.wait("sync", "t", pp_mm[bi])
        ppd[bi] = P.dma("sync", buf[:, :], kvc_dram.ap()[8 + h2], "pp%d" % bi)
        P.wait("scalar", "pp%d" % bi, ppd[bi])
        P.wait("scalar", "t", sq_mm[bi])
        at = P.op("scalar", lambda E, buf=buf, bi=bi: E.activation(
            sqb[bi][:, :], buf[:, :], AF.Square), inc="a")
        P.wait("tensor", "pp%d" % bi, ppd[bi])
        for sl in range(0, S, 512):
            P.op("tensor", lambda E, h2=h2, sl=sl, buf=buf: E.matmul(
                ps_m[0:16, sl:sl + 512], sel16c[:, h2 * 16:(h2 + 1) * 16],
                buf[:, sl:sl + 512], start=(h2 == 0), stop=(h2 == 15)))
        pp_mm[bi] = P.inc_last("tensor", "t")
        P.wait("tensor", "a", at)
        for sl in range(0, S, 512):
            P.op("tensor", lambda E, h2=h2, sl=sl, bi=bi: E.matmul(
                ps_q[0:16, sl:sl + 512], sel16c[:, h2 * 16:(h2 + 1) * 16],
                sqb[bi][:, sl:sl + 512], start=(h2 == 0), stop=(h2 == 15)))
        sq_mm[bi] = P.inc_last("tensor", "t")
    PP_MM = list(pp_mm)
    SQ_MM = list(sq_mm)
    P.wait("vector", "t", sq_mm[1])
    P.wait("vector", "t", pp_mm[1])
    vt = P.op("vector", lambda E: E.tensor_scalar(
        Smat[0:16, :], ps_m[0:16, :], fc[0:16, 6:7], None, ALU.mult), inc="v")
    vt = P.op("vector", lambda E: E.tensor_scalar(
        Qmat[0:16, :], ps_q[0:16, :], fc[0:16, 6:7], None, ALU.mult),
        inc="v")
    PRE_PS_V = vt
    P.wait("vector", "v", vt)
    vt = P.op("vector", lambda E: E.tensor_mul(
        ps_m[0:16, :], Smat[0:16, :], Smat[0:16, :]), inc="v")
    P.wait("vector", "v", vt)
    vt = P.op("vector", lambda E: E.tensor_sub(
        Qmat[0:16, :], Qmat[0:16, :], ps_m[0:16, :]), inc="v")
    P.wait("scalar", "v", vt)
    at = P.op("scalar", lambda E: E.activation(
        ps_m[0:16, :], Qmat[0:16, :], AF.Sqrt, bias=fc[0:16, 0:1]),
        inc="a")
    P.wait("vector", "a", at)
    vt = P.op("vector", lambda E: E.reciprocal(
        Qmat[0:16, :], ps_m[0:16, :]), inc="v")
    P.wait("vector", "v", vt)
    vt = P.op("vector", lambda E: E.scalar_tensor_tensor(
        Smat[0:16, :], Smat[0:16, :], -1.0, Qmat[0:16, :],
        ALU.mult, ALU.mult), inc="v")
    P.wait("vector", "v", vt)
    P.op("vector", lambda E: E.tensor_copy(Qb[0:16, :], Qmat[0:16, :]))
    vt = P.op("vector", lambda E: E.tensor_copy(Sb[0:16, :], Smat[0:16, :]),
              inc="v")
    PRE_V = vt
    P.wait("vector", "v", PRE_V)
    P.wait("tensor", "v", PRE_V)
    es8a.close()

    def emit_rope(src_pe, tab1, tab2, dst, Wd):
        P.op("vector", lambda E: E.tensor_mul(
            rsA[:, 0:Wd], src_pe[:, 0:Wd], tab1[:, 0:Wd]))
        mt = 0
        for j in range(4):
            b = 32 * j
            pb = b + 32 if j % 2 == 0 else b - 32
            mt = P.op("vector", lambda E, b=b, pb=pb: E.tensor_mul(
                rsB[b:b + 32, 0:Wd], src_pe[pb:pb + 32, 0:Wd],
                tab2[pb:pb + 32, 0:Wd]), inc="v")
        P.wait("vector", "v", mt)
        ft = 0
        for j in range(4):
            b = 32 * j
            op = ALU.subtract if j % 2 == 0 else ALU.add
            ft = P.op("vector", lambda E, b=b, op=op: E.tensor_tensor(
                dst[b:b + 32, 0:Wd], rsA[b:b + 32, 0:Wd],
                rsB[b:b + 32, 0:Wd], op), inc="v")
        P.wait("vector", "v", ft)
        return mt, ft

    kr_free_v = [0, 0, 0]
    kr_free_t = [PP_MM[0], PP_MM[1], 0]
    kvn_gate_t = SQ_MM[0]
    esb_gate_t = SQ_MM[1]
    wq_free_t = 0
    qp_gate_t = 0
    knope_gate_t = 0
    prev_exp_a = 0
    prev_max_v = 0
    prev_evac_v = PRE_PS_V
    av_mm_t = 0
    rb_mm_t = 0
    rbcp_v = 0
    avt_v = 0
    rr_v = 0
    pT_ev = 0

    for hp in range(8):
        P.wait("sync", "v", kr_free_v[0])
        P.wait("sync", "t", kr_free_t[0])
        k0 = P.dma("sync", kraw[0][:, :], kvc_dram.ap()[hp], "kv0")
        P.wait("sync", "v", kr_free_v[1])
        P.wait("sync", "t", kr_free_t[1])
        k1 = P.dma("sync", kraw[1][:, :], kvc_dram.ap()[8 + 2 * hp], "kv1")
        P.wait("sync", "v", kr_free_v[2])
        P.wait("sync", "t", kr_free_t[2])
        k2 = P.dma("sync", kraw[2][:, :], kvc_dram.ap()[9 + 2 * hp], "kv2")
        P.wait("gpsimd", "t", wq_free_t)
        wqt = P.dma("gpsimd", wq_sb[:, :], D["wqblk"].ap()[hp], "wq")

        es_qp = contextlib.ExitStack()
        ps_qp = [es_qp.enter_context(
            nc.psum_tensor("ps_qp%d_%d" % (i, hp), [128, TOK], F32))
            for i in range(3)]
        P.wait("tensor", "wq", wqt)
        P.wait("tensor", "a", prev_exp_a)
        P.wait("tensor", "v", prev_evac_v)
        for k in range(12):
            for m in range(3):
                for sl in (0, 512):
                    P.op("tensor", lambda E, k=k, m=m, sl=sl: E.matmul(
                        ps_qp[m][:, sl:sl + 512],
                        wq_sb[:, (m * 12 + k) * 128:(m * 12 + k + 1) * 128],
                        qcn[:, k * TOK + sl:k * TOK + sl + 512],
                        start=(k == 0), stop=(k == 11)))
        wq_free_t = P.inc_last("tensor", "t")
        P.wait("vector", "t", wq_free_t)
        P.wait("vector", "t", qp_gate_t)
        qp_ev = 0
        for m in range(3):
            qp_ev = P.op("vector", lambda E, m=m: E.tensor_copy(
                qp_sb[:, m * TOK:(m + 1) * TOK], ps_qp[m][:, :]), inc="v")
        P.wait("vector", "v", qp_ev)
        P.wait("tensor", "v", qp_ev)
        es_qp.close()

        for par in range(2):
            h = 2 * hp + par
            kr = kraw[1 + par]
            ksem, ktok = ("kv1", k1) if par == 0 else ("kv2", k2)

            es_kv = contextlib.ExitStack()
            ps_a = es_kv.enter_context(
                nc.psum_tensor("ps_a%d" % h, [128, 512], F32))
            ps_b = es_kv.enter_context(
                nc.psum_tensor("ps_b%d" % h, [128, 512], F32))
            P.wait("vector", ksem, ktok)
            P.wait("vector", "t", kvn_gate_t)
            P.wait("tensor", "a", prev_exp_a)
            P.wait("tensor", "v", prev_evac_v)
            ab_read = 0
            t1_tok = 0
            for sl in (0, 512, 1024, 1536):
                P.wait("tensor", "v", ab_read)
                P.op("tensor", lambda E, h=h, sl=sl: E.matmul(
                    ps_a[:, :], sel16w[0:16, h * 128:(h + 1) * 128],
                    Qb[0:16, sl:sl + 512], start=True, stop=True))
                mmt = P.op("tensor", lambda E, h=h, sl=sl: E.matmul(
                    ps_b[:, :], sel16w[0:16, h * 128:(h + 1) * 128],
                    Sb[0:16, sl:sl + 512], start=True, stop=True), inc="t")
                P.wait("vector", "t", mmt)
                t1_tok = P.op("vector", lambda E, kr=kr, sl=sl: E.tensor_mul(
                    kvt1[:, :], kr[:, sl:sl + 512], ps_a[:, :]), inc="v")
                P.wait("vector", "v", t1_tok)
                ab_read = P.op("vector", lambda E, sl=sl:
                               E.scalar_tensor_tensor(
                                   kvn[:, sl:sl + 512], ps_b[:, :],
                                   kvnb[:, 0:1], kvt1[:, :],
                                   ALU.add, ALU.add), inc="v")
            kr_free_v[1 + par] = t1_tok
            kr_free_t[1 + par] = 0
            KVN_V = ab_read
            P.wait("tensor", "v", KVN_V)
            es_kv.close()

            es_kp = contextlib.ExitStack()
            ps_k = [es_kp.enter_context(
                nc.psum_tensor("ps_k%d_%d" % (i, h), [128, 512], F32))
                for i in range(2)]
            P.wait("vector", "t", knope_gate_t)
            kp_ev = [0, 0]
            for i4, sl in enumerate((0, 512, 1024, 1536)):
                bi = i4 % 2
                P.wait("tensor", "v", kp_ev[bi])
                mmt = P.op("tensor", lambda E, sl=sl, bi=bi: E.matmul(
                    ps_k[bi][:, :], wkv[:, 0:128], kvn[:, sl:sl + 512],
                    start=True, stop=True), inc="t")
                P.wait("vector", "t", mmt)
                kp_ev[bi] = P.op("vector", lambda E, sl=sl, bi=bi:
                                 E.tensor_copy(knope[:, sl:sl + 512],
                                               ps_k[bi][:, :]), inc="v")
            for grp in range(4):
                bi = grp % 2
                P.wait("tensor", "v", kp_ev[bi])
                for tc4 in range(4):
                    tc = grp * 4 + tc4
                    P.op("tensor", lambda E, tc=tc, tc4=tc4, bi=bi: E.matmul(
                        ps_k[bi][:, tc4 * 128:(tc4 + 1) * 128],
                        kvn[:, tc * 128:(tc + 1) * 128], wkv[:, 128:256],
                        start=True, stop=True))
                mmt = P.inc_last("tensor", "t")
                kvn_gate_t = mmt
                P.wait("vector", "t", mmt)
                kp_ev[bi] = P.op("vector", lambda E, grp=grp, bi=bi:
                                 E.tensor_copy(
                                     vtm[:, grp * 512:(grp + 1) * 512],
                                     ps_k[bi][:, :]), inc="v")
            KVP_V = kp_ev[1]
            P.wait("tensor", "v", kp_ev[0])
            P.wait("tensor", "v", kp_ev[1])
            es_kp.close()

            if par == 0:
                P.wait("vector", "kv0", k0)
                P.wait("vector", "t", qp_gate_t)
                krmul, krv = emit_rope(kraw[0], tabk1, tabk2, kpe_rope, S)
                kr_free_v[0] = krmul
                kr_free_t[0] = 0
                qmul, qrv = emit_rope(qp_sb[:, 2 * TOK:3 * TOK], tabq1,
                                      tabq2, qrope, TOK)
                ROPE_V = qrv
                P.wait("tensor", "v", ROPE_V)

            es_sc = contextlib.ExitStack()
            ps_sc = es_sc.enter_context(
                nc.psum_tensor("ps_sc%d" % h, [128, 2048], F32))
            ps_tr = es_sc.enter_context(
                nc.psum_tensor("ps_tr%d" % h, [128, 512], BF16))
            av_ps = es_sc.enter_context(
                nc.psum_tensor("av_ps%d" % h, [128, 512], F32))
            ps_rb = es_sc.enter_context(
                nc.psum_tensor("ps_rb%d" % h, [128, 512], F32))
            ps_rr = es_sc.enter_context(
                nc.psum_tensor("ps_rr%d" % h, [1, 128], F32))
            exp_a = prev_exp_a
            max_v = prev_max_v
            sc_tt = 0
            tr_last_t = esb_gate_t
            for g in range(2):
                for p4 in range(4):
                    p = 4 * g + p4
                    Wp = (p + 1) * 128
                    moff = (p * (p + 1) // 2) * 128
                    qoff = par * TOK + p * 128
                    P.wait("tensor", "a", exp_a)
                    P.wait("tensor", "v", max_v)
                    for sl in range(0, Wp, 512):
                        ww = min(512, Wp - sl)
                        P.op("tensor", lambda E, qoff=qoff, sl=sl, ww=ww:
                             E.matmul(ps_sc[:, sl:sl + ww],
                                      qp_sb[:, qoff:qoff + 128],
                                      knope[:, sl:sl + ww],
                                      start=True, stop=False))
                        P.op("tensor", lambda E, par=par, p=p, sl=sl, ww=ww:
                             E.matmul(ps_sc[:, sl:sl + ww],
                                      qrope[par * 64:(par + 1) * 64,
                                            p * 128:(p + 1) * 128],
                                      kpe_rope[par * 64:(par + 1) * 64,
                                               sl:sl + ww],
                                      start=False, stop=True))
                    for sl in range(0, Wp, 512):
                        ww = min(512, Wp - sl)
                        P.op("tensor", lambda E, qoff=qoff, sl=sl, ww=ww:
                             E.matmul(ps_sc[:, 1024 + sl:1024 + sl + ww],
                                      qp_sb[:, qoff:qoff + 128],
                                      knope[:, 1024 + sl:1024 + sl + ww],
                                      start=True, stop=False))
                        P.op("tensor", lambda E, moff=moff, sl=sl, ww=ww:
                             E.matmul(ps_sc[:, 1024 + sl:1024 + sl + ww],
                                      onesrow_bf[:, :],
                                      maskrow[0:1, moff + sl:moff + sl + ww],
                                      start=False, stop=False))
                        P.op("tensor", lambda E, par=par, p=p, sl=sl, ww=ww:
                             E.matmul(ps_sc[:, 1024 + sl:1024 + sl + ww],
                                      qrope[par * 64:(par + 1) * 64,
                                            p * 128:(p + 1) * 128],
                                      kpe_rope[par * 64:(par + 1) * 64,
                                               1024 + sl:1024 + sl + ww],
                                      start=False, stop=True))
                    sc_tt = P.inc_last("tensor", "t")
                    P.wait("vector", "t", sc_tt)
                    P.op("vector", lambda E, p=p: E.tensor_add(
                        diag[:, :], ps_sc[:, p * 128:(p + 1) * 128],
                        tri[:, :]))
                    P.op("vector", lambda E, Wp=Wp: E.reduce_max(
                        mxcol[:, 0:1], ps_sc[:, 0:Wp], axis=AX.X))
                    vt = P.op("vector", lambda E, Wp=Wp: E.reduce_max(
                        mxcol[:, 1:2], ps_sc[:, 1024:1024 + Wp], axis=AX.X),
                        inc="v")
                    P.wait("vector", "v", vt)
                    vt = P.op("vector", lambda E: E.tensor_tensor(
                        mxcol[:, 2:3], mxcol[:, 0:1], mxcol[:, 1:2],
                        ALU.max), inc="v")
                    P.wait("vector", "v", vt)
                    max_v = P.op("vector", lambda E: E.tensor_scalar(
                        mxcol[:, 4:5], mxcol[:, 2:3], -SC_ATT, None,
                        ALU.mult), inc="v")
                    P.wait("scalar", "v", max_v)
                    P.wait("scalar", "t", sc_tt)
                    P.wait("scalar", "t", tr_last_t)
                    if p > 0:
                        P.op("scalar", lambda E, p=p: E.activation(
                            e_sb[:, 0:p * 128], ps_sc[:, 0:p * 128], AF.Exp,
                            bias=mxcol[:, 4:5], scale=fc[:, 2:3],
                            accum_out=acc_col[:, 0:1]), inc="a")
                    P.op("scalar", lambda E, p=p, Wp=Wp: E.activation(
                        e_sb[:, p * 128:Wp], diag[:, :], AF.Exp,
                        bias=mxcol[:, 4:5], scale=fc[:, 2:3],
                        accum_out=acc_col[:, 1:2]), inc="a")
                    exp_a = P.op("scalar", lambda E, Wp=Wp: E.activation(
                        e_sb[:, Wp:2 * Wp], ps_sc[:, 1024:1024 + Wp], AF.Exp,
                        bias=mxcol[:, 4:5], scale=fc[:, 2:3],
                        accum_out=acc_col[:, 2:3]), inc="a")
                    P.wait("vector", "a", exp_a)
                    if p > 0:
                        vt = P.op("vector", lambda E: E.tensor_add(
                            acc_col[:, 4:5], acc_col[:, 0:1],
                            acc_col[:, 1:2]), inc="v")
                        P.wait("vector", "v", vt)
                        vt = P.op("vector", lambda E: E.tensor_add(
                            acc_col[:, 4:5], acc_col[:, 4:5],
                            acc_col[:, 2:3]), inc="v")
                    else:
                        vt = P.op("vector", lambda E: E.tensor_add(
                            acc_col[:, 4:5], acc_col[:, 1:2],
                            acc_col[:, 2:3]), inc="v")
                    P.wait("vector", "v", vt)
                    rec_v = P.op("vector", lambda E: E.reciprocal(
                        rec_col[:, 0:1], acc_col[:, 4:5]), inc="v")
                    P.wait("tensor", "v", rec_v)
                    P.wait("tensor", "v", rr_v)
                    rrt = P.op("tensor", lambda E: E.transpose(
                        ps_rr[0:1, 0:128], rec_col[:, 0:1], identf[:, :]),
                        inc="t")
                    P.wait("vector", "t", rrt)
                    P.wait("vector", "t", rb_mm_t)
                    rr_v = P.op("vector", lambda E, p4=p4: E.tensor_copy(
                        recrow[0:1, p4 * 128:(p4 + 1) * 128],
                        ps_rr[0:1, 0:128]), inc="v")
                    P.wait("tensor", "a", exp_a)
                    for j0 in range(0, 2 * (p + 1), 4):
                        jn = min(j0 + 4, 2 * (p + 1))
                        P.wait("tensor", "v", pT_ev)
                        for j in range(j0, jn):
                            P.op("tensor", lambda E, j=j, j0=j0,
                                 st=(j == j0), sp=(j == jn - 1): E.matmul(
                                     ps_tr[:, (j - j0) * 128:
                                           (j - j0 + 1) * 128],
                                     e_sb[:, j * 128:(j + 1) * 128],
                                     ident[:, :], start=st, stop=sp,
                                     is_transpose=True))
                        tr_last_t = P.inc_last("tensor", "t")
                        P.wait("vector", "t", tr_last_t)
                        P.wait("vector", "t", av_mm_t)
                        for j in range(j0, jn):
                            kc = j if j <= p else 8 + (j - (p + 1))
                            pT_ev = P.op("vector", lambda E, kc=kc, p4=p4,
                                         j=j, j0=j0: E.tensor_copy(
                                             pT[kc][:, p4 * 128:
                                                    (p4 + 1) * 128],
                                             ps_tr[:, (j - j0) * 128:
                                                   (j - j0 + 1) * 128]),
                                         inc="v")
                P.wait("tensor", "v", pT_ev)
                P.wait("tensor", "v", avt_v)
                first = True
                for reg in range(2):
                    for kcl in range(4 * g + 4):
                        kc = kcl + 8 * reg
                        a0 = max(0, kcl - 4 * g) * 128
                        last = (reg == 1 and kcl == 4 * g + 3)
                        P.op("tensor", lambda E, kc=kc, a0=a0, st=first,
                             sp=last: E.matmul(
                                 av_ps[:, a0:512],
                                 vtm[:, kc * 128:(kc + 1) * 128],
                                 pT[kc][:, a0:512], start=st, stop=sp))
                        first = False
                av_mm_t = P.inc_last("tensor", "t")
                knope_gate_t = av_mm_t
                P.wait("tensor", "v", rr_v)
                P.wait("tensor", "v", rbcp_v)
                rb_mm_t = P.op("tensor", lambda E: E.matmul(
                    ps_rb[:, :], onesrow_f[:, :], recrow[0:1, 0:512],
                    start=True, stop=True), inc="t")
                P.wait("vector", "t", rb_mm_t)
                rbcp_v = P.op("vector", lambda E: E.tensor_copy(
                    rb_sb[:, :], ps_rb[:, :]), inc="v")
                P.wait("vector", "v", rbcp_v)
                P.wait("vector", "t", av_mm_t)
                avt_v = P.op("vector", lambda E: E.tensor_mul(
                    avt[:, :], av_ps[:, :], rb_sb[:, :]), inc="v")
                P.wait("vector", "v", avt_v)
                prev_evac_v = P.op("vector", lambda E, h=h, g=g:
                                   E.tensor_scalar(
                                       attn_all[:, h * TOK + g * 512:
                                                h * TOK + (g + 1) * 512],
                                       avt[:, :],
                                       iot[:, (h // 2) * 4:(h // 2) * 4 + 1],
                                       None, ALU.add), inc="v")
            prev_exp_a = exp_a
            prev_max_v = max_v
            esb_gate_t = tr_last_t
            es_sc.close()
        qp_gate_t = sc_tt
    fence_wait(mark_fence())
    es8.close()

    es9 = contextlib.ExitStack()
    es9w = contextlib.ExitStack()
    x2 = es9.enter_context(nc.sbuf_tensor("x_own2", [128, 16 * TOK], F32))
    owt2 = [es9w.enter_context(
        nc.sbuf_tensor("wo_t%d" % i, [128, 2 * 16 * 128], BF16))
        for i in range(2)]
    pso2 = [es9w.enter_context(
        nc.psum_tensor("pso2_%d" % i, [128, TOK], F32)) for i in range(4)]
    P.wait("sync", "spx", sp_tok)
    xre = 0
    for c in range(16):
        xre = P.dma("sync", x2[:, c * TOK:(c + 1) * TOK], xres_dram.ap()[c],
                    "xre")
    P.wait("vector", "xre", xre)
    wo_tok = [0, 0]
    wo_free = [0, 0]
    wo_ev = [0, 0, 0, 0]
    for mg in range(8):
        bi = mg % 2
        P.wait("gpsimd", "t", wo_free[bi])
        wo_tok[bi] = P.dma("gpsimd", owt2[bi][:, :],
                           D["woblk"].ap()[:, 2 * mg:2 * mg + 2, :, :],
                           "wo%d" % bi)
        P.wait("tensor", "wo%d" % bi, wo_tok[bi])
        for i in range(2):
            ps = pso2[2 * bi + i]
            P.wait("tensor", "v", wo_ev[2 * bi + i])
            for k in range(16):
                for sl in (0, 512):
                    P.op("tensor", lambda E, bi=bi, i=i, k=k, sl=sl, ps=ps:
                         E.matmul(ps[:, sl:sl + 512],
                                  owt2[bi][:, (i * 16 + k) * 128:
                                           (i * 16 + k + 1) * 128],
                                  attn_all[:, k * TOK + sl:k * TOK + sl + 512],
                                  start=(k == 0), stop=(k == 15)))
            tt = P.inc_last("tensor", "t")
            wo_free[bi] = tt
            mp = 2 * mg + i
            P.wait("vector", "t", tt)
            vt = P.op("vector", lambda E, ps=ps, mp=mp: E.tensor_add(
                x2[:, mp * TOK:(mp + 1) * TOK],
                x2[:, mp * TOK:(mp + 1) * TOK], ps[:, :]), inc="v")
            wo_ev[2 * bi + i] = vt
    WO_V = P.cnt["v"]
    fence_wait(mark_fence())
    es9w.close()

    if dev_stop == "attn":
        _emit_store(P, nc, x2, out_d, WO_V)
        return _run(nc, P, top, in_maps)

    lnx2 = arena[:, 0:16 * TOK]
    comb2 = arena[:, 16 * TOK:32 * TOK]
    P.wait("vector", "v", WO_V)
    ffn2_v = emit_ffn(2, x2, "own", xtile=x2, ext=(lnx2, comb2))
    _emit_store(P, nc, x2, out_d, ffn2_v)
    return _run(nc, P, top, in_maps)




def _emit_store(P, nc, x_own, out_d, wait_v):
    P.wait("sync", "v", wait_v)
    st = 0
    for c in range(16):
        st = P.dma("sync", out_d.ap()[c], x_own[:, c * TOK:(c + 1) * TOK], "st")
    P.wait("sync", "st", st)


def _run(nc, P, top, in_maps):
    with nc.Block() as block:
        @block.sync
        def _(E):
            for f in P.q["sync"]:
                f(E)

        @block.tensor
        def _(E):
            for f in P.q["tensor"]:
                f(E)

        @block.vector
        def _(E):
            for f in P.q["vector"]:
                f(E)

        @block.scalar
        def _(E):
            for f in P.q["scalar"]:
                f(E)

        @block.gpsimd
        def _(E):
            for f in P.q["gpsimd"]:
                f(E)
    top.close()
    res = run_bass_kernel_spmd(nc, in_maps, core_ids=list(range(NCORES)))
    return res


def kernel(**inputs):
    in_maps = host_inputs(inputs)
    res = build_and_run(in_maps, DEV_STOP)
    x = np.asarray(inputs["L0_attn_x"])
    out = np.empty((B, S, H), np.float32)
    for core in range(NCORES):
        b, z = core // 2, core % 2
        o = res.results[core]["out"]
        feat = o.reshape(16 * 128, TOK)
        toks = feat.T
        own = ZIG[z]
        for i, c in enumerate(own):
            out[b, c * 128:(c + 1) * 128, :] = toks[i * 128:(i + 1) * 128]
    return out
